# version 02
# baseline (speedup 1.0000x reference)
import math
import numpy as np
import ml_dtypes
import concourse.bass as bass
import concourse.mybir as mybir
from concourse.bass_utils import run_bass_kernel_spmd
from concourse.alu_op_type import AluOpType as ALU
from contextlib import ExitStack

AF = mybir.ActivationFunctionType
F32 = mybir.dt.float32
BF16 = mybir.dt.bfloat16

DM = 4096
T = 4096
HALF = 2048
NFC = HALF // 128
NTT = T // 128
NTQ = 16
PAIRS = [[0, 1], [2, 3], [4, 5], [6, 7]]
WORLD = [[0, 1, 2, 3, 4, 5, 6, 7]]
EPS = 1e-5

BUILD_SUBLAYERS = 4


class Ctr:
    def __init__(self, handle):
        self.h = handle
        self.v = 0

    def bump(self, n=1):
        self.v += n
        return self.v


def _rows(ap, c0):
    return ap[c0 * 128:(c0 + 1) * 128, :]


def build_nc():
    nc = bass.Bass("TRN2", target_bir_lowering=False, debug=False)

    x_in = nc.dram_tensor("x_in", [HALF, T], F32, kind="ExternalInput").ap()
    ident = nc.dram_tensor("ident", [128, 128], BF16, kind="ExternalInput").ap()
    identf = nc.dram_tensor("identf", [128, 128], F32, kind="ExternalInput").ap()
    bvq = nc.dram_tensor("bvq", [2, 1024], F32, kind="ExternalInput").ap()
    bvk = nc.dram_tensor("bvk", [2, 1024], F32, kind="ExternalInput").ap()
    bvv = nc.dram_tensor("bvv", [2, 1024], F32, kind="ExternalInput").ap()
    amask_d = nc.dram_tensor("amask", [2, 128, 256], BF16, kind="ExternalInput").ap()
    worgT = nc.dram_tensor("worgT", [DM, HALF], F32, kind="ExternalInput").ap()
    down_d = nc.dram_tensor("down_d", [DM, 512], F32, kind="ExternalInput").ap()
    up_d = nc.dram_tensor("up_d", [512, HALF], F32, kind="ExternalInput").ap()
    wqT_d = nc.dram_tensor("wqT_d", [HALF, 128], F32, kind="ExternalInput").ap()
    keysT_d = nc.dram_tensor("keysT_d", [128, 8], F32, kind="ExternalInput").ap()
    sel_d = nc.dram_tensor("sel_d", [4, 8, 128], F32, kind="ExternalInput").ap()
    qwT_d = nc.dram_tensor("qwT_d", [DM, HALF], F32, kind="ExternalInput").ap()
    kwT_d = nc.dram_tensor("kwT_d", [DM, HALF], F32, kind="ExternalInput").ap()
    vwT_d = nc.dram_tensor("vwT_d", [DM, HALF], F32, kind="ExternalInput").ap()
    owT_d = nc.dram_tensor("owT_d", [DM, HALF], F32, kind="ExternalInput").ap()
    ropec_d = nc.dram_tensor("ropec_d", [128, T], BF16, kind="ExternalInput").ap()
    ropes_d = nc.dram_tensor("ropes_d", [128, T], BF16, kind="ExternalInput").ap()
    cbias_d = nc.dram_tensor("cbias_d", [4, 128, 512], F32, kind="ExternalInput").ap()
    iota512_d = nc.dram_tensor("iota512_d", [1, 512], F32, kind="ExternalInput").ap()
    iotac_d = nc.dram_tensor("iotac_d", [128, 1], F32, kind="ExternalInput").ap()
    impg_d = nc.dram_tensor("impg_d", [1, 1], F32, kind="ExternalInput").ap()
    out = nc.dram_tensor("out", [HALF, T], F32, kind="ExternalOutput").ap()
    dbg = nc.dram_tensor("dbg", [HALF, T], BF16, kind="ExternalOutput").ap()

    resid = nc.dram_tensor("resid", [HALF, T], F32).ap()
    xn_dram = nc.dram_tensor("xn_dram", [HALF, T], BF16).ap()
    arbuf_i = nc.dram_tensor("arbuf_i", [128, 64], F32).ap()
    arbuf_o = nc.dram_tensor("arbuf_o", [128, 64], F32).ap()
    rowbounce = nc.dram_tensor("rowbounce", [64, 128], F32).ap()
    bind_dram = nc.dram_tensor("bind_dram", [2, 2048], F32).ap()
    agA = nc.dram_tensor("agA", [DM, T], BF16).ap()
    arq_i = nc.dram_tensor("arq_i", [128, 1], F32).ap()
    arq_o = nc.dram_tensor("arq_o", [128, 1], F32).ap()
    agB = nc.dram_tensor("agB", [DM, T], BF16).ap()
    qsp = nc.dram_tensor("qsp", [HALF, T], BF16).ap()
    ksp = nc.dram_tensor("ksp", [HALF, T], BF16).ap()
    vsp = nc.dram_tensor("vsp", [16, 128, 32, 128], BF16).ap()
    obsp = nc.dram_tensor("obsp", [HALF, T], BF16).ap()
    ari_imp = nc.dram_tensor("ari_imp", [1, 512], F32).ap()
    aro_imp = nc.dram_tensor("aro_imp", [1, 512], F32).ap()

    with ExitStack() as outer:
        def SEM(name):
            return Ctr(outer.enter_context(nc.semaphore(name)))

        def SBO(name, shape, dt):
            return outer.enter_context(nc.sbuf_tensor(name, shape, dt))

        id_bf = SBO("id_bf", [128, 128], BF16)
        id_f32 = SBO("id_f32", [128, 128], F32)
        ones_bf = SBO("ones_bf", [128, 2], BF16)
        ones_row = SBO("ones_row", [1, 128], F32)
        eps_col = SBO("eps_col", [128, 1], F32)
        sc125 = SBO("sc125", [128, 1], F32)
        stat_s = SBO("stat_s", [128, 64], F32)
        stat_w = SBO("stat_w", [128, 224], F32)
        qkcol = SBO("qkcol", [128, 16], F32)
        mcol = SBO("mcol", [128, 16], F32)
        cvec = SBO("cvec", [128, 4], F32)
        scC = SBO("scC", [1, 2], F32)
        vcol = SBO("vcol", [128, 16], F32)
        amask_sb = SBO("amask_sb", [128, 2, 256], BF16)
        bind_stack = ExitStack()
        bindw = bind_stack.enter_context(nc.sbuf_tensor("bindw", [2, 8192], F32))

        psb = [outer.enter_context(nc.psum_tensor(f"psb{i}", [128, 512], F32))
               for i in range(7)]
        psT = outer.enter_context(nc.psum_tensor("psT", [128, 1024], BF16))
        ps_stat = psb[6]

        pe = SEM("s_pe"); ve = SEM("s_ve"); ac = SEM("s_ac"); gp = SEM("s_gp")
        cc = SEM("s_cc"); st = SEM("s_st"); ld = SEM("s_ld"); bar = SEM("s_bar")
        lda = [SEM("s_lda0"), SEM("s_lda1")]
        ldq = [SEM("s_ldq0"), SEM("s_ldq1")]
        ldx = [SEM("s_ldx0"), SEM("s_ldx1")]
        stn = [SEM("s_stn0"), SEM("s_stn1")]
        stx = [SEM("s_stx0"), SEM("s_stx1")]

        _uid = [0]

        def mk(ph, nm, shape, dt):
            _uid[0] += 1
            return ph.enter_context(nc.sbuf_tensor(f"{nm}_{_uid[0]}", shape, dt))

        block = outer.enter_context(nc.Block())
        ENG = {"sync": block.sync, "tensor": block.tensor, "vector": block.vector,
               "scalar": block.scalar, "gpsimd": block.gpsimd}

        def end_phase(waits_by_engine):
            for name, wlist in waits_by_engine.items():
                dec = ENG[name]

                @dec
                def _(e, wlist=wlist):
                    for (h, v) in wlist:
                        e.wait_ge(h, v)
                    e.sem_inc(bar.h, 1)
                bar.bump(1)
            for name, dec in ENG.items():
                @dec
                def _(e):
                    e.wait_ge(bar.h, bar.v)

        @block.sync
        def _(sync):
            sync.dma_start(out=id_bf[:], in_=ident).then_inc(ld.h, 16)
            sync.dma_start(out=id_f32[:], in_=identf).then_inc(ld.h, 16)
            sync.dma_start(out=bindw[:, 0:1024], in_=bvq).then_inc(ld.h, 16)
            sync.dma_start(out=bindw[:, 1024:2048], in_=bvk).then_inc(ld.h, 16)
            sync.dma_start(out=bindw[:, 2048:3072], in_=bvv).then_inc(ld.h, 16)
            sync.dma_start(out=amask_sb[:], in_=amask_d.rearrange("d p j -> p d j")).then_inc(ld.h, 16)
        ld.bump(96)

        @block.vector
        def _(v):
            v.wait_ge(ld.h, ld.v)
            v.memset(ones_bf[:], 1.0)
            v.memset(ones_row[:], 1.0)
            v.memset(eps_col[:], EPS)
            v.memset(sc125[:], 0.125)
            v.memset(scC[0:1, 0:1], 1.0 / (4096.0 * math.sqrt(128.0)))
            v.memset(scC[0:1, 1:2], -1.0 / (4096.0 * math.sqrt(128.0))).then_inc(ve.h, 1)
            for i, base in enumerate((0, 1024, 2048)):
                v.tensor_reduce(out=bindw[:, 3072 + i:3073 + i], in_=bindw[:, base:base + 1024],
                                axis=mybir.AxisListType.X, op=ALU.add,
                                apply_absolute_value=True).then_inc(ve.h, 1)
        ve.bump(4)

        @block.scalar
        def _(s):
            s.wait_ge(ve.h, ve.v)
            s.sign(out=bindw[:, 4096:5120], in_=bindw[:, 0:1024])
            s.sign(out=bindw[:, 5120:6144], in_=bindw[:, 1024:2048])
            s.sign(out=bindw[:, 6144:7168], in_=bindw[:, 2048:3072]).then_inc(ac.h, 1)
        ac.bump(1)

        @block.vector
        def _(v):
            veB = ve.v
            v.wait_ge(ac.h, ac.v)
            v.tensor_scalar(out=bindw[:, 3072:3075], in0=bindw[:, 3072:3075],
                            scalar1=1.0 / 1024.0, scalar2=None, op0=ALU.mult).then_inc(ve.h, 1)
            k = 1
            for i in range(3):
                sgn = bindw[:, 4096 + 1024 * i:4096 + 1024 * (i + 1)]
                al = bindw[:, 3072 + i:3073 + i]
                v.wait_ge(ve.h, veB + k)
                v.tensor_scalar(out=sgn, in0=sgn, scalar1=al, scalar2=None,
                                op0=ALU.mult).then_inc(ve.h, 1); k += 1
                v.wait_ge(ve.h, veB + k)
                v.tensor_scalar(out=bindw[:, 7168:8192], in0=sgn, scalar1=0.0, scalar2=al,
                                op0=ALU.is_equal, op1=ALU.mult).then_inc(ve.h, 1); k += 1
                v.wait_ge(ve.h, veB + k)
                v.tensor_tensor(out=sgn, in0=sgn, in1=bindw[:, 7168:8192],
                                op=ALU.add).then_inc(ve.h, 1); k += 1
            v.wait_ge(ve.h, veB + k)
            v.tensor_tensor(out=bindw[:, 0:1024], in0=bindw[:, 4096:5120],
                            in1=bindw[:, 5120:6144], op=ALU.mult).then_inc(ve.h, 1)
        ve.bump(11)

        @block.tensor
        def _(t):
            peB, acB = pe.v, ac.v
            t.wait_ge(ve.h, ve.v)
            for dc in range(8):
                if dc >= 2:
                    t.wait_ge(ac.h, acB + (dc - 1))
                t.transpose(out=psb[dc % 2][0:128, 0:2],
                            in_=bindw[0:2, dc * 128:(dc + 1) * 128],
                            identity=id_f32[0:2, 0:2]).then_inc(pe.h, 1)
            for dc in range(8):
                if dc >= 2:
                    t.wait_ge(ac.h, acB + 8 + (dc - 1))
                t.transpose(out=psb[2 + dc % 2][0:128, 0:2],
                            in_=bindw[0:2, 6144 + dc * 128:6144 + (dc + 1) * 128],
                            identity=id_f32[0:2, 0:2]).then_inc(pe.h, 1)
        pe.bump(16)

        @block.scalar
        def _(s):
            peB, acB = pe.v - 16, ac.v
            for dc in range(8):
                s.wait_ge(pe.h, peB + dc + 1)
                s.copy(out=qkcol[:, 2 * dc:2 * dc + 2],
                       in_=psb[dc % 2][0:128, 0:2]).then_inc(ac.h, 1)
            for dc in range(8):
                s.wait_ge(pe.h, peB + 8 + dc + 1)
                s.copy(out=vcol[:, 2 * dc:2 * dc + 2],
                       in_=psb[2 + dc % 2][0:128, 0:2]).then_inc(ac.h, 1)
        ac.bump(16)

        end_phase({"scalar": [(ac.h, ac.v)], "vector": [(ve.h, ve.v)],
                   "tensor": [(pe.h, pe.v)], "sync": [(ld.h, ld.v)]})
        bind_stack.close()

        def emit_ln_statspass(src_dram):
            peB, veB = pe.v, ve.v
            ldaB = [lda[0].v, lda[1].v]
            with ExitStack() as ph:
                xch = mk(ph, "lns_xch", [128, 2, T], F32)
                xsq = mk(ph, "lns_xsq", [128, T], BF16)
                xbf = mk(ph, "lns_xbf", [128, T], BF16)

                @block.sync
                def _(sync):
                    for c in range(NFC):
                        if c >= 2:
                            sync.wait_ge(ve.h, veB + 2 * (c - 2) + 2)
                        sync.dma_start(out=xch[:, c % 2, :],
                                       in_=_rows(src_dram, c)).then_inc(lda[c % 2].h, 16)
                        lda[c % 2].bump(16)

                @block.vector
                def _(v):
                    for c in range(NFC):
                        v.wait_ge(lda[c % 2].h, ldaB[c % 2] + 16 * (c // 2 + 1))
                        if c >= 1:
                            v.wait_ge(pe.h, peB + c)
                        v.tensor_tensor(out=xsq[:], in0=xch[:, c % 2, :],
                                        in1=xch[:, c % 2, :], op=ALU.mult)
                        v.tensor_copy(out=xbf[:], in_=xch[:, c % 2, :]).then_inc(ve.h, 2)
                ve.bump(2 * NFC)

                @block.tensor
                def _(t):
                    for c in range(NFC):
                        t.wait_ge(ve.h, veB + 2 * c + 2)
                        for i in range(NTT):
                            first = (c == 0 and i == 0)
                            t.matmul(out=ps_stat[:, i:i + 1],
                                     lhsT=xbf[:, i * 128:(i + 1) * 128], rhs=ones_bf[:, 0:1],
                                     start=first, stop=False)
                            mm = t.matmul(out=ps_stat[:, 32 + i:33 + i],
                                          lhsT=xsq[:, i * 128:(i + 1) * 128], rhs=ones_bf[:, 0:1],
                                          start=False, stop=(c == NFC - 1 and i == NTT - 1))
                        mm.then_inc(pe.h, 1)
                pe.bump(NFC)
                end_phase({"tensor": [(pe.h, pe.v)], "vector": [(ve.h, ve.v)],
                           "sync": [(lda[0].h, lda[0].v), (lda[1].h, lda[1].v)]})

        def emit_ln_finish(src_dram, with_mean=False, do_ag=False):
            peB, veB, acB, gpB, ccB = pe.v, ve.v, ac.v, gp.v, cc.v
            with ExitStack() as ph:
                ab_row = mk(ph, "lnf_ab", [128, T], F32)
                rbT = mk(ph, "lnf_rbT", [64, 128], F32)
                arow_a = mk(ph, "lnf_ara", [1, T], F32)
                arow_b = mk(ph, "lnf_arb", [1, T], F32)
                bb_row = mk(ph, "lnf_bb", [128, T], F32)
                xch = mk(ph, "lnf_xch", [128, 2, T], F32)
                xt0 = mk(ph, "lnf_xt0", [128, T], F32)
                xnc = mk(ph, "lnf_xnc", [128, 2, T], BF16)

                @block.scalar
                def _(s):
                    s.copy(out=stat_s[:, 0:64], in_=ps_stat[:, 0:64]).then_inc(ac.h, 1)
                ac.bump(1)

                @block.gpsimd
                def _(g):
                    g.wait_ge(ac.h, acB + 1)
                    g.dma_start(out=arbuf_i, in_=stat_s[:, 0:64]).then_inc(gp.h, 16)
                    g.wait_ge(gp.h, gpB + 16)
                    g.collective_compute("AllReduce", ALU.add, replica_groups=PAIRS,
                                         ins=[arbuf_i.opt()], outs=[arbuf_o.opt()]).then_inc(cc.h, 1)
                    g.wait_ge(cc.h, ccB + 1)
                    g.dma_start(out=stat_s[:, 0:64], in_=arbuf_o).then_inc(gp.h, 16)
                gp.bump(32); cc.bump(1)

                @block.vector
                def _(v):
                    v.wait_ge(gp.h, gpB + 32)
                    v.tensor_scalar(out=stat_w[:, 0:32], in0=stat_s[:, 0:32],
                                    scalar1=1.0 / DM, scalar2=None, op0=ALU.mult)
                    v.tensor_scalar(out=stat_w[:, 32:64], in0=stat_s[:, 32:64],
                                    scalar1=1.0 / DM, scalar2=None, op0=ALU.mult).then_inc(ve.h, 1)
                    v.wait_ge(ve.h, veB + 1)
                    v.tensor_tensor(out=stat_w[:, 64:96], in0=stat_w[:, 0:32],
                                    in1=stat_w[:, 0:32], op=ALU.mult).then_inc(ve.h, 1)
                    v.wait_ge(ve.h, veB + 2)
                    v.tensor_tensor(out=stat_w[:, 96:128], in0=stat_w[:, 32:64],
                                    in1=stat_w[:, 64:96], op=ALU.subtract).then_inc(ve.h, 1)
                ve.bump(3)

                @block.scalar
                def _(s):
                    s.wait_ge(ve.h, veB + 3)
                    s.activation(out=stat_w[:, 128:160], in_=stat_w[:, 96:128],
                                 func=AF.Sqrt, bias=eps_col[:, 0:1], scale=1.0).then_inc(ac.h, 1)
                ac.bump(1)

                @block.vector
                def _(v):
                    v.wait_ge(ac.h, acB + 2)
                    v.reciprocal(out=stat_w[:, 160:192], in_=stat_w[:, 128:160]).then_inc(ve.h, 1)
                    v.wait_ge(ve.h, veB + 4)
                    v.scalar_tensor_tensor(out=stat_w[:, 192:224], in0=stat_w[:, 0:32],
                                           scalar=-1.0, in1=stat_w[:, 160:192],
                                           op0=ALU.mult, op1=ALU.mult).then_inc(ve.h, 1)
                ve.bump(2)

                @block.tensor
                def _(t):
                    t.wait_ge(ve.h, veB + 5)
                    t.transpose(out=psb[4][0:64, 0:128], in_=stat_w[:, 160:224],
                                identity=id_f32[:]).then_inc(pe.h, 1)
                pe.bump(1)

                @block.scalar
                def _(s):
                    s.wait_ge(pe.h, peB + 1)
                    s.copy(out=rbT[:], in_=psb[4][0:64, 0:128]).then_inc(ac.h, 1)
                ac.bump(1)

                @block.gpsimd
                def _(g):
                    g.wait_ge(ac.h, acB + 3)
                    g.dma_start(out=rowbounce, in_=rbT[:]).then_inc(gp.h, 16)
                    g.wait_ge(gp.h, gpB + 48)
                    g.dma_start(out=arow_a[:],
                                in_=rowbounce[0:32, :].rearrange("i p -> (i p)")[None, :]).then_inc(gp.h, 16)
                    g.dma_start(out=arow_b[:],
                                in_=rowbounce[32:64, :].rearrange("i p -> (i p)")[None, :]).then_inc(gp.h, 16)
                gp.bump(48)

                @block.tensor
                def _(t):
                    t.wait_ge(gp.h, gpB + 80)
                    for j in range(8):
                        if j >= 2:
                            t.wait_ge(ac.h, acB + 4 + (j - 2))
                        dst = psb[4] if j % 2 == 0 else psb[5]
                        t.matmul(out=dst[:, 0:512], lhsT=ones_row[:],
                                 rhs=arow_a[:, j * 512:(j + 1) * 512],
                                 start=True, stop=True).then_inc(pe.h, 1)
                pe.bump(8)

                @block.scalar
                def _(s):
                    for j in range(8):
                        s.wait_ge(pe.h, peB + 2 + j)
                        src = psb[4] if j % 2 == 0 else psb[5]
                        s.copy(out=ab_row[:, j * 512:(j + 1) * 512],
                               in_=src[:, 0:512]).then_inc(ac.h, 1)
                ac.bump(8)

                @block.tensor
                def _(t):
                    for j in range(8):
                        if j < 2:
                            t.wait_ge(ac.h, acB + 11)
                        else:
                            t.wait_ge(ac.h, acB + 12 + (j - 2))
                        dst = psb[4] if j % 2 == 0 else psb[5]
                        t.matmul(out=dst[:, 0:512], lhsT=ones_row[:],
                                 rhs=arow_b[:, j * 512:(j + 1) * 512],
                                 start=True, stop=True).then_inc(pe.h, 1)
                pe.bump(8)

                @block.scalar
                def _(s):
                    for j in range(8):
                        s.wait_ge(pe.h, peB + 10 + j)
                        src = psb[4] if j % 2 == 0 else psb[5]
                        s.copy(out=bb_row[:, j * 512:(j + 1) * 512],
                               in_=src[:, 0:512]).then_inc(ac.h, 1)
                ac.bump(8)

                ldaB = [lda[0].v, lda[1].v]
                stnB = [stn[0].v, stn[1].v]
                veB2 = veB + 5
                K2_ = 3 if with_mean else 2

                @block.sync
                def _(sync):
                    sync.wait_ge(ac.h, acB + 19)
                    for c in range(NFC):
                        if c >= 2:
                            sync.wait_ge(ve.h, veB2 + K2_ * (c - 2) + K2_)
                        sync.dma_start(out=xch[:, c % 2, :],
                                       in_=_rows(src_dram, c)).then_inc(lda[c % 2].h, 16)
                        lda[c % 2].bump(16)

                K_ = 3 if with_mean else 2

                @block.vector
                def _(v):
                    for c in range(NFC):
                        v.wait_ge(lda[c % 2].h, ldaB[c % 2] + 16 * (c // 2 + 1))
                        if c >= 2:
                            v.wait_ge(stn[c % 2].h, stnB[c % 2] + 16 * (c // 2))
                        if c >= 1:
                            v.wait_ge(ve.h, veB2 + K_ * (c - 1) + K_)
                        v.tensor_tensor(out=xt0[:], in0=xch[:, c % 2, :],
                                        in1=ab_row[:], op=ALU.mult).then_inc(ve.h, 1)
                        v.wait_ge(ve.h, veB2 + K_ * c + 1)
                        v.tensor_tensor(out=xnc[:, c % 2, :], in0=xt0[:],
                                        in1=bb_row[:], op=ALU.add).then_inc(ve.h, 1)
                        if with_mean:
                            v.wait_ge(ve.h, veB2 + K_ * c + 2)
                            v.tensor_reduce(out=mcol[:, c:c + 1], in_=xnc[:, c % 2, :],
                                            axis=mybir.AxisListType.X,
                                            op=ALU.add).then_inc(ve.h, 1)
                ve.bump(K_ * NFC)

                stnB2 = [stn[0].v, stn[1].v]
                gpB2 = gp.v

                @block.gpsimd
                def _(g):
                    for c in range(NFC):
                        g.wait_ge(ve.h, veB2 + K2_ * c + 2)
                        g.dma_start(out=_rows(xn_dram, c),
                                    in_=xnc[:, c % 2, :]).then_inc(stn[c % 2].h, 16)
                        stn[c % 2].bump(16)
                    if do_ag:
                        for q in range(4):
                            g.wait_ge(stn[0].h, stnB2[0] + 16 * (2 * q + 2))
                            g.wait_ge(stn[1].h, stnB2[1] + 16 * (2 * q + 2))
                            g.collective_compute(
                                "AllGather", ALU.bypass, replica_groups=PAIRS,
                                ins=[xn_dram[512 * q:512 * (q + 1), :].opt()],
                                outs=[agA[1024 * q:1024 * (q + 1), :].opt()]).then_inc(cc.h, 1)
                if do_ag:
                    cc.bump(4)

                end_phase({"gpsimd": [(stn[0].h, stn[0].v), (stn[1].h, stn[1].v)],
                           "vector": [(ve.h, ve.v)], "tensor": [(pe.h, pe.v)],
                           "scalar": [(ac.h, ac.v)],
                           "sync": [(lda[0].h, lda[0].v), (lda[1].h, lda[1].v)]})

        def emit_attn_a(xsrc_dram):
            xn3d = xn_dram.rearrange("(c p) t -> p c t", p=128)
            xs3d = xsrc_dram.rearrange("(c p) t -> p c t", p=128)
            rs3d = resid.rearrange("(c p) t -> p c t", p=128)
            ntks = [2 * (tq + 1) for tq in range(NTQ)]

            with ExitStack() as ph:
                Ks = mk(ph, "aa_Ks", [128, 8, T], BF16)
                Vt = mk(ph, "aa_Vt", [128, 32, 1024], BF16)
                diag = mk(ph, "aa_diag", [128, 8, 128], BF16)
                raw = mk(ph, "aa_raw", [128, 2, T], BF16)
                qt = mk(ph, "aa_qt", [128, 2, 8, 256], BF16)
                sig = mk(ph, "aa_sig", [128, 2, 256], BF16)
                sigm = mk(ph, "aa_sigm", [128, 2, 256], BF16)
                xres = mk(ph, "aa_xres", [128, 2, 8, 256], F32)
                x2t = mk(ph, "aa_x2t", [128, 2, 256], F32)
                xb2 = mk(ph, "aa_xb2", [128, 2, 256], BF16)
                xq2 = mk(ph, "aa_xq2", [128, 2, 256], BF16)

                first_stats = [True]

                for g in range(2):
                    peB, veB, acB = pe.v, ve.v, ac.v
                    ldaB = [lda[0].v, lda[1].v]

                    @block.sync
                    def _(sync, g=g, peB=peB, acB=acB):
                        for dc in range(8):
                            if dc >= 2:
                                sync.wait_ge(pe.h, peB + 8 * (dc - 1))
                                sync.wait_ge(ac.h, acB + 9 * (dc - 1))
                            sync.dma_start(out=raw[:, dc % 2, :],
                                           in_=xn3d[:, 8 * g + dc, :]).then_inc(lda[dc % 2].h, 16)
                            lda[dc % 2].bump(16)

                    @block.vector
                    def _(v, g=g, peB=peB, veB=veB):
                        if g == 1:
                            v.wait_ge(pe.h, peB)
                        for dc in range(8):
                            v.tensor_scalar(out=diag[:, dc, :], in0=id_bf[:],
                                            scalar1=vcol[:, 2 * dc + g:2 * dc + g + 1],
                                            scalar2=None, op0=ALU.mult).then_inc(ve.h, 1)
                    ve.bump(8)

                    def _ks_ac(dc):
                        return acB + dc * 9 + 1

                    def _ev_ac(gi):
                        return acB + (gi // 8) * 9 + 1 + (gi % 8) + 1

                    @block.scalar
                    def _(s, g=g, peB=peB, acB=acB, ldaB=ldaB):
                        if g == 1:
                            s.wait_ge(pe.h, peB)
                        for dc in range(8):
                            s.wait_ge(lda[dc % 2].h, ldaB[dc % 2] + 16 * (dc // 2 + 1))
                            s.mul(out=Ks[:, dc, :], in_=raw[:, dc % 2, :],
                                  mul=qkcol[:, 2 * dc + g:2 * dc + g + 1]).then_inc(ac.h, 1)
                            for i4 in range(8):
                                gi = dc * 8 + i4
                                bank = psb[4] if gi % 2 == 0 else psb[5]
                                s.wait_ge(pe.h, peB + gi + 1)
                                s.copy(out=Vt[:, i4 * 4:(i4 + 1) * 4, dc * 128:(dc + 1) * 128],
                                       in_=bank[:, 0:512].rearrange("p (k c) -> p k c", k=4)
                                       ).then_inc(ac.h, 1)
                    ac.bump(72)

                    @block.tensor
                    def _(t, g=g, peB=peB, veB=veB, acB=acB, ldaB=ldaB):
                        for dc in range(8):
                            t.wait_ge(ve.h, veB + dc + 1)
                            t.wait_ge(lda[dc % 2].h, ldaB[dc % 2] + 16 * (dc // 2 + 1))
                            for i4 in range(8):
                                gi = dc * 8 + i4
                                bank = psb[4] if gi % 2 == 0 else psb[5]
                                if gi >= 2:
                                    t.wait_ge(ac.h, _ev_ac(gi - 2))
                                for k in range(4):
                                    i = i4 * 4 + k
                                    mm = t.matmul(out=bank[:, k * 128:(k + 1) * 128],
                                                  lhsT=raw[:, dc % 2, i * 128:(i + 1) * 128],
                                                  rhs=diag[:, dc, :],
                                                  start=(k == 0), stop=(k == 3))
                                mm.then_inc(pe.h, 1)
                    pe.bump(64)



                    peS, acS, veS = pe.v, ac.v, ve.v
                    ldqB = [ldq[0].v, ldq[1].v]
                    ldxB = [ldx[0].v, ldx[1].v]
                    stxB = [stx[0].v, stx[1].v]

                    plans = []
                    for tq in range(NTQ):
                        ntk = ntks[tq]
                        seq = [("S", 0), ("S", 1)]
                        for k in range(2, ntk):
                            seq.append(("PV", k - 2))
                            seq.append(("S", k))
                        seq.append(("PV", ntk - 2))
                        seq.append(("PV", ntk - 1))
                        seq += [("T", dc) for dc in range(8)]
                        plans.append(seq)
                    pe_pos = {}
                    pc = peS
                    for tq, seq in enumerate(plans):
                        for ent in seq:
                            pc += 1
                            pe_pos[(tq,) + ent] = pc
                    pe_total = pc
                    ac_pos = {}
                    a = acS
                    for tq in range(NTQ):
                        for tk in range(ntks[tq]):
                            a += 1
                            ac_pos[(tq, tk)] = a
                    ac_total = a
                    ve_pos = {}
                    w = veS
                    for tq in range(NTQ):
                        for j in range(2):
                            w += 1
                            ve_pos[(tq, "m", j)] = w
                        for dc in range(8):
                            w += 1
                            ve_pos[(tq, "a", dc)] = w
                            w += 1
                            ve_pos[(tq, "q", dc)] = w
                    ve_total = w

                    @block.sync
                    def _(sync, g=g, peS=peS, ldqB=ldqB, ldxB=ldxB):
                        if g == 1:
                            sync.wait_ge(pe.h, peS - 64)
                        for tq in range(NTQ):
                            if tq >= 2:
                                sync.wait_ge(pe.h, pe_pos[(tq - 2, "PV", ntks[tq - 2] - 1)])
                                sync.wait_ge(ve.h, ve_pos[(tq - 2, "q", 7)])
                            sync.dma_start(out=qt[:, tq % 2, :, :],
                                           in_=xn3d[:, 8 * g:8 * g + 8, tq * 256:(tq + 1) * 256]
                                           ).then_inc(ldq[tq % 2].h, 16)
                            ldq[tq % 2].bump(16)
                            sync.dma_start(out=xres[:, tq % 2, :, :],
                                           in_=xs3d[:, 8 * g:8 * g + 8, tq * 256:(tq + 1) * 256]
                                           ).then_inc(ldx[tq % 2].h, 16)
                            ldx[tq % 2].bump(16)

                    @block.tensor
                    def _(t, g=g, peS=peS, acS=acS, veS=veS, ldqB=ldqB):
                        t.wait_ge(ac.h, acS)
                        for tq in range(NTQ):
                            ntk = ntks[tq]
                            t.wait_ge(ldq[tq % 2].h, ldqB[tq % 2] + 16 * (tq // 2 + 1))
                            for ent in plans[tq]:
                                kind, idx = ent
                                if kind == "S":
                                    tk = idx
                                    sb = psb[4] if tk % 2 == 0 else psb[5]
                                    for dc in range(8):
                                        mm = t.matmul(out=sb[:, 0:256],
                                                      lhsT=Ks[:, dc, tk * 128:(tk + 1) * 128],
                                                      rhs=qt[:, tq % 2, dc, :],
                                                      start=(dc == 0), stop=(dc == 7))
                                    mm.then_inc(pe.h, 1)
                                elif kind == "PV":
                                    tk = idx
                                    masked = (tk >= ntk - 2)
                                    if tk == 0:
                                        if tq > 0:
                                            t.wait_ge(ve.h, ve_pos[(tq - 1, "q", 7)])
                                        elif g == 1:
                                            t.wait_ge(ve.h, veS)
                                    if masked:
                                        j = tk - (ntk - 2)
                                        t.wait_ge(ve.h, ve_pos[(tq, "m", j)])
                                        src = sigm[:, j, :]
                                    else:
                                        t.wait_ge(ac.h, ac_pos[(tq, tk)])
                                        src = sig[:, tk % 2, :]
                                    for dc in range(8):
                                        mm = t.matmul(
                                            out=psb[dc // 2][:, (dc % 2) * 256:(dc % 2) * 256 + 256],
                                            lhsT=Vt[:, tk, dc * 128:(dc + 1) * 128],
                                            rhs=src,
                                            start=(tk == 0 and dc % 2 == 0),
                                            stop=(tk == ntk - 1 and dc % 2 == 1))
                                    mm.then_inc(pe.h, 1)
                                else:
                                    dc = idx
                                    t.wait_ge(ve.h, ve_pos[(tq, "q", dc)])
                                    for ii in range(2):
                                        i = 2 * tq + ii
                                        fs = first_stats[0]
                                        t.matmul(out=ps_stat[:, i:i + 1],
                                                 lhsT=xb2[:, dc % 2, ii * 128:(ii + 1) * 128],
                                                 rhs=ones_bf[:, 0:1], start=fs, stop=False)
                                        first_stats[0] = False
                                        mm = t.matmul(out=ps_stat[:, 32 + i:33 + i],
                                                      lhsT=xq2[:, dc % 2, ii * 128:(ii + 1) * 128],
                                                      rhs=ones_bf[:, 0:1], start=False,
                                                      stop=(g == 1 and tq == NTQ - 1
                                                            and dc == 7 and ii == 1))
                                    mm.then_inc(pe.h, 1)

                    @block.scalar
                    def _(s, g=g, peS=peS, acS=acS, veS=veS):
                        for tq in range(NTQ):
                            ntk = ntks[tq]
                            for tk in range(ntk):
                                s.wait_ge(pe.h, pe_pos[(tq, "S", tk)])
                                if tk >= 2:
                                    s.wait_ge(pe.h, pe_pos[(tq, "PV", tk - 2)])
                                elif tq > 0:
                                    s.wait_ge(pe.h, pe_pos[(tq - 1, "PV", ntks[tq - 1] - 1)])
                                    s.wait_ge(ve.h, ve_pos[(tq - 1, "m", 1)])
                                elif g == 1:
                                    s.wait_ge(pe.h, peS - 64)
                                    s.wait_ge(ve.h, veS)
                                sb = psb[4] if tk % 2 == 0 else psb[5]
                                s.activation(out=sig[:, tk % 2, :], in_=sb[:, 0:256],
                                             func=AF.Sigmoid,
                                             scale=sc125[:, 0:1]).then_inc(ac.h, 1)

                    @block.vector
                    def _(v, g=g, peS=peS, veS=veS, ldxB=ldxB, stxB=stxB):
                        if g == 1:
                            v.wait_ge(pe.h, peS - 64)
                        for tq in range(NTQ):
                            ntk = ntks[tq]
                            for j in range(2):
                                tk = ntk - 2 + j
                                v.wait_ge(ac.h, ac_pos[(tq, tk)])
                                v.tensor_tensor(out=sigm[:, j, :], in0=sig[:, tk % 2, :],
                                                in1=amask_sb[:, j, :], op=ALU.mult).then_inc(ve.h, 1)
                            v.wait_ge(pe.h, pe_pos[(tq, "PV", ntk - 1)])
                            v.wait_ge(ldx[tq % 2].h, ldxB[tq % 2] + 16 * (tq // 2 + 1))
                            for dc in range(8):
                                if tq > 0 or dc >= 2:
                                    ptq, pdc = (tq, dc - 2) if dc >= 2 else (tq - 1, dc + 6)
                                    v.wait_ge(pe.h, pe_pos[(ptq, "T", pdc)])
                                    v.wait_ge(stx[dc % 2].h,
                                              stxB[dc % 2] + 16 * ((tq * 8 + dc) // 2))
                                elif g == 1:
                                    v.wait_ge(stx[dc % 2].h, stxB[dc % 2])
                                v.tensor_tensor(out=x2t[:, dc % 2, :],
                                                in0=psb[dc // 2][:, (dc % 2) * 256:(dc % 2) * 256 + 256],
                                                in1=xres[:, tq % 2, dc, :], op=ALU.add).then_inc(ve.h, 1)
                                v.wait_ge(ve.h, ve_pos[(tq, "a", dc)])
                                v.tensor_copy(out=xb2[:, dc % 2, :], in_=x2t[:, dc % 2, :])
                                v.tensor_tensor(out=xq2[:, dc % 2, :], in0=x2t[:, dc % 2, :],
                                                in1=x2t[:, dc % 2, :], op=ALU.mult).then_inc(ve.h, 1)

                    @block.gpsimd
                    def _(g_, g_head=g, veS=veS, stxB=stxB):
                        for tq in range(NTQ):
                            for dc in range(8):
                                g_.wait_ge(ve.h, ve_pos[(tq, "a", dc)])
                                g_.dma_start(out=rs3d[:, 8 * g_head + dc, tq * 256:(tq + 1) * 256],
                                             in_=x2t[:, dc % 2, :]).then_inc(stx[dc % 2].h, 16)
                                stx[dc % 2].bump(16)

                    pe.v = pe_total
                    ac.v = ac_total
                    ve.v = ve_total

                end_phase({"gpsimd": [(stx[0].h, stx[0].v), (stx[1].h, stx[1].v)],
                           "vector": [(ve.h, ve.v)], "tensor": [(pe.h, pe.v)],
                           "scalar": [(ac.h, ac.v)],
                           "sync": [(ldq[0].h, ldq[0].v), (ldq[1].h, ldq[1].v),
                                    (ldx[0].h, ldx[0].v), (ldx[1].h, ldx[1].v),
                                    (lda[0].h, lda[0].v), (lda[1].h, lda[1].v)]})


        ldf4 = [SEM("s_ldf0"), SEM("s_ldf1"), SEM("s_ldf2"), SEM("s_ldf3")]
        gq = SEM("s_gq"); gw = SEM("s_gw")

        def emit_ffn(ccB_ag, first_ffn, src_ag, wT_d, dst_dram,
                     with_lora=True, do_stats=True):
            peB, veB, acB, gpB, ccB = pe.v, ve.v, ac.v, gp.v, cc.v
            gqB, gwB = gq.v, gw.v
            ldxB = [ldx[0].v, ldx[1].v]
            stxB = [stx[0].v, stx[1].v]
            ldfB = [s.v for s in ldf4]
            ford = [c for q in range(4)
                    for c in (list(range(4 * q, 4 * q + 4)) + list(range(16 + 4 * q, 16 + 4 * q + 4)))]
            ag3d = src_ag.rearrange("(c p) t -> p c t", p=128)
            rs3d = resid.rearrange("(c p) t -> p c t", p=128)
            ds3d = dst_dram.rearrange("(c p) t -> p c t", p=128)
            wo3d = wT_d.rearrange("(c p) o -> p c o", p=128)
            dn3d = down_d.rearrange("(c p) r -> p c r", p=128)
            up3d = up_d.rearrange("(r p) o -> p r o", p=128)
            wq3d = wqT_d.rearrange("(c p) k -> p c k", p=128)

            with ExitStack() as ph:
                xq = mk(ph, "ff_xq", [128, 32, 1024], BF16)
                Wp = mk(ph, "ff_Wp", [128, 32, 512], BF16)
                dn = mk(ph, "ff_dn", [128, 32, 512], BF16)
                up = mk(ph, "ff_up", [128, 4, 2048], BF16)
                hTr = mk(ph, "ff_hTr", [128, 4, 1024], BF16)
                hT = mk(ph, "ff_hT", [128, 4, 1024], BF16)
                wq_sb = mk(ph, "ff_wq", [128, 16, 128], F32)
                keys_sb = mk(ph, "ff_keys", [128, 8], F32)
                sel_sb = mk(ph, "ff_sel", [8, 4, 128], F32)
                qpart = mk(ph, "ff_qp", [128, 1], F32)
                qTc = mk(ph, "ff_qT", [128, 1], F32)
                srow = mk(ph, "ff_srow", [1, 40], F32)
                scol = mk(ph, "ff_scol", [8, 1], F32)
                ccol = mk(ph, "ff_ccol", [8, 1], F32)
                xr2 = mk(ph, "ff_xr2", [128, 2, 4, 512], F32)
                x3t = mk(ph, "ff_x3t", [128, 2, 512], F32)
                xb3 = mk(ph, "ff_xb3", [128, 2, 512], BF16)
                xq3 = mk(ph, "ff_xq3", [128, 2, 512], BF16)

                pe_pos = {}
                pc = peB
                if with_lora:
                    pc += 1; pe_pos["qmm"] = pc
                for tqr in range(4):
                    if with_lora:
                        for rnd in range(8):
                            pc += 1; pe_pos[("h", tqr, rnd)] = pc
                        if tqr == 0:
                            pc += 1; pe_pos["scoresT"] = pc
                            pc += 1; pe_pos["sT"] = pc
                            pc += 1; pe_pos["cT"] = pc
                            pc += 1; pe_pos["sel"] = pc
                    for og in range(4):
                        for tch in range(2):
                            pc += 1; pe_pos[("b", tqr, og, tch)] = pc
                            if do_stats:
                                for j in range(4):
                                    pc += 1; pe_pos[("t", tqr, og, tch, j)] = pc
                pe_total = pc
                ac_pos = {}
                a = acB
                if with_lora:
                    a += 1; ac_pos["qpart"] = a
                    for rnd in range(8):
                        a += 1; ac_pos[("hev", 0, rnd)] = a
                    for nm in ("scol", "srow", "exp", "ccol", "cvec"):
                        a += 1; ac_pos[nm] = a
                    for tqr in range(1, 4):
                        for rnd in range(8):
                            a += 1; ac_pos[("hev", tqr, rnd)] = a
                ac_total = a
                ve_pos = {}
                w = veB
                if with_lora:
                    for nm in ("mx", "bias", "sum", "rec", "cf"):
                        w += 1; ve_pos[nm] = w
                for tqr in range(4):
                    if with_lora:
                        for rnd in range(8):
                            w += 1; ve_pos[("hs", tqr, rnd)] = w
                    for og in range(4):
                        for tch in range(2):
                            for j in range(4):
                                w += 1; ve_pos[("a", tqr, og, tch, j)] = w
                                if do_stats:
                                    w += 1
                                ve_pos[("q", tqr, og, tch, j)] = w
                ve_total = w
                gp_pos = {}
                gg = gpB
                for tqr in range(4):
                    for og in range(4):
                        gg += 16; gp_pos[("wp", tqr, og)] = gg
                gp_total = gg

                def evac_idx(tqr, og, tch, j):
                    return ((tqr * 4 + og) * 2 + tch) * 4 + j

                @block.sync
                def _(sync):
                    if with_lora:
                        sync.dma_start(out=wq_sb[:], in_=wq3d).then_inc(ld.h, 16)
                        sync.dma_start(out=keys_sb[:], in_=keysT_d).then_inc(ld.h, 16)
                        sync.dma_start(out=sel_sb[:], in_=sel_d.rearrange("g e k -> e g k")).then_inc(ld.h, 16)
                        ld.bump(48)
                    ii = 0
                    for tqr in range(4):
                        if tqr > 0:
                            key = ("t", tqr - 1, 3, 1, 3) if do_stats else ("b", tqr - 1, 3, 1)
                            sync.wait_ge(pe.h, pe_pos[key])
                        for q in range(4):
                            sync.wait_ge(cc.h, ccB_ag + q + 1)
                            ts = slice(tqr * 1024, (tqr + 1) * 1024)
                            sync.dma_start(out=xq[:, 8 * q:8 * q + 8, :],
                                           in_=ag3d[:, 8 * q:8 * q + 8, ts]).then_inc(ldf4[q].h, 16)
                            ldf4[q].bump(16)
                        for og in range(4):
                            for tch in range(2):
                                ii = (tqr * 4 + og) * 2 + tch
                                if ii >= 2:
                                    pv = ii - 2
                                    ptqr, pog, ptch = pv // 8, (pv // 2) % 4, pv % 2
                                    sync.wait_ge(ve.h, ve_pos[("q", ptqr, pog, ptch, 3)])
                                sync.dma_start(
                                    out=xr2[:, ii % 2, :, :],
                                    in_=rs3d[:, og * 4:(og + 1) * 4,
                                             tqr * 1024 + tch * 512:tqr * 1024 + (tch + 1) * 512]
                                ).then_inc(ldx[ii % 2].h, 16)
                                ldx[ii % 2].bump(16)

                @block.gpsimd
                def _(g):
                    if with_lora:
                        g.dma_start(out=dn[:], in_=dn3d).then_inc(gw.h, 16)
                        g.dma_start(out=up[:], in_=up3d).then_inc(gw.h, 16)
                        g.wait_ge(ac.h, ac_pos["qpart"])
                        g.dma_start(out=arq_i, in_=qpart[:]).then_inc(gq.h, 16)
                        g.wait_ge(gq.h, gqB + 16)
                        g.collective_compute("AllReduce", ALU.add, replica_groups=PAIRS,
                                             ins=[arq_i.opt()], outs=[arq_o.opt()]).then_inc(cc.h, 1)
                        g.wait_ge(cc.h, ccB + 1)
                        g.dma_start(out=qTc[:], in_=arq_o).then_inc(gq.h, 16)
                    sii = 0
                    for tqr in range(4):
                        for og in range(4):
                            if tqr + og > 0:
                                pv = tqr * 4 + og - 1
                                g.wait_ge(pe.h, pe_pos[("b", pv // 4, pv % 4, 1)])
                            g.dma_start(out=Wp[:],
                                        in_=wo3d[:, :, (og % 4) * 512:(og % 4 + 1) * 512]
                                        ).then_inc(gp.h, 16)
                            for tch in range(2):
                                for j in range(4):
                                    g.wait_ge(ve.h, ve_pos[("a", tqr, og, tch, j)])
                                    g.dma_start(
                                        out=ds3d[:, og * 4 + j,
                                                 tqr * 1024 + tch * 512:tqr * 1024 + (tch + 1) * 512],
                                        in_=x3t[:, sii % 2, :]).then_inc(stx[sii % 2].h, 16)
                                    stx[sii % 2].bump(16)
                                    sii += 1
                if with_lora:
                    cc.bump(1)
                    gq.bump(32)
                    gw.bump(32)
                gp.v = gp_total

                @block.tensor
                def _(t):
                    if with_lora:
                        t.wait_ge(ld.h, ld.v)
                        for c in range(16):
                            mm = t.matmul(out=psb[0][:, 0:1], lhsT=wq_sb[:, c, :],
                                          rhs=mcol[:, c:c + 1], start=(c == 0), stop=(c == 15))
                        mm.then_inc(pe.h, 1)
                    for tqr in range(4):
                        for q in range(4):
                            t.wait_ge(ldf4[q].h, ldfB[q] + 16 * (tqr + 1))
                        if with_lora:
                            for rnd in range(8):
                                g4, tch = rnd // 2, rnd % 2
                                bank = psb[4] if rnd % 2 == 0 else psb[5]
                                if (tqr, rnd) > (0, 1):
                                    pv = tqr * 8 + rnd - 2
                                    t.wait_ge(ac.h, ac_pos[("hev", pv // 8, pv % 8)])
                                if tqr == 0 and rnd == 0:
                                    t.wait_ge(gw.h, gwB + 32)
                                for pi in range(32):
                                    fc = ford[pi]
                                    mm = t.matmul(out=bank[:, 0:512],
                                                  lhsT=dn[:, fc, g4 * 128:(g4 + 1) * 128],
                                                  rhs=xq[:, pi, tch * 512:(tch + 1) * 512],
                                                  start=(pi == 0), stop=(pi == 31))
                                mm.then_inc(pe.h, 1)
                            if tqr == 0:
                                t.wait_ge(gq.h, gqB + 32)
                                t.matmul(out=psb[1][0:8, 0:1], lhsT=keys_sb[:],
                                         rhs=qTc[:], start=True, stop=True).then_inc(pe.h, 1)
                                t.wait_ge(ac.h, ac_pos["scol"])
                                t.transpose(out=psb[2][0:1, 0:8], in_=scol[:],
                                            identity=id_f32[0:8, 0:8]).then_inc(pe.h, 1)
                                t.wait_ge(ve.h, ve_pos["cf"])
                                t.transpose(out=psb[3][0:8, 0:1], in_=srow[0:1, 26:34],
                                            identity=id_f32[0:1, 0:1]).then_inc(pe.h, 1)
                                t.wait_ge(ac.h, ac_pos["ccol"])
                                for g4 in range(4):
                                    mm = t.matmul(out=psb[1][:, g4:g4 + 1], lhsT=sel_sb[:, g4, :],
                                                  rhs=ccol[:], start=(g4 == 0), stop=(g4 == 3))
                                mm.then_inc(pe.h, 1)
                        for og in range(4):
                            t.wait_ge(gp.h, gp_pos[("wp", tqr, og)])
                            if with_lora and tqr == 0 and og == 0:
                                t.wait_ge(ac.h, ac_pos["cvec"])
                            for tch in range(2):
                                ii = (tqr * 4 + og) * 2 + tch
                                lag = 2 if do_stats else 1
                                if ii >= lag:
                                    pv = ii - lag
                                    t.wait_ge(ve.h, ve_pos[("q", pv // 8, (pv // 2) % 4, pv % 2, 3)])
                                for pi in range(32):
                                    fc = ford[pi]
                                    for j in range(4):
                                        mm = t.matmul(out=psb[j][:, 0:512],
                                                      lhsT=Wp[:, fc, j * 128:(j + 1) * 128],
                                                      rhs=xq[:, pi, tch * 512:(tch + 1) * 512],
                                                      start=(pi == 0),
                                                      stop=(not with_lora and pi == 31))
                                if with_lora:
                                    t.wait_ge(ve.h, ve_pos[("hs", tqr, 6 + tch)])
                                    for j in range(4):
                                        for rt in range(4):
                                            mm = t.matmul(
                                                out=psb[j][:, 0:512],
                                                lhsT=up[:, rt, (og * 4 + j) * 128:(og * 4 + j + 1) * 128],
                                                rhs=hT[:, rt, tch * 512:(tch + 1) * 512],
                                                start=False, stop=(rt == 3))
                                mm.then_inc(pe.h, 1)
                                if do_stats:
                                    for j in range(4):
                                        t.wait_ge(ve.h, ve_pos[("q", tqr, og, tch, j)])
                                        for k in range(4):
                                            it = tqr * 8 + tch * 4 + k
                                            fs = first_ffn[0]
                                            t.matmul(out=ps_stat[:, it:it + 1],
                                                     lhsT=xb3[:, (evac_idx(tqr, og, tch, j) % 2),
                                                              k * 128:(k + 1) * 128],
                                                     rhs=ones_bf[:, 0:1], start=fs, stop=False)
                                            first_ffn[0] = False
                                            mm = t.matmul(out=ps_stat[:, 32 + it:33 + it],
                                                          lhsT=xq3[:, (evac_idx(tqr, og, tch, j) % 2),
                                                                   k * 128:(k + 1) * 128],
                                                          rhs=ones_bf[:, 0:1], start=False,
                                                          stop=(tqr == 3 and og == 3 and tch == 1
                                                                and j == 3 and k == 3))
                                        mm.then_inc(pe.h, 1)

                @block.scalar
                def _(s):
                    if not with_lora:
                        return

                    def hev(tqr, rnd):
                        bank = psb[4] if rnd % 2 == 0 else psb[5]
                        g4, tch = rnd // 2, rnd % 2
                        s.wait_ge(pe.h, pe_pos[("h", tqr, rnd)])
                        s.copy(out=hTr[:, g4, tch * 512:(tch + 1) * 512],
                               in_=bank[:, 0:512]).then_inc(ac.h, 1)

                    s.wait_ge(pe.h, pe_pos["qmm"])
                    s.copy(out=qpart[:], in_=psb[0][:, 0:1]).then_inc(ac.h, 1)
                    for rnd in range(8):
                        hev(0, rnd)
                    s.wait_ge(pe.h, pe_pos["scoresT"])
                    s.copy(out=scol[:], in_=psb[1][0:8, 0:1]).then_inc(ac.h, 1)
                    s.wait_ge(pe.h, pe_pos["sT"])
                    s.copy(out=srow[0:1, 0:8], in_=psb[2][0:1, 0:8]).then_inc(ac.h, 1)
                    s.wait_ge(ve.h, ve_pos["bias"])
                    s.activation(out=srow[0:1, 16:24], in_=srow[0:1, 0:8], func=AF.Exp,
                                 scale=scC[0:1, 0:1], bias=srow[0:1, 9:10]).then_inc(ac.h, 1)
                    s.wait_ge(pe.h, pe_pos["cT"])
                    s.copy(out=ccol[:], in_=psb[3][0:8, 0:1]).then_inc(ac.h, 1)
                    s.wait_ge(pe.h, pe_pos["sel"])
                    s.copy(out=cvec[:], in_=psb[1][:, 0:4]).then_inc(ac.h, 1)
                    for tqr in range(1, 4):
                        for rnd in range(8):
                            hev(tqr, rnd)
                ac.v = ac_total

                @block.vector
                def _(v):
                    if with_lora:
                        v.wait_ge(ac.h, ac_pos["srow"])
                        v.tensor_reduce(out=srow[0:1, 8:9], in_=srow[0:1, 0:8],
                                        axis=mybir.AxisListType.X, op=ALU.max).then_inc(ve.h, 1)
                        v.wait_ge(ve.h, ve_pos["mx"])
                        v.tensor_scalar(out=srow[0:1, 9:10], in0=srow[0:1, 8:9],
                                        scalar1=scC[0:1, 1:2], scalar2=None,
                                        op0=ALU.mult).then_inc(ve.h, 1)
                        v.wait_ge(ac.h, ac_pos["exp"])
                        v.tensor_reduce(out=srow[0:1, 24:25], in_=srow[0:1, 16:24],
                                        axis=mybir.AxisListType.X, op=ALU.add).then_inc(ve.h, 1)
                        v.wait_ge(ve.h, ve_pos["sum"])
                        v.reciprocal(out=srow[0:1, 25:26], in_=srow[0:1, 24:25]).then_inc(ve.h, 1)
                    if with_lora:
                        v.wait_ge(ve.h, ve_pos["rec"])
                        v.tensor_scalar(out=srow[0:1, 26:34], in0=srow[0:1, 16:24],
                                        scalar1=srow[0:1, 25:26], scalar2=None,
                                        op0=ALU.mult).then_inc(ve.h, 1)
                    for tqr in range(4):
                        if with_lora:
                            for rnd in range(8):
                                g4, tch = rnd // 2, rnd % 2
                                v.wait_ge(ac.h, ac_pos[("hev", tqr, rnd)])
                                if tqr == 0 and rnd == 0:
                                    v.wait_ge(ac.h, ac_pos["cvec"])
                                v.tensor_scalar(out=hT[:, g4, tch * 512:(tch + 1) * 512],
                                                in0=hTr[:, g4, tch * 512:(tch + 1) * 512],
                                                scalar1=cvec[:, g4:g4 + 1], scalar2=None,
                                                op0=ALU.mult).then_inc(ve.h, 1)
                        for og in range(4):
                            for tch in range(2):
                                ii = (tqr * 4 + og) * 2 + tch
                                v.wait_ge(pe.h, pe_pos[("b", tqr, og, tch)])
                                v.wait_ge(ldx[ii % 2].h, ldxB[ii % 2] + 16 * (ii // 2 + 1))
                                for j in range(4):
                                    ei = evac_idx(tqr, og, tch, j)
                                    if ei >= 2:
                                        pj = ei - 2
                                        ptqr = pj // 32
                                        prem = pj % 32
                                        pog, ptch, ppj = prem // 8, (prem // 4) % 2, prem % 4
                                        if do_stats:
                                            v.wait_ge(pe.h, pe_pos[("t", ptqr, pog, ptch, ppj)])
                                        v.wait_ge(stx[ei % 2].h, stxB[ei % 2] + 16 * (ei // 2))
                                    v.tensor_tensor(out=x3t[:, ei % 2, :],
                                                    in0=psb[j][:, 0:512],
                                                    in1=xr2[:, ii % 2, j, :],
                                                    op=ALU.add).then_inc(ve.h, 1)
                                    if do_stats:
                                        v.wait_ge(ve.h, ve_pos[("a", tqr, og, tch, j)])
                                        v.tensor_copy(out=xb3[:, ei % 2, :], in_=x3t[:, ei % 2, :])
                                        v.tensor_tensor(out=xq3[:, ei % 2, :], in0=x3t[:, ei % 2, :],
                                                        in1=x3t[:, ei % 2, :],
                                                        op=ALU.mult).then_inc(ve.h, 1)
                ve.v = ve_total
                pe.v = pe_total

                end_phase({"gpsimd": [(stx[0].h, stx[0].v), (stx[1].h, stx[1].v)],
                           "vector": [(ve.h, ve.v)], "tensor": [(pe.h, pe.v)],
                           "scalar": [(ac.h, ac.v)],
                           "sync": [(ldx[0].h, ldx[0].v), (ldx[1].h, ldx[1].v)] +
                                   [(s_.h, s_.v) for s_ in ldf4]})


        spq = [SEM("s_spq0"), SEM("s_spq1")]
        kvq = [SEM("s_kvq0"), SEM("s_kvq1")]
        ldfb = [SEM("s_ldfb0"), SEM("s_ldfb1"), SEM("s_ldfb2"), SEM("s_ldfb3")]

        def emit_attn_b(ccB_ag):
            ag3d = agA.rearrange("(c p) t -> p c t", p=128)
            q3d = qsp.rearrange("(c p) t -> p c t", p=128)
            k3d = ksp.rearrange("(c p) t -> p c t", p=128)
            ob3d = obsp.rearrange("(c p) t -> p c t", p=128)
            w3ds = [qwT_d.rearrange("(c p) o -> p c o", p=128),
                    kwT_d.rearrange("(c p) o -> p c o", p=128),
                    vwT_d.rearrange("(c p) o -> p c o", p=128)]
            ford = [c for q in range(4)
                    for c in (list(range(4 * q, 4 * q + 4)) + list(range(16 + 4 * q, 16 + 4 * q + 4)))]
            def text_rng(r):
                lo = max(513, 512 * r) - 512 * r
                hi = min(4040, 512 * (r + 1)) - 512 * r
                return (lo, hi) if hi > lo else None

            with ExitStack() as pho:
                visb = mk(pho, "ab_visb", [128, 33], F32)
                scD = mk(pho, "ab_scD", [128, 1], F32)
                qbars = mk(pho, "ab_qbars", [128, 16], F32)

                @block.vector
                def _(v):
                    v.memset(scD[:], 1.0 / math.sqrt(128.0))
                    v.memset(qbars[:], 0.0).then_inc(ve.h, 1)
                ve.bump(1)

                with ExitStack() as ph1:
                    xsl = mk(ph1, "ab_xsl", [128, 2, 32, 512], BF16)
                    Wp = mk(ph1, "ab_Wp", [128, 32, 512], BF16)
                    vtm = mk(ph1, "ab_vtm", [128, 4, 32, 128], BF16)
                    vev = mk(ph1, "ab_vev", [128, 2, 512], BF16)
                    ropec = mk(ph1, "ab_rc", [128, T], BF16)
                    ropes = mk(ph1, "ab_rs", [128, T], BF16)
                    rsw = mk(ph1, "ab_rsw", [128, 2, 512], F32)
                    rt1 = mk(ph1, "ab_rt1", [128, 512], F32)
                    rt2 = mk(ph1, "ab_rt2", [128, 512], F32)
                    qtmp = mk(ph1, "ab_qtmp", [128, 1], F32)
                    robuf = mk(ph1, "ab_rob", [128, 2, 512], BF16)
                    kv32 = mk(ph1, "ab_kv32", [128, 2, 512], F32)

                    @block.sync
                    def _(sync):
                        sync.dma_start(out=ropec[:], in_=ropec_d).then_inc(ld.h, 16)
                        sync.dma_start(out=ropes[:], in_=ropes_d).then_inc(ld.h, 16)
                    ld.bump(32)
                    ld_rope = ld.v

                    for Wi in range(3):
                        for og in range(4):
                            peB_, acB_, veB_ = pe.v, ac.v, ve.v
                            ldfB_ = [s.v for s in ldf4]
                            ldfbB_ = [s.v for s in ldfb]
                            spqB_ = [spq[0].v, spq[1].v]

                            @block.gpsimd
                            def _(g, Wi=Wi, og=og, peB_=peB_):
                                g.wait_ge(pe.h, peB_)
                                g.dma_start(out=Wp[:],
                                            in_=w3ds[Wi][:, :, og * 512:(og + 1) * 512]
                                            ).then_inc(gp.h, 16)
                            gp.bump(16)
                            gpv = gp.v

                            nr = 1 if Wi < 2 else 5

                            @block.sync
                            def _(sync, Wi=Wi, og=og, peB_=peB_, nr=nr):
                                for t8 in range(8):
                                    if t8 < 2 and not (Wi == 0 and og == 0):
                                        sync.wait_ge(pe.h, peB_)
                                    if t8 >= 2:
                                        sync.wait_ge(pe.h, peB_ + nr * (t8 - 1))
                                    for q in range(4):
                                        if Wi == 0 and og == 0:
                                            sync.wait_ge(cc.h, ccB_ag + q + 1)
                                        sem = ldf4[q] if t8 % 2 == 0 else ldfb[q]
                                        sync.dma_start(out=xsl[:, t8 % 2, 8 * q:8 * q + 8, :],
                                                       in_=ag3d[:, 8 * q:8 * q + 8,
                                                                t8 * 512:(t8 + 1) * 512]
                                                       ).then_inc(sem.h, 16)
                                        sem.bump(16)

                            if Wi < 2:
                                ve_unit = []
                                ve_t2 = []
                                w = veB_
                                for r in range(8):
                                    for j in range(4):
                                        ve_t2.append(w + 2)
                                        w += 3
                                        if Wi == 0 and text_rng(r) is not None:
                                            w += 2
                                        ve_unit.append(w)
                                ve_tot_ = w

                                @block.tensor
                                def _(t, Wi=Wi, og=og, peB_=peB_, acB_=acB_, veB_=veB_,
                                      ldfB_=ldfB_, ldfbB_=ldfbB_, gpv=gpv, ve_unit=list(ve_unit)):
                                    t.wait_ge(gp.h, gpv)
                                    for r in range(8):
                                        for q in range(4):
                                            sem = ldf4[q] if r % 2 == 0 else ldfb[q]
                                            base = ldfB_[q] if r % 2 == 0 else ldfbB_[q]
                                            t.wait_ge(sem.h, base + 16 * (r // 2 + 1))
                                        if r >= 1:
                                            t.wait_ge(ac.h, acB_ + 4 * (r - 1) + 4)
                                            t.wait_ge(ve.h, ve_unit[4 * (r - 1) + 3])
                                        for pi in range(32):
                                            fc = ford[pi]
                                            for j in range(4):
                                                mm = t.matmul(out=psb[j][:, 0:512],
                                                              lhsT=Wp[:, fc, j * 128:(j + 1) * 128],
                                                              rhs=xsl[:, r % 2, pi, :],
                                                              start=(pi == 0), stop=(pi == 31))
                                        mm.then_inc(pe.h, 1)
                                pe.bump(8)

                                @block.scalar
                                def _(s, og=og, peB_=peB_, veB_=veB_, ve_t2=list(ve_t2)):
                                    for r in range(8):
                                        for j in range(4):
                                            s.wait_ge(pe.h, peB_ + r + 1)
                                            if r * 4 + j >= 2:
                                                pu = r * 4 + j - 2
                                                s.wait_ge(ve.h, ve_t2[pu])
                                            s.copy(out=rsw[0:64, j % 2, :],
                                                   in_=psb[j][64:128, 0:512])
                                            s.copy(out=rsw[64:128, j % 2, :],
                                                   in_=psb[j][0:64, 0:512]).then_inc(ac.h, 1)
                                ac.bump(32)

                                @block.vector
                                def _(v, Wi=Wi, og=og, peB_=peB_, acB_=acB_, veB_=veB_,
                                      spqB_=spqB_, ve_unit=list(ve_unit), ld_rope=ld_rope):
                                    if Wi == 0 and og == 0:
                                        v.wait_ge(ld.h, ld_rope)
                                    for r in range(8):
                                        ts = slice(r * 512, (r + 1) * 512)
                                        for j in range(4):
                                            u = 4 * r + j
                                            v.wait_ge(pe.h, peB_ + r + 1)
                                            v.wait_ge(ac.h, acB_ + 4 * r + j + 1)
                                            if u >= 2:
                                                v.wait_ge(spq[u % 2].h,
                                                          spqB_[u % 2] + 16 * (u // 2))
                                            elif og > 0 or Wi > 0:
                                                v.wait_ge(spq[u % 2].h, spqB_[u % 2])
                                            if u >= 1:
                                                v.wait_ge(ve.h, ve_unit[u - 1])
                                            v.tensor_tensor(out=rt1[:], in0=psb[j][:, 0:512],
                                                            in1=ropec[:, ts],
                                                            op=ALU.mult).then_inc(ve.h, 1)
                                            v.tensor_tensor(out=rt2[:], in0=rsw[:, j % 2, :],
                                                            in1=ropes[:, ts],
                                                            op=ALU.mult).then_inc(ve.h, 1)
                                            v.wait_ge(ve.h, ve_unit[u] - 1 -
                                                      (2 if (Wi == 0 and text_rng(r)) else 0))
                                            v.tensor_tensor(out=robuf[:, u % 2, :], in0=rt1[:],
                                                            in1=rt2[:], op=ALU.add).then_inc(ve.h, 1)
                                            if Wi == 0 and text_rng(r) is not None:
                                                lo, hi = text_rng(r)
                                                v.wait_ge(ve.h, ve_unit[u] - 2)
                                                v.tensor_reduce(out=qtmp[:], in_=robuf[:, u % 2, lo:hi],
                                                                axis=mybir.AxisListType.X,
                                                                op=ALU.add).then_inc(ve.h, 1)
                                                v.wait_ge(ve.h, ve_unit[u] - 1)
                                                v.tensor_tensor(out=qbars[:, 4 * og + j:4 * og + j + 1],
                                                                in0=qbars[:, 4 * og + j:4 * og + j + 1],
                                                                in1=qtmp[:], op=ALU.add).then_inc(ve.h, 1)
                                ve.v = ve_tot_

                                @block.gpsimd
                                def _(g, Wi=Wi, og=og, ve_unit=list(ve_unit)):
                                    dst = q3d if Wi == 0 else k3d
                                    for r in range(8):
                                        for j in range(4):
                                            u = 4 * r + j
                                            g.wait_ge(ve.h, ve_unit[u])
                                            g.dma_start(out=dst[:, og * 4 + j, r * 512:(r + 1) * 512],
                                                        in_=robuf[:, u % 2, :]
                                                        ).then_inc(spq[u % 2].h, 16)
                                            spq[u % 2].bump(16)
                            else:
                                @block.tensor
                                def _(t, og=og, peB_=peB_, acB_=acB_, ldfB_=ldfB_,
                                      ldfbB_=ldfbB_, gpv=gpv):
                                    t.wait_ge(gp.h, gpv)
                                    for r in range(8):
                                        for q in range(4):
                                            sem = ldf4[q] if r % 2 == 0 else ldfb[q]
                                            base = ldfB_[q] if r % 2 == 0 else ldfbB_[q]
                                            t.wait_ge(sem.h, base + 16 * (r // 2 + 1))
                                        if r >= 1:
                                            t.wait_ge(ac.h, acB_ + 8 * (r - 1) + 7)
                                        for pi in range(32):
                                            fc = ford[pi]
                                            for j in range(4):
                                                mm = t.matmul(out=psb[j][:, 0:512],
                                                              lhsT=Wp[:, fc, j * 128:(j + 1) * 128],
                                                              rhs=xsl[:, r % 2, pi, :],
                                                              start=(pi == 0), stop=(pi == 31))
                                        mm.then_inc(pe.h, 1)
                                        for j in range(4):
                                            t.wait_ge(ac.h, acB_ + 8 * r + 2 * j + 1)
                                            if r * 4 + j >= 2:
                                                pj = r * 4 + j - 2
                                                t.wait_ge(ac.h, acB_ + 8 * (pj // 4) + 2 * (pj % 4) + 2)
                                            for k in range(4):
                                                mm = t.transpose(
                                                    out=psT[:, (j % 2) * 512 + k * 128:(j % 2) * 512 + (k + 1) * 128],
                                                    in_=vev[:, j % 2, k * 128:(k + 1) * 128],
                                                    identity=id_bf[:])
                                            mm.then_inc(pe.h, 1)
                                pe.bump(40)

                                @block.scalar
                                def _(s, og=og, peB_=peB_, spqB_=spqB_):
                                    for r in range(8):
                                        for j in range(4):
                                            s.wait_ge(pe.h, peB_ + 5 * r + 1)
                                            if r * 4 + j >= 2:
                                                s.wait_ge(pe.h, peB_ + 5 * r + j)
                                            s.copy(out=vev[:, j % 2, :],
                                                   in_=psb[j][:, 0:512]).then_inc(ac.h, 1)
                                            if og > 0 and r == 0 and j == 0:
                                                s.wait_ge(spq[(og - 1) % 2].h, spqB_[(og - 1) % 2])
                                            s.wait_ge(pe.h, peB_ + 5 * r + 1 + (j + 1))
                                            s.copy(out=vtm[:, j, 4 * r:4 * r + 4, :],
                                                   in_=psT[:, (j % 2) * 512:(j % 2) * 512 + 512]
                                                   ).then_inc(ac.h, 1)
                                ac.bump(64)

                                @block.gpsimd
                                def _(g, og=og, acE=ac.v):
                                    g.wait_ge(ac.h, acE)
                                    g.dma_start(out=vsp[og * 4:(og + 1) * 4].rearrange("h p tt d -> p h tt d"),
                                                in_=vtm[:]).then_inc(spq[og % 2].h, 16)
                                spq[og % 2].bump(16)

                    peI, acI, gpI = pe.v, ac.v, gp.v
                    kvqB = [kvq[0].v, kvq[1].v]
                    spq_fin = [spq[0].v, spq[1].v]

                    @block.gpsimd
                    def _(g):
                        g.wait_ge(spq[0].h, spq_fin[0])
                        g.wait_ge(spq[1].h, spq_fin[1])
                        for h in range(16):
                            if h >= 2:
                                g.wait_ge(pe.h, peI + h - 1)
                            g.dma_start(out=kv32[:, h % 2, :],
                                        in_=ksp[128 * h:128 * (h + 1), 1:513]
                                        ).then_inc(kvq[h % 2].h, 16)
                            kvq[h % 2].bump(16)

                    @block.tensor
                    def _(t):
                        for h in range(16):
                            t.wait_ge(kvq[h % 2].h, kvqB[h % 2] + 16 * (h // 2 + 1))
                            t.matmul(out=psb[6][0:1, 0:512],
                                     lhsT=qbars[:, h:h + 1], rhs=kv32[:, h % 2, :],
                                     start=(h == 0), stop=(h == 15)).then_inc(pe.h, 1)
                    pe.bump(16)

                    end_phase({"gpsimd": [(spq[0].h, spq[0].v), (spq[1].h, spq[1].v),
                                          (kvq[0].h, kvq[0].v), (kvq[1].h, kvq[1].v)],
                               "vector": [(ve.h, ve.v)], "tensor": [(pe.h, pe.v)],
                               "scalar": [(ac.h, ac.v)],
                               "sync": [(s_.h, s_.v) for s_ in ldf4] +
                                       [(s_.h, s_.v) for s_ in ldfb] + [(ld.h, ld.v)]})

                with ExitStack() as ph15:
                    impw = mk(ph15, "ab_impw", [128, 560], F32)
                    ext = mk(ph15, "ab_ext", [1, 640], F32)
                    irow = mk(ph15, "ab_irow", [1, 512], F32)
                    iotab = mk(ph15, "ab_iotab", [128, 512], F32)
                    impg_sb = mk(ph15, "ab_impg", [1, 1], F32)
                    w1 = mk(ph15, "ab_w1", [128, 512], F32)
                    w2 = mk(ph15, "ab_w2", [128, 512], F32)
                    w3 = mk(ph15, "ab_w3", [128, 512], F32)
                    iota_r = mk(ph15, "ab_iotar", [1, 512], F32)
                    iota_c = mk(ph15, "ab_iotac", [128, 1], F32)

                    peM, acM, veM, gpM, ccM = pe.v, ac.v, ve.v, gp.v, cc.v

                    @block.sync
                    def _(sync):
                        sync.dma_start(out=iota_r[:], in_=iota512_d).then_inc(ld.h, 16)
                        sync.dma_start(out=iota_c[:], in_=iotac_d).then_inc(ld.h, 16)
                        sync.dma_start(out=impg_sb[:], in_=impg_d).then_inc(ld.h, 16)
                    ld.bump(48)

                    @block.scalar
                    def _(s):
                        s.copy(out=irow[:], in_=psb[6][0:1, 0:512]).then_inc(ac.h, 1)
                    ac.bump(1)

                    @block.vector
                    def _(v):
                        v.wait_ge(ac.h, acM + 1)
                        v.wait_ge(ld.h, ld.v)
                        v.tensor_scalar(out=irow[:], in0=irow[:], scalar1=impg_sb[0:1, 0:1],
                                        scalar2=None, op0=ALU.mult).then_inc(ve.h, 1)
                    ve.bump(1)

                    @block.gpsimd
                    def _(g):
                        g.wait_ge(ve.h, veM + 1)
                        g.dma_start(out=ari_imp, in_=irow[:]).then_inc(gp.h, 16)
                        g.wait_ge(gp.h, gpM + 16)
                        g.collective_compute("AllReduce", ALU.add, replica_groups=WORLD,
                                             ins=[ari_imp.opt()], outs=[aro_imp.opt()]).then_inc(cc.h, 1)
                        g.wait_ge(cc.h, ccM + 1)
                        g.dma_start(out=irow[:], in_=aro_imp).then_inc(gp.h, 16)
                    gp.bump(32); cc.bump(1)

                    @block.tensor
                    def _(t):
                        t.wait_ge(gp.h, gpM + 32)
                        t.matmul(out=psb[0][:, 0:512], lhsT=ones_row[:], rhs=irow[:],
                                 start=True, stop=True).then_inc(pe.h, 1)
                        t.matmul(out=psb[1][:, 0:512], lhsT=ones_row[:], rhs=iota_r[:],
                                 start=True, stop=True).then_inc(pe.h, 1)
                        for g4 in range(4):
                            t.wait_ge(ac.h, (acM + 1) if g4 == 0 else (acM + 3 + g4))
                            t.transpose(out=psb[2][0:128, 0:1],
                                        in_=irow[0:1, 128 * g4:128 * (g4 + 1)],
                                        identity=id_f32[0:1, 0:1]).then_inc(pe.h, 1)
                    pe.bump(6)

                    @block.scalar
                    def _(s):
                        s.wait_ge(pe.h, peM + 1)
                        s.copy(out=impw[:, 0:512], in_=psb[0][:, 0:512]).then_inc(ac.h, 1)
                        s.wait_ge(pe.h, peM + 2)
                        s.copy(out=iotab[:], in_=psb[1][:, 0:512]).then_inc(ac.h, 1)
                        for g4 in range(4):
                            s.wait_ge(pe.h, peM + 3 + g4)
                            s.copy(out=impw[:, 512 + g4:513 + g4],
                                   in_=psb[2][0:128, 0:1]).then_inc(ac.h, 1)
                    ac.bump(6)

                    @block.vector
                    def _(v):
                        k = veM + 1
                        v.wait_ge(ac.h, acM + 7)
                        for g4 in range(4):
                            v.tensor_scalar(out=impw[:, 516 + g4:517 + g4], in0=iota_c[:],
                                            scalar1=float(128 * g4), scalar2=None,
                                            op0=ALU.add).then_inc(ve.h, 1); k += 1
                        for g4 in range(4):
                            v.wait_ge(ve.h, k)
                            v.tensor_scalar(out=w1[:], in0=impw[:, 0:512],
                                            scalar1=impw[:, 512 + g4:513 + g4], scalar2=None,
                                            op0=ALU.is_gt).then_inc(ve.h, 1); k += 1
                            v.wait_ge(ve.h, k)
                            v.tensor_reduce(out=impw[:, 524:525], in_=w1[:],
                                            axis=mybir.AxisListType.X, op=ALU.add).then_inc(ve.h, 1); k += 1
                            v.wait_ge(ve.h, k)
                            v.tensor_scalar(out=w2[:], in0=impw[:, 0:512],
                                            scalar1=impw[:, 512 + g4:513 + g4], scalar2=None,
                                            op0=ALU.is_equal).then_inc(ve.h, 1); k += 1
                            v.wait_ge(ve.h, k)
                            v.tensor_scalar(out=w3[:], in0=iotab[:],
                                            scalar1=impw[:, 516 + g4:517 + g4], scalar2=None,
                                            op0=ALU.is_lt).then_inc(ve.h, 1); k += 1
                            v.wait_ge(ve.h, k)
                            v.tensor_tensor(out=w2[:], in0=w2[:], in1=w3[:],
                                            op=ALU.mult).then_inc(ve.h, 1); k += 1
                            v.wait_ge(ve.h, k)
                            v.tensor_reduce(out=impw[:, 525:526], in_=w2[:],
                                            axis=mybir.AxisListType.X, op=ALU.add).then_inc(ve.h, 1); k += 1
                            v.wait_ge(ve.h, k)
                            v.tensor_tensor(out=impw[:, 526:527], in0=impw[:, 524:525],
                                            in1=impw[:, 525:526], op=ALU.add).then_inc(ve.h, 1); k += 1
                            v.wait_ge(ve.h, k)
                            v.tensor_scalar(out=impw[:, 520 + g4:521 + g4], in0=impw[:, 526:527],
                                            scalar1=255.5, scalar2=-1.0e30,
                                            op0=ALU.is_gt, op1=ALU.mult).then_inc(ve.h, 1); k += 1
                        v.memset(ext[:], 0.0)
                        v.memset(visb[:], -30.0).then_inc(ve.h, 1); k += 1
                    ve.bump(37)

                    peM2, acM2, veM2 = pe.v, ac.v, ve.v

                    @block.tensor
                    def _(t):
                        t.wait_ge(ve.h, veM2)
                        for g4 in range(4):
                            if g4 >= 1:
                                t.wait_ge(ac.h, acM2 + g4)
                            t.transpose(out=psb[3][0:1, 0:128],
                                        in_=impw[:, 520 + g4:521 + g4],
                                        identity=id_f32[:]).then_inc(pe.h, 1)
                        for tt in range(5):
                            t.wait_ge(ac.h, acM2 + 4 + tt)
                            t.transpose(out=psb[3][0:128, 0:1],
                                        in_=ext[0:1, 128 * tt:128 * (tt + 1)],
                                        identity=id_f32[0:1, 0:1]).then_inc(pe.h, 1)
                    pe.bump(9)

                    @block.scalar
                    def _(s):
                        for g4 in range(4):
                            s.wait_ge(pe.h, peM2 + g4 + 1)
                            s.copy(out=ext[0:1, 1 + 128 * g4:1 + 128 * (g4 + 1)],
                                   in_=psb[3][0:1, 0:128]).then_inc(ac.h, 1)
                        for tt in range(5):
                            s.wait_ge(pe.h, peM2 + 4 + tt + 1)
                            s.copy(out=visb[:, tt:tt + 1],
                                   in_=psb[3][0:128, 0:1]).then_inc(ac.h, 1)
                    ac.bump(9)

                    @block.vector
                    def _(v):
                        v.wait_ge(ac.h, acM2 + 9)
                        v.tensor_scalar(out=visb[:, 0:5], in0=visb[:, 0:5],
                                        scalar1=-30.0, scalar2=None, op0=ALU.add).then_inc(ve.h, 1)
                    ve.bump(1)

                    end_phase({"vector": [(ve.h, ve.v)], "tensor": [(pe.h, pe.v)],
                               "scalar": [(ac.h, ac.v)], "gpsimd": [(gp.h, gp.v)],
                               "sync": [(ld.h, ld.v)]})

                with ExitStack() as ph2:
                    cbias_sb = mk(ph2, "ab_cb", [128, 4, 512], F32)
                    qh = mk(ph2, "ab_qh", [128, 2, T], BF16)
                    kh = mk(ph2, "ab_kh", [128, 2, T], BF16)
                    vh = mk(ph2, "ab_vh", [128, 2, 32, 128], BF16)
                    Eb = mk(ph2, "ab_E", [128, 2, 512], BF16)
                    s2 = mk(ph2, "ab_s2", [1, 512], F32)
                    s2r = mk(ph2, "ab_s2r", [1, 512], F32)
                    ibc = mk(ph2, "ab_ibc", [128, 512], F32)
                    ob = mk(ph2, "ab_ob", [128, 2, T], BF16)

                    @block.sync
                    def _(sync):
                        sync.dma_start(out=cbias_sb[:], in_=cbias_d.rearrange("d p j -> p d j")
                                       ).then_inc(ld.h, 16)
                    ld.bump(16)

                    head_tots = []
                    for h in range(16):
                        peH, acH, veH = pe.v, ac.v, ve.v
                        ldqB = [ldq[0].v, ldq[1].v]
                        ldxB = [ldx[0].v, ldx[1].v]
                        ldaB = [lda[0].v, lda[1].v]

                        @block.sync
                        def _(sync, h=h, hts=list(head_tots)):
                            if h >= 2:
                                sync.wait_ge(pe.h, hts[h - 2])
                            if h == 0:
                                sync.wait_ge(ld.h, ld.v)
                            sync.dma_start(out=qh[:, h % 2, :],
                                           in_=q3d[:, h, :]).then_inc(ldq[h % 2].h, 16)
                            ldq[h % 2].bump(16)
                            sync.dma_start(out=kh[:, h % 2, :],
                                           in_=k3d[:, h, :]).then_inc(ldx[h % 2].h, 16)
                            ldx[h % 2].bump(16)
                            sync.dma_start(out=vh[:, h % 2, :, :],
                                           in_=vsp[h].rearrange("p tt d -> p tt d")).then_inc(lda[h % 2].h, 16)
                            lda[h % 2].bump(16)

                        ntks_b = [4 * (t8 + 1) for t8 in range(8)]
                        plan = []
                        for t8 in range(8):
                            ntk = ntks_b[t8]
                            seq = [("S", 0), ("S", 1)]
                            for k in range(2, ntk):
                                seq.append(("PVS", k - 2))
                                seq.append(("S", k))
                            seq.append(("PVS", ntk - 2))
                            seq.append(("PVS", ntk - 1))
                            seq.append(("BC", 0))
                            plan.append(seq)
                        pe_pos = {}
                        pc = pe.v
                        for t8, seq in enumerate(plan):
                            for ent in seq:
                                pc += 1
                                pe_pos[(t8,) + ent] = pc
                        pe_tot = pc
                        ac_pos = {}
                        a = ac.v
                        for t8 in range(8):
                            for tk in range(ntks_b[t8]):
                                a += 1
                                ac_pos[(t8, "e", tk)] = a
                            a += 1
                            ac_pos[(t8, "sum")] = a
                            a += 1
                            ac_pos[(t8, "ibc")] = a
                        ac_tot = a
                        ve_pos = {}
                        w = ve.v
                        for t8 in range(8):
                            for d4 in range(4):
                                w += 1
                                ve_pos[(t8, "cb", d4)] = w
                            w += 1
                            ve_pos[(t8, "rec")] = w
                            w += 1
                            ve_pos[(t8, "oev")] = w
                        ve_tot = w

                        @block.tensor
                        def _(t, h=h, peH=peH, acH=acH, veH=veH, ldqB=ldqB, ldxB=ldxB,
                              ldaB=ldaB, pe_pos=pe_pos, ac_pos=ac_pos, ve_pos=ve_pos,
                              ntks_b=ntks_b, plan=plan):
                            t.wait_ge(ldq[h % 2].h, ldqB[h % 2] + 16)
                            t.wait_ge(ldx[h % 2].h, ldxB[h % 2] + 16)
                            t.wait_ge(lda[h % 2].h, ldaB[h % 2] + 16)
                            if h == 0:
                                t.wait_ge(ve.h, veH)
                            for t8 in range(8):
                                ntk = ntks_b[t8]
                                for ent in plan[t8]:
                                    kind, idx = ent
                                    if kind == "S":
                                        tk = idx
                                        sb = psb[2] if tk % 2 == 0 else psb[3]
                                        if tk <= 1:
                                            if t8 > 0:
                                                t.wait_ge(ac.h, ac_pos[(t8 - 1, "ibc")])
                                            elif h > 0:
                                                t.wait_ge(ac.h, acH)
                                        t.matmul(out=sb[:, 0:512],
                                                 lhsT=kh[:, h % 2, tk * 128:(tk + 1) * 128],
                                                 rhs=qh[:, h % 2, t8 * 512:(t8 + 1) * 512],
                                                 start=True, stop=True).then_inc(pe.h, 1)
                                    elif kind == "PVS":
                                        tk = idx
                                        t.wait_ge(ac.h, ac_pos[(t8, "e", tk)])
                                        if tk == 0:
                                            if t8 > 0:
                                                t.wait_ge(ve.h, ve_pos[(t8 - 1, "oev")])
                                            elif h > 0:
                                                t.wait_ge(ve.h, veH)
                                        t.matmul(out=psb[0][:, 0:512],
                                                 lhsT=vh[:, h % 2, tk, :],
                                                 rhs=Eb[:, tk % 2, :],
                                                 start=(tk == 0), stop=(tk == ntk - 1))
                                        mm = t.matmul(out=psb[1][0:1, 0:512],
                                                      lhsT=ones_bf[:, 0:1],
                                                      rhs=Eb[:, tk % 2, :],
                                                      start=(tk == 0), stop=(tk == ntk - 1))
                                        mm.then_inc(pe.h, 1)
                                    else:
                                        t.wait_ge(ve.h, ve_pos[(t8, "rec")])
                                        t.matmul(out=psb[2][:, 0:512], lhsT=ones_row[:],
                                                 rhs=s2r[:], start=True, stop=True).then_inc(pe.h, 1)

                        @block.scalar
                        def _(s, h=h, acH=acH, pe_pos=pe_pos, ac_pos=ac_pos,
                              ve_pos=ve_pos, ntks_b=ntks_b):
                            for t8 in range(8):
                                ntk = ntks_b[t8]
                                for tk in range(ntk):
                                    s.wait_ge(pe.h, pe_pos[(t8, "S", tk)])
                                    if tk >= ntk - 4:
                                        s.wait_ge(ve.h, ve_pos[(t8, "cb", tk - (ntk - 4))])
                                    if tk >= 2:
                                        s.wait_ge(pe.h, pe_pos[(t8, "PVS", tk - 2)])
                                    sb = psb[2] if tk % 2 == 0 else psb[3]
                                    vcolix = tk if tk < 5 else 32
                                    s.activation(out=Eb[:, tk % 2, :], in_=sb[:, 0:512],
                                                 func=AF.Exp, scale=scD[:, 0:1],
                                                 bias=visb[:, vcolix:vcolix + 1]).then_inc(ac.h, 1)
                                s.wait_ge(pe.h, pe_pos[(t8, "PVS", ntk - 1)])
                                s.copy(out=s2[:], in_=psb[1][0:1, 0:512]).then_inc(ac.h, 1)
                                s.wait_ge(pe.h, pe_pos[(t8, "BC", 0)])
                                s.copy(out=ibc[:], in_=psb[2][:, 0:512]).then_inc(ac.h, 1)

                        spqB2 = [spq[0].v, spq[1].v]

                        @block.vector
                        def _(v, h=h, pe_pos=pe_pos, ac_pos=ac_pos, ve_pos=ve_pos,
                              ntks_b=ntks_b, spqB2=spqB2):
                            if h >= 2:
                                v.wait_ge(spq[h % 2].h, spqB2[h % 2])
                            for t8 in range(8):
                                ntk = ntks_b[t8]
                                for d4 in range(4):
                                    tk = ntk - 4 + d4
                                    sb = psb[2] if tk % 2 == 0 else psb[3]
                                    v.wait_ge(pe.h, pe_pos[(t8, "S", tk)])
                                    v.tensor_tensor(out=sb[:, 0:512], in0=sb[:, 0:512],
                                                    in1=cbias_sb[:, d4, :],
                                                    op=ALU.add).then_inc(ve.h, 1)
                                v.wait_ge(ac.h, ac_pos[(t8, "sum")])
                                v.reciprocal(out=s2r[:], in_=s2[:]).then_inc(ve.h, 1)
                                v.wait_ge(ac.h, ac_pos[(t8, "ibc")])
                                v.tensor_tensor(out=ob[:, h % 2, t8 * 512:(t8 + 1) * 512],
                                                in0=psb[0][:, 0:512], in1=ibc[:],
                                                op=ALU.mult).then_inc(ve.h, 1)
                        pe.v = pe_tot
                        ac.v = ac_tot
                        ve.v = ve_tot
                        head_tots.append(pe_tot)

                        @block.gpsimd
                        def _(g, h=h, veE=ve.v):
                            g.wait_ge(ve.h, veE)
                            g.dma_start(out=ob3d[:, h, :], in_=ob[:, h % 2, :]).then_inc(spq[h % 2].h, 16)
                            if h % 4 == 3:
                                qq = h // 4
                                g.wait_ge(spq[0].h, spq[0].v + (16 if h % 2 == 0 else 0))
                                g.wait_ge(spq[1].h, spq[1].v + (16 if h % 2 == 1 else 0))
                                g.collective_compute(
                                    "AllGather", ALU.bypass, replica_groups=PAIRS,
                                    ins=[obsp[512 * qq:512 * (qq + 1), :].opt()],
                                    outs=[agB[1024 * qq:1024 * (qq + 1), :].opt()]).then_inc(cc.h, 1)
                        spq[h % 2].bump(16)
                        if h % 4 == 3:
                            cc.bump(1)

                    end_phase({"gpsimd": [(spq[0].h, spq[0].v), (spq[1].h, spq[1].v)],
                               "vector": [(ve.h, ve.v)], "tensor": [(pe.h, pe.v)],
                               "scalar": [(ac.h, ac.v)],
                               "sync": [(ldq[0].h, ldq[0].v), (ldq[1].h, ldq[1].v),
                                        (ldx[0].h, ldx[0].v), (ldx[1].h, ldx[1].v),
                                        (lda[0].h, lda[0].v), (lda[1].h, lda[1].v)]})

        emit_ln_statspass(x_in)
        emit_ln_finish(x_in)
        emit_attn_a(x_in)
        if BUILD_SUBLAYERS >= 2:
            emit_ln_finish(resid, with_mean=True, do_ag=True)
            emit_ffn(cc.v - 4, [True], agA, worgT, resid)
        if BUILD_SUBLAYERS >= 3:
            emit_ln_finish(resid, with_mean=False, do_ag=True)
            emit_attn_b(cc.v - 4)
            emit_ffn(cc.v - 4, [True], agB, owT_d, resid,
                     with_lora=False, do_stats=True)
        if BUILD_SUBLAYERS >= 4:
            emit_ln_finish(resid, with_mean=True, do_ag=True)
            emit_ffn(cc.v - 4, [True], agA, worgT, out,
                     with_lora=True, do_stats=False)

        @block.sync
        def _(sync):
            if BUILD_SUBLAYERS < 4:
                sync.dma_start(out=out, in_=resid).then_inc(st.h, 16)
                st.bump(16)
            sync.dma_start(out=dbg, in_=xn_dram).then_inc(st.h, 16)
            st.bump(16)
            sync.wait_ge(st.h, st.v)

    return nc


def kernel(L0_attn_a_x, L0_attn_a_bv_q, L0_attn_a_bv_k, L0_attn_a_bv_v,
           L0_attn_b_q_w, L0_attn_b_k_w, L0_attn_b_v_w, L0_attn_b_o_w,
           L0_ffn_W_org, L0_ffn_lora_down, L0_ffn_lora_up, L0_ffn_lora_keys,
           L0_ffn_Wq):
    x = np.asarray(L0_attn_a_x)
    bq = np.asarray(L0_attn_a_bv_q); bk = np.asarray(L0_attn_a_bv_k)
    bv = np.asarray(L0_attn_a_bv_v)
    W_org = np.asarray(L0_ffn_W_org)
    lora_down = np.asarray(L0_ffn_lora_down); lora_up = np.asarray(L0_ffn_lora_up)
    lora_keys = np.asarray(L0_ffn_lora_keys); Wq = np.asarray(L0_ffn_Wq)
    worgT_full = np.ascontiguousarray(W_org.T)
    down_cat = np.ascontiguousarray(lora_down.transpose(1, 0, 2).reshape(DM, 512))
    up_cat = np.ascontiguousarray(lora_up.reshape(512, DM))
    wqT_full = np.ascontiguousarray(Wq.T)
    keysT = np.ascontiguousarray(lora_keys.T)
    sel = np.zeros((4, 8, 128), np.float32)
    for g4 in range(4):
        sel[g4, 2 * g4, 0:64] = 1.0
        sel[g4, 2 * g4 + 1, 64:128] = 1.0
    q_w = np.asarray(L0_attn_b_q_w); k_w = np.asarray(L0_attn_b_k_w)
    v_w = np.asarray(L0_attn_b_v_w); o_w = np.asarray(L0_attn_b_o_w)
    qwT = np.ascontiguousarray(q_w.T); kwT = np.ascontiguousarray(k_w.T)
    vwT = np.ascontiguousarray(v_w.T); owT = np.ascontiguousarray(o_w.T)
    dd = np.arange(128)
    invf = (10000.0 ** (-(dd % 64).astype(np.float64) / 64.0))
    tt_ = np.arange(4096, dtype=np.float64)
    ang = invf[:, None] * tt_[None, :]
    ropec_h = np.cos(ang).astype(ml_dtypes.bfloat16)
    sgn_ = np.where(dd < 64, -1.0, 1.0)[:, None]
    ropes_h = (sgn_ * np.sin(ang)).astype(ml_dtypes.bfloat16)
    ppj, jjj = np.meshgrid(np.arange(128), np.arange(512), indexing="ij")
    cbias_h = np.stack([np.where(ppj + 128 * d <= jjj, 0.0, -1.0e30)
                        for d in range(4)]).astype(np.float32)
    iota512 = np.arange(512, dtype=np.float32)[None, :]
    iotac = np.arange(128, dtype=np.float32)[:, None]
    ident_bf = np.eye(128, dtype=ml_dtypes.bfloat16)
    ident_f = np.eye(128, dtype=np.float32)
    pp, jj = np.meshgrid(np.arange(128), np.arange(256), indexing="ij")
    amask = np.stack([(pp <= jj), (pp + 128 <= jj)]).astype(ml_dtypes.bfloat16)

    nc = build_nc()
    in_maps = []
    for c in range(8):
        b, z = c // 2, c % 2
        xi = np.ascontiguousarray(x[b].T[z * HALF:(z + 1) * HALF, :]).astype(np.float32)
        m = {
            "x_in": xi, "ident": ident_bf, "identf": ident_f, "amask": amask,
            "bvq": np.ascontiguousarray(bq[2 * z:2 * z + 2]).astype(np.float32),
            "bvk": np.ascontiguousarray(bk[2 * z:2 * z + 2]).astype(np.float32),
            "bvv": np.ascontiguousarray(bv[2 * z:2 * z + 2]).astype(np.float32),
            "worgT": np.ascontiguousarray(worgT_full[:, z * HALF:(z + 1) * HALF]).astype(np.float32),
            "down_d": down_cat.astype(np.float32),
            "up_d": np.ascontiguousarray(up_cat[:, z * HALF:(z + 1) * HALF]).astype(np.float32),
            "wqT_d": np.ascontiguousarray(wqT_full[z * HALF:(z + 1) * HALF, :]).astype(np.float32),
            "keysT_d": keysT.astype(np.float32),
            "sel_d": sel,
            "qwT_d": np.ascontiguousarray(qwT[:, z * HALF:(z + 1) * HALF]).astype(np.float32),
            "kwT_d": np.ascontiguousarray(kwT[:, z * HALF:(z + 1) * HALF]).astype(np.float32),
            "vwT_d": np.ascontiguousarray(vwT[:, z * HALF:(z + 1) * HALF]).astype(np.float32),
            "owT_d": np.ascontiguousarray(owT[:, z * HALF:(z + 1) * HALF]).astype(np.float32),
            "ropec_d": ropec_h, "ropes_d": ropes_h, "cbias_d": cbias_h,
            "iota512_d": iota512, "iotac_d": iotac,
            "impg_d": np.array([[1.0 if c < 2 else 0.0]], np.float32),
        }
        in_maps.append(m)
    res = run_bass_kernel_spmd(nc, in_maps, core_ids=list(range(8)))
    outs = []
    for b in range(4):
        fm = np.concatenate([res.results[2 * b]["out"], res.results[2 * b + 1]["out"]], axis=0)
        outs.append(fm.T)
    kernel._dbg = [r["dbg"] for r in res.results]
    return np.stack(outs, axis=0).astype(np.float32)


# version 03 (final)
# speedup vs baseline: 1.0322x; 1.0322x over previous
import math
import numpy as np
import ml_dtypes
import concourse.bass as bass
import concourse.mybir as mybir
from concourse.bass_utils import run_bass_kernel_spmd
from concourse.alu_op_type import AluOpType as ALU
from contextlib import ExitStack

AF = mybir.ActivationFunctionType
F32 = mybir.dt.float32
BF16 = mybir.dt.bfloat16

DM = 4096
T = 4096
HALF = 2048
NFC = HALF // 128
NTT = T // 128
NTQ = 16
PAIRS = [[0, 1], [2, 3], [4, 5], [6, 7]]
WORLD = [[0, 1, 2, 3, 4, 5, 6, 7]]
EPS = 1e-5

BUILD_SUBLAYERS = 4


class Ctr:
    def __init__(self, handle):
        self.h = handle
        self.v = 0

    def bump(self, n=1):
        self.v += n
        return self.v


def _rows(ap, c0):
    return ap[c0 * 128:(c0 + 1) * 128, :]


def build_nc():
    nc = bass.Bass("TRN2", target_bir_lowering=False, debug=False)

    x_in = nc.dram_tensor("x_in", [HALF, T], F32, kind="ExternalInput").ap()
    ident = nc.dram_tensor("ident", [128, 128], BF16, kind="ExternalInput").ap()
    identf = nc.dram_tensor("identf", [128, 128], F32, kind="ExternalInput").ap()
    bvq = nc.dram_tensor("bvq", [2, 1024], F32, kind="ExternalInput").ap()
    bvk = nc.dram_tensor("bvk", [2, 1024], F32, kind="ExternalInput").ap()
    bvv = nc.dram_tensor("bvv", [2, 1024], F32, kind="ExternalInput").ap()
    amask_d = nc.dram_tensor("amask", [2, 128, 256], BF16, kind="ExternalInput").ap()
    worgT = nc.dram_tensor("worgT", [DM, HALF], F32, kind="ExternalInput").ap()
    down_d = nc.dram_tensor("down_d", [DM, 512], F32, kind="ExternalInput").ap()
    up_d = nc.dram_tensor("up_d", [512, HALF], F32, kind="ExternalInput").ap()
    wqT_d = nc.dram_tensor("wqT_d", [HALF, 128], F32, kind="ExternalInput").ap()
    keysT_d = nc.dram_tensor("keysT_d", [128, 8], F32, kind="ExternalInput").ap()
    sel_d = nc.dram_tensor("sel_d", [4, 8, 128], F32, kind="ExternalInput").ap()
    qwT_d = nc.dram_tensor("qwT_d", [DM, HALF], F32, kind="ExternalInput").ap()
    kwT_d = nc.dram_tensor("kwT_d", [DM, HALF], F32, kind="ExternalInput").ap()
    vwT_d = nc.dram_tensor("vwT_d", [DM, HALF], F32, kind="ExternalInput").ap()
    owT_d = nc.dram_tensor("owT_d", [DM, HALF], F32, kind="ExternalInput").ap()
    ropec_d = nc.dram_tensor("ropec_d", [128, T], BF16, kind="ExternalInput").ap()
    ropes_d = nc.dram_tensor("ropes_d", [128, T], BF16, kind="ExternalInput").ap()
    cbias_d = nc.dram_tensor("cbias_d", [4, 128, 512], F32, kind="ExternalInput").ap()
    iota512_d = nc.dram_tensor("iota512_d", [1, 512], F32, kind="ExternalInput").ap()
    iotac_d = nc.dram_tensor("iotac_d", [128, 1], F32, kind="ExternalInput").ap()
    impg_d = nc.dram_tensor("impg_d", [1, 1], F32, kind="ExternalInput").ap()
    out = nc.dram_tensor("out", [HALF, T], F32, kind="ExternalOutput").ap()
    dbg = nc.dram_tensor("dbg", [HALF, T], BF16, kind="ExternalOutput").ap()

    resid = nc.dram_tensor("resid", [HALF, T], F32).ap()
    xn_dram = nc.dram_tensor("xn_dram", [HALF, T], BF16).ap()
    arbuf_i = nc.dram_tensor("arbuf_i", [128, 64], F32).ap()
    arbuf_o = nc.dram_tensor("arbuf_o", [128, 64], F32).ap()
    rowbounce = nc.dram_tensor("rowbounce", [64, 128], F32).ap()
    bind_dram = nc.dram_tensor("bind_dram", [2, 2048], F32).ap()
    agA = nc.dram_tensor("agA", [DM, T], BF16).ap()
    arq_i = nc.dram_tensor("arq_i", [128, 1], F32).ap()
    arq_o = nc.dram_tensor("arq_o", [128, 1], F32).ap()
    agB = nc.dram_tensor("agB", [DM, T], BF16).ap()
    qsp = nc.dram_tensor("qsp", [HALF, T], BF16).ap()
    ksp = nc.dram_tensor("ksp", [HALF, T], BF16).ap()
    vsp = nc.dram_tensor("vsp", [16, 128, 32, 128], BF16).ap()
    obsp = nc.dram_tensor("obsp", [HALF, T], BF16).ap()
    ari_imp = nc.dram_tensor("ari_imp", [1, 512], F32).ap()
    aro_imp = nc.dram_tensor("aro_imp", [1, 512], F32).ap()

    with ExitStack() as outer:
        def SEM(name):
            return Ctr(outer.enter_context(nc.semaphore(name)))

        def SBO(name, shape, dt):
            return outer.enter_context(nc.sbuf_tensor(name, shape, dt))

        id_bf = SBO("id_bf", [128, 128], BF16)
        id_f32 = SBO("id_f32", [128, 128], F32)
        ones_bf = SBO("ones_bf", [128, 2], BF16)
        ones_row = SBO("ones_row", [1, 128], F32)
        eps_col = SBO("eps_col", [128, 1], F32)
        sc125 = SBO("sc125", [128, 1], F32)
        stat_s = SBO("stat_s", [128, 64], F32)
        stat_w = SBO("stat_w", [128, 224], F32)
        qkcol = SBO("qkcol", [128, 16], F32)
        mcol = SBO("mcol", [128, 16], F32)
        cvec = SBO("cvec", [128, 4], F32)
        scC = SBO("scC", [1, 2], F32)
        vcol = SBO("vcol", [128, 16], F32)
        amask_sb = SBO("amask_sb", [128, 2, 256], BF16)
        bind_stack = ExitStack()
        bindw = bind_stack.enter_context(nc.sbuf_tensor("bindw", [2, 8192], F32))

        psb = [outer.enter_context(nc.psum_tensor(f"psb{i}", [128, 512], F32))
               for i in range(7)]
        psT = outer.enter_context(nc.psum_tensor("psT", [128, 1024], BF16))
        ps_stat = psb[6]

        pe = SEM("s_pe"); ve = SEM("s_ve"); ac = SEM("s_ac"); gp = SEM("s_gp")
        cc = SEM("s_cc"); st = SEM("s_st"); ld = SEM("s_ld"); bar = SEM("s_bar")
        lda = [SEM("s_lda0"), SEM("s_lda1")]
        ldq = [SEM("s_ldq0"), SEM("s_ldq1")]
        ldx = [SEM("s_ldx0"), SEM("s_ldx1")]
        stn = [SEM("s_stn0"), SEM("s_stn1")]
        stx = [SEM("s_stx0"), SEM("s_stx1")]

        _uid = [0]

        def mk(ph, nm, shape, dt):
            _uid[0] += 1
            return ph.enter_context(nc.sbuf_tensor(f"{nm}_{_uid[0]}", shape, dt))

        block = outer.enter_context(nc.Block())
        ENG = {"sync": block.sync, "tensor": block.tensor, "vector": block.vector,
               "scalar": block.scalar, "gpsimd": block.gpsimd}

        def end_phase(waits_by_engine):
            for name, wlist in waits_by_engine.items():
                dec = ENG[name]

                @dec
                def _(e, wlist=wlist):
                    for (h, v) in wlist:
                        e.wait_ge(h, v)
                    e.sem_inc(bar.h, 1)
                bar.bump(1)
            for name, dec in ENG.items():
                @dec
                def _(e):
                    e.wait_ge(bar.h, bar.v)

        @block.sync
        def _(sync):
            sync.dma_start(out=id_bf[:], in_=ident).then_inc(ld.h, 16)
            sync.dma_start(out=id_f32[:], in_=identf).then_inc(ld.h, 16)
            sync.dma_start(out=bindw[:, 0:1024], in_=bvq).then_inc(ld.h, 16)
            sync.dma_start(out=bindw[:, 1024:2048], in_=bvk).then_inc(ld.h, 16)
            sync.dma_start(out=bindw[:, 2048:3072], in_=bvv).then_inc(ld.h, 16)
            sync.dma_start(out=amask_sb[:], in_=amask_d.rearrange("d p j -> p d j")).then_inc(ld.h, 16)
        ld.bump(96)

        @block.vector
        def _(v):
            v.wait_ge(ld.h, ld.v)
            v.memset(ones_bf[:], 1.0)
            v.memset(ones_row[:], 1.0)
            v.memset(eps_col[:], EPS)
            v.memset(sc125[:], 0.125)
            v.memset(scC[0:1, 0:1], 1.0 / (4096.0 * math.sqrt(128.0)))
            v.memset(scC[0:1, 1:2], -1.0 / (4096.0 * math.sqrt(128.0))).then_inc(ve.h, 1)
            for i, base in enumerate((0, 1024, 2048)):
                v.tensor_reduce(out=bindw[:, 3072 + i:3073 + i], in_=bindw[:, base:base + 1024],
                                axis=mybir.AxisListType.X, op=ALU.add,
                                apply_absolute_value=True).then_inc(ve.h, 1)
        ve.bump(4)

        @block.scalar
        def _(s):
            s.wait_ge(ve.h, ve.v)
            s.sign(out=bindw[:, 4096:5120], in_=bindw[:, 0:1024])
            s.sign(out=bindw[:, 5120:6144], in_=bindw[:, 1024:2048])
            s.sign(out=bindw[:, 6144:7168], in_=bindw[:, 2048:3072]).then_inc(ac.h, 1)
        ac.bump(1)

        @block.vector
        def _(v):
            veB = ve.v
            v.wait_ge(ac.h, ac.v)
            v.tensor_scalar(out=bindw[:, 3072:3075], in0=bindw[:, 3072:3075],
                            scalar1=1.0 / 1024.0, scalar2=None, op0=ALU.mult).then_inc(ve.h, 1)
            k = 1
            for i in range(3):
                sgn = bindw[:, 4096 + 1024 * i:4096 + 1024 * (i + 1)]
                al = bindw[:, 3072 + i:3073 + i]
                v.wait_ge(ve.h, veB + k)
                v.tensor_scalar(out=sgn, in0=sgn, scalar1=al, scalar2=None,
                                op0=ALU.mult).then_inc(ve.h, 1); k += 1
                v.wait_ge(ve.h, veB + k)
                v.tensor_scalar(out=bindw[:, 7168:8192], in0=sgn, scalar1=0.0, scalar2=al,
                                op0=ALU.is_equal, op1=ALU.mult).then_inc(ve.h, 1); k += 1
                v.wait_ge(ve.h, veB + k)
                v.tensor_tensor(out=sgn, in0=sgn, in1=bindw[:, 7168:8192],
                                op=ALU.add).then_inc(ve.h, 1); k += 1
            v.wait_ge(ve.h, veB + k)
            v.tensor_tensor(out=bindw[:, 0:1024], in0=bindw[:, 4096:5120],
                            in1=bindw[:, 5120:6144], op=ALU.mult).then_inc(ve.h, 1)
        ve.bump(11)

        @block.tensor
        def _(t):
            peB, acB = pe.v, ac.v
            t.wait_ge(ve.h, ve.v)
            for dc in range(8):
                if dc >= 2:
                    t.wait_ge(ac.h, acB + (dc - 1))
                t.transpose(out=psb[dc % 2][0:128, 0:2],
                            in_=bindw[0:2, dc * 128:(dc + 1) * 128],
                            identity=id_f32[0:2, 0:2]).then_inc(pe.h, 1)
            for dc in range(8):
                if dc >= 2:
                    t.wait_ge(ac.h, acB + 8 + (dc - 1))
                t.transpose(out=psb[2 + dc % 2][0:128, 0:2],
                            in_=bindw[0:2, 6144 + dc * 128:6144 + (dc + 1) * 128],
                            identity=id_f32[0:2, 0:2]).then_inc(pe.h, 1)
        pe.bump(16)

        @block.scalar
        def _(s):
            peB, acB = pe.v - 16, ac.v
            for dc in range(8):
                s.wait_ge(pe.h, peB + dc + 1)
                s.copy(out=qkcol[:, 2 * dc:2 * dc + 2],
                       in_=psb[dc % 2][0:128, 0:2]).then_inc(ac.h, 1)
            for dc in range(8):
                s.wait_ge(pe.h, peB + 8 + dc + 1)
                s.copy(out=vcol[:, 2 * dc:2 * dc + 2],
                       in_=psb[2 + dc % 2][0:128, 0:2]).then_inc(ac.h, 1)
        ac.bump(16)

        end_phase({"scalar": [(ac.h, ac.v)], "vector": [(ve.h, ve.v)],
                   "tensor": [(pe.h, pe.v)], "sync": [(ld.h, ld.v)]})
        bind_stack.close()

        def emit_ln_statspass(src_dram):
            peB, veB = pe.v, ve.v
            ldaB = [lda[0].v, lda[1].v]
            with ExitStack() as ph:
                xch = mk(ph, "lns_xch", [128, 2, T], F32)
                xsq = mk(ph, "lns_xsq", [128, T], BF16)
                xbf = mk(ph, "lns_xbf", [128, T], BF16)

                @block.sync
                def _(sync):
                    for c in range(NFC):
                        if c >= 2:
                            sync.wait_ge(ve.h, veB + 2 * (c - 2) + 2)
                        sync.dma_start(out=xch[:, c % 2, :],
                                       in_=_rows(src_dram, c)).then_inc(lda[c % 2].h, 16)
                        lda[c % 2].bump(16)

                @block.vector
                def _(v):
                    for c in range(NFC):
                        v.wait_ge(lda[c % 2].h, ldaB[c % 2] + 16 * (c // 2 + 1))
                        if c >= 1:
                            v.wait_ge(pe.h, peB + c)
                        v.tensor_tensor(out=xsq[:], in0=xch[:, c % 2, :],
                                        in1=xch[:, c % 2, :], op=ALU.mult)
                        v.tensor_copy(out=xbf[:], in_=xch[:, c % 2, :]).then_inc(ve.h, 2)
                ve.bump(2 * NFC)

                @block.tensor
                def _(t):
                    for c in range(NFC):
                        t.wait_ge(ve.h, veB + 2 * c + 2)
                        for i in range(NTT):
                            first = (c == 0 and i == 0)
                            t.matmul(out=ps_stat[:, i:i + 1],
                                     lhsT=xbf[:, i * 128:(i + 1) * 128], rhs=ones_bf[:, 0:1],
                                     start=first, stop=False)
                            mm = t.matmul(out=ps_stat[:, 32 + i:33 + i],
                                          lhsT=xsq[:, i * 128:(i + 1) * 128], rhs=ones_bf[:, 0:1],
                                          start=False, stop=(c == NFC - 1 and i == NTT - 1))
                        mm.then_inc(pe.h, 1)
                pe.bump(NFC)
                end_phase({"tensor": [(pe.h, pe.v)], "vector": [(ve.h, ve.v)],
                           "sync": [(lda[0].h, lda[0].v), (lda[1].h, lda[1].v)]})

        def emit_ln_finish(src_dram, with_mean=False, do_ag=False):
            peB, veB, acB, gpB, ccB = pe.v, ve.v, ac.v, gp.v, cc.v
            with ExitStack() as ph:
                ab_row = mk(ph, "lnf_ab", [128, T], F32)
                rbT = mk(ph, "lnf_rbT", [64, 128], F32)
                arow_a = mk(ph, "lnf_ara", [1, T], F32)
                arow_b = mk(ph, "lnf_arb", [1, T], F32)
                bb_row = mk(ph, "lnf_bb", [128, T], F32)
                xch = mk(ph, "lnf_xch", [128, 2, T], F32)
                xt0 = mk(ph, "lnf_xt0", [128, T], F32)
                xnc = mk(ph, "lnf_xnc", [128, 2, T], BF16)

                @block.scalar
                def _(s):
                    s.copy(out=stat_s[:, 0:64], in_=ps_stat[:, 0:64]).then_inc(ac.h, 1)
                ac.bump(1)

                @block.gpsimd
                def _(g):
                    g.wait_ge(ac.h, acB + 1)
                    g.dma_start(out=arbuf_i, in_=stat_s[:, 0:64]).then_inc(gp.h, 16)
                    g.wait_ge(gp.h, gpB + 16)
                    g.collective_compute("AllReduce", ALU.add, replica_groups=PAIRS,
                                         ins=[arbuf_i.opt()], outs=[arbuf_o.opt()]).then_inc(cc.h, 1)
                    g.wait_ge(cc.h, ccB + 1)
                    g.dma_start(out=stat_s[:, 0:64], in_=arbuf_o).then_inc(gp.h, 16)
                gp.bump(32); cc.bump(1)

                @block.vector
                def _(v):
                    v.wait_ge(gp.h, gpB + 32)
                    v.tensor_scalar(out=stat_w[:, 0:32], in0=stat_s[:, 0:32],
                                    scalar1=1.0 / DM, scalar2=None, op0=ALU.mult)
                    v.tensor_scalar(out=stat_w[:, 32:64], in0=stat_s[:, 32:64],
                                    scalar1=1.0 / DM, scalar2=None, op0=ALU.mult).then_inc(ve.h, 1)
                    v.wait_ge(ve.h, veB + 1)
                    v.tensor_tensor(out=stat_w[:, 64:96], in0=stat_w[:, 0:32],
                                    in1=stat_w[:, 0:32], op=ALU.mult).then_inc(ve.h, 1)
                    v.wait_ge(ve.h, veB + 2)
                    v.tensor_tensor(out=stat_w[:, 96:128], in0=stat_w[:, 32:64],
                                    in1=stat_w[:, 64:96], op=ALU.subtract).then_inc(ve.h, 1)
                ve.bump(3)

                @block.scalar
                def _(s):
                    s.wait_ge(ve.h, veB + 3)
                    s.activation(out=stat_w[:, 128:160], in_=stat_w[:, 96:128],
                                 func=AF.Sqrt, bias=eps_col[:, 0:1], scale=1.0).then_inc(ac.h, 1)
                ac.bump(1)

                @block.vector
                def _(v):
                    v.wait_ge(ac.h, acB + 2)
                    v.reciprocal(out=stat_w[:, 160:192], in_=stat_w[:, 128:160]).then_inc(ve.h, 1)
                    v.wait_ge(ve.h, veB + 4)
                    v.scalar_tensor_tensor(out=stat_w[:, 192:224], in0=stat_w[:, 0:32],
                                           scalar=-1.0, in1=stat_w[:, 160:192],
                                           op0=ALU.mult, op1=ALU.mult).then_inc(ve.h, 1)
                ve.bump(2)

                @block.tensor
                def _(t):
                    t.wait_ge(ve.h, veB + 5)
                    t.transpose(out=psb[4][0:64, 0:128], in_=stat_w[:, 160:224],
                                identity=id_f32[:]).then_inc(pe.h, 1)
                pe.bump(1)

                @block.scalar
                def _(s):
                    s.wait_ge(pe.h, peB + 1)
                    s.copy(out=rbT[:], in_=psb[4][0:64, 0:128]).then_inc(ac.h, 1)
                ac.bump(1)

                @block.gpsimd
                def _(g):
                    g.wait_ge(ac.h, acB + 3)
                    g.dma_start(out=rowbounce, in_=rbT[:]).then_inc(gp.h, 16)
                    g.wait_ge(gp.h, gpB + 48)
                    g.dma_start(out=arow_a[:],
                                in_=rowbounce[0:32, :].rearrange("i p -> (i p)")[None, :]).then_inc(gp.h, 16)
                    g.dma_start(out=arow_b[:],
                                in_=rowbounce[32:64, :].rearrange("i p -> (i p)")[None, :]).then_inc(gp.h, 16)
                gp.bump(48)

                @block.tensor
                def _(t):
                    t.wait_ge(gp.h, gpB + 80)
                    for j in range(8):
                        if j >= 2:
                            t.wait_ge(ac.h, acB + 4 + (j - 2))
                        dst = psb[4] if j % 2 == 0 else psb[5]
                        t.matmul(out=dst[:, 0:512], lhsT=ones_row[:],
                                 rhs=arow_a[:, j * 512:(j + 1) * 512],
                                 start=True, stop=True).then_inc(pe.h, 1)
                pe.bump(8)

                @block.scalar
                def _(s):
                    for j in range(8):
                        s.wait_ge(pe.h, peB + 2 + j)
                        src = psb[4] if j % 2 == 0 else psb[5]
                        s.copy(out=ab_row[:, j * 512:(j + 1) * 512],
                               in_=src[:, 0:512]).then_inc(ac.h, 1)
                ac.bump(8)

                @block.tensor
                def _(t):
                    for j in range(8):
                        if j < 2:
                            t.wait_ge(ac.h, acB + 11)
                        else:
                            t.wait_ge(ac.h, acB + 12 + (j - 2))
                        dst = psb[4] if j % 2 == 0 else psb[5]
                        t.matmul(out=dst[:, 0:512], lhsT=ones_row[:],
                                 rhs=arow_b[:, j * 512:(j + 1) * 512],
                                 start=True, stop=True).then_inc(pe.h, 1)
                pe.bump(8)

                @block.scalar
                def _(s):
                    for j in range(8):
                        s.wait_ge(pe.h, peB + 10 + j)
                        src = psb[4] if j % 2 == 0 else psb[5]
                        s.copy(out=bb_row[:, j * 512:(j + 1) * 512],
                               in_=src[:, 0:512]).then_inc(ac.h, 1)
                ac.bump(8)

                ldaB = [lda[0].v, lda[1].v]
                stnB = [stn[0].v, stn[1].v]
                veB2 = veB + 5
                K2_ = 3 if with_mean else 2

                @block.sync
                def _(sync):
                    for c in range(NFC):
                        if c >= 2:
                            sync.wait_ge(ve.h, veB2 + K2_ * (c - 2) + K2_)
                        sync.dma_start(out=xch[:, c % 2, :],
                                       in_=_rows(src_dram, c)).then_inc(lda[c % 2].h, 16)
                        lda[c % 2].bump(16)

                K_ = 3 if with_mean else 2

                @block.vector
                def _(v):
                    for c in range(NFC):
                        v.wait_ge(lda[c % 2].h, ldaB[c % 2] + 16 * (c // 2 + 1))
                        if c == 0:
                            v.wait_ge(ac.h, acB + 11)
                        if c >= 2:
                            v.wait_ge(stn[c % 2].h, stnB[c % 2] + 16 * (c // 2))
                        if c >= 1:
                            v.wait_ge(ve.h, veB2 + K_ * (c - 1) + K_)
                        v.tensor_tensor(out=xt0[:], in0=xch[:, c % 2, :],
                                        in1=ab_row[:], op=ALU.mult).then_inc(ve.h, 1)
                        if c == 0:
                            v.wait_ge(ac.h, acB + 19)
                        v.wait_ge(ve.h, veB2 + K_ * c + 1)
                        v.tensor_tensor(out=xnc[:, c % 2, :], in0=xt0[:],
                                        in1=bb_row[:], op=ALU.add).then_inc(ve.h, 1)
                        if with_mean:
                            v.wait_ge(ve.h, veB2 + K_ * c + 2)
                            v.tensor_reduce(out=mcol[:, c:c + 1], in_=xnc[:, c % 2, :],
                                            axis=mybir.AxisListType.X,
                                            op=ALU.add).then_inc(ve.h, 1)
                ve.bump(K_ * NFC)

                stnB2 = [stn[0].v, stn[1].v]
                gpB2 = gp.v

                @block.gpsimd
                def _(g):
                    for c in range(NFC):
                        g.wait_ge(ve.h, veB2 + K2_ * c + 2)
                        g.dma_start(out=_rows(xn_dram, c),
                                    in_=xnc[:, c % 2, :]).then_inc(stn[c % 2].h, 16)
                        stn[c % 2].bump(16)
                    if do_ag:
                        for q in range(4):
                            g.wait_ge(stn[0].h, stnB2[0] + 16 * (2 * q + 2))
                            g.wait_ge(stn[1].h, stnB2[1] + 16 * (2 * q + 2))
                            g.collective_compute(
                                "AllGather", ALU.bypass, replica_groups=PAIRS,
                                ins=[xn_dram[512 * q:512 * (q + 1), :].opt()],
                                outs=[agA[1024 * q:1024 * (q + 1), :].opt()]).then_inc(cc.h, 1)
                if do_ag:
                    cc.bump(4)

                end_phase({"gpsimd": [(stn[0].h, stn[0].v), (stn[1].h, stn[1].v)],
                           "vector": [(ve.h, ve.v)], "tensor": [(pe.h, pe.v)],
                           "scalar": [(ac.h, ac.v)],
                           "sync": [(lda[0].h, lda[0].v), (lda[1].h, lda[1].v)]})

        def emit_attn_a(xsrc_dram):
            xn3d = xn_dram.rearrange("(c p) t -> p c t", p=128)
            xs3d = xsrc_dram.rearrange("(c p) t -> p c t", p=128)
            rs3d = resid.rearrange("(c p) t -> p c t", p=128)
            ntks = [2 * (tq + 1) for tq in range(NTQ)]

            with ExitStack() as ph:
                Ks = mk(ph, "aa_Ks", [128, 8, T], BF16)
                Vt = mk(ph, "aa_Vt", [128, 32, 1024], BF16)
                diag = mk(ph, "aa_diag", [128, 8, 128], BF16)
                raw = mk(ph, "aa_raw", [128, 2, T], BF16)
                qt = mk(ph, "aa_qt", [128, 2, 8, 256], BF16)
                sig = mk(ph, "aa_sig", [128, 2, 256], BF16)
                sigm = mk(ph, "aa_sigm", [128, 2, 256], BF16)
                xres = mk(ph, "aa_xres", [128, 2, 8, 256], F32)
                x2t = mk(ph, "aa_x2t", [128, 2, 256], F32)
                xb2 = mk(ph, "aa_xb2", [128, 2, 256], BF16)
                xq2 = mk(ph, "aa_xq2", [128, 2, 256], BF16)

                first_stats = [True]

                for g in range(2):
                    peB, veB, acB = pe.v, ve.v, ac.v
                    ldaB = [lda[0].v, lda[1].v]

                    @block.sync
                    def _(sync, g=g, peB=peB, acB=acB):
                        for dc in range(8):
                            if dc >= 2:
                                sync.wait_ge(pe.h, peB + 8 * (dc - 1))
                                sync.wait_ge(ac.h, acB + 9 * (dc - 1))
                            sync.dma_start(out=raw[:, dc % 2, :],
                                           in_=xn3d[:, 8 * g + dc, :]).then_inc(lda[dc % 2].h, 16)
                            lda[dc % 2].bump(16)

                    @block.vector
                    def _(v, g=g, peB=peB, veB=veB):
                        if g == 1:
                            v.wait_ge(pe.h, peB)
                        for dc in range(8):
                            v.tensor_scalar(out=diag[:, dc, :], in0=id_bf[:],
                                            scalar1=vcol[:, 2 * dc + g:2 * dc + g + 1],
                                            scalar2=None, op0=ALU.mult).then_inc(ve.h, 1)
                    ve.bump(8)

                    def _ks_ac(dc):
                        return acB + dc * 9 + 1

                    def _ev_ac(gi):
                        return acB + (gi // 8) * 9 + 1 + (gi % 8) + 1

                    @block.scalar
                    def _(s, g=g, peB=peB, acB=acB, ldaB=ldaB):
                        if g == 1:
                            s.wait_ge(pe.h, peB)
                        for dc in range(8):
                            s.wait_ge(lda[dc % 2].h, ldaB[dc % 2] + 16 * (dc // 2 + 1))
                            s.mul(out=Ks[:, dc, :], in_=raw[:, dc % 2, :],
                                  mul=qkcol[:, 2 * dc + g:2 * dc + g + 1]).then_inc(ac.h, 1)
                            for i4 in range(8):
                                gi = dc * 8 + i4
                                bank = psb[4] if gi % 2 == 0 else psb[5]
                                s.wait_ge(pe.h, peB + gi + 1)
                                s.copy(out=Vt[:, i4 * 4:(i4 + 1) * 4, dc * 128:(dc + 1) * 128],
                                       in_=bank[:, 0:512].rearrange("p (k c) -> p k c", k=4)
                                       ).then_inc(ac.h, 1)
                    ac.bump(72)

                    @block.tensor
                    def _(t, g=g, peB=peB, veB=veB, acB=acB, ldaB=ldaB):
                        for dc in range(8):
                            t.wait_ge(ve.h, veB + dc + 1)
                            t.wait_ge(lda[dc % 2].h, ldaB[dc % 2] + 16 * (dc // 2 + 1))
                            for i4 in range(8):
                                gi = dc * 8 + i4
                                bank = psb[4] if gi % 2 == 0 else psb[5]
                                if gi >= 2:
                                    t.wait_ge(ac.h, _ev_ac(gi - 2))
                                for k in range(4):
                                    i = i4 * 4 + k
                                    mm = t.matmul(out=bank[:, k * 128:(k + 1) * 128],
                                                  lhsT=raw[:, dc % 2, i * 128:(i + 1) * 128],
                                                  rhs=diag[:, dc, :],
                                                  start=(k == 0), stop=(k == 3))
                                mm.then_inc(pe.h, 1)
                    pe.bump(64)



                    peS, acS, veS = pe.v, ac.v, ve.v
                    ldqB = [ldq[0].v, ldq[1].v]
                    ldxB = [ldx[0].v, ldx[1].v]
                    stxB = [stx[0].v, stx[1].v]

                    plans = []
                    for tq in range(NTQ):
                        ntk = ntks[tq]
                        seq = [("S", 0), ("S", 1)]
                        for k in range(2, ntk):
                            seq.append(("PV", k - 2))
                            seq.append(("S", k))
                        seq.append(("PV", ntk - 2))
                        seq.append(("PV", ntk - 1))
                        seq += [("T", dc) for dc in range(8)]
                        plans.append(seq)
                    pe_pos = {}
                    pc = peS
                    for tq, seq in enumerate(plans):
                        for ent in seq:
                            pc += 1
                            pe_pos[(tq,) + ent] = pc
                    pe_total = pc
                    ac_pos = {}
                    a = acS
                    for tq in range(NTQ):
                        for tk in range(ntks[tq]):
                            a += 1
                            ac_pos[(tq, tk)] = a
                    ac_total = a
                    ve_pos = {}
                    w = veS
                    for tq in range(NTQ):
                        for j in range(2):
                            w += 1
                            ve_pos[(tq, "m", j)] = w
                        for dc in range(8):
                            w += 1
                            ve_pos[(tq, "a", dc)] = w
                            w += 1
                            ve_pos[(tq, "q", dc)] = w
                    ve_total = w

                    @block.sync
                    def _(sync, g=g, peS=peS, ldqB=ldqB, ldxB=ldxB):
                        if g == 1:
                            sync.wait_ge(pe.h, peS - 64)
                        for tq in range(NTQ):
                            if tq >= 2:
                                sync.wait_ge(pe.h, pe_pos[(tq - 2, "PV", ntks[tq - 2] - 1)])
                                sync.wait_ge(ve.h, ve_pos[(tq - 2, "q", 7)])
                            sync.dma_start(out=qt[:, tq % 2, :, :],
                                           in_=xn3d[:, 8 * g:8 * g + 8, tq * 256:(tq + 1) * 256]
                                           ).then_inc(ldq[tq % 2].h, 16)
                            ldq[tq % 2].bump(16)
                            sync.dma_start(out=xres[:, tq % 2, :, :],
                                           in_=xs3d[:, 8 * g:8 * g + 8, tq * 256:(tq + 1) * 256]
                                           ).then_inc(ldx[tq % 2].h, 16)
                            ldx[tq % 2].bump(16)

                    @block.tensor
                    def _(t, g=g, peS=peS, acS=acS, veS=veS, ldqB=ldqB):
                        t.wait_ge(ac.h, acS)
                        for tq in range(NTQ):
                            ntk = ntks[tq]
                            t.wait_ge(ldq[tq % 2].h, ldqB[tq % 2] + 16 * (tq // 2 + 1))
                            for ent in plans[tq]:
                                kind, idx = ent
                                if kind == "S":
                                    tk = idx
                                    sb = psb[4] if tk % 2 == 0 else psb[5]
                                    for dc in range(8):
                                        mm = t.matmul(out=sb[:, 0:256],
                                                      lhsT=Ks[:, dc, tk * 128:(tk + 1) * 128],
                                                      rhs=qt[:, tq % 2, dc, :],
                                                      start=(dc == 0), stop=(dc == 7))
                                    mm.then_inc(pe.h, 1)
                                elif kind == "PV":
                                    tk = idx
                                    masked = (tk >= ntk - 2)
                                    if tk == 0:
                                        if tq > 0:
                                            t.wait_ge(ve.h, ve_pos[(tq - 1, "q", 7)])
                                        elif g == 1:
                                            t.wait_ge(ve.h, veS)
                                    if masked:
                                        j = tk - (ntk - 2)
                                        t.wait_ge(ve.h, ve_pos[(tq, "m", j)])
                                        src = sigm[:, j, :]
                                    else:
                                        t.wait_ge(ac.h, ac_pos[(tq, tk)])
                                        src = sig[:, tk % 2, :]
                                    for dc in range(8):
                                        mm = t.matmul(
                                            out=psb[dc // 2][:, (dc % 2) * 256:(dc % 2) * 256 + 256],
                                            lhsT=Vt[:, tk, dc * 128:(dc + 1) * 128],
                                            rhs=src,
                                            start=(tk == 0 and dc % 2 == 0),
                                            stop=(tk == ntk - 1 and dc % 2 == 1))
                                    mm.then_inc(pe.h, 1)
                                else:
                                    dc = idx
                                    t.wait_ge(ve.h, ve_pos[(tq, "q", dc)])
                                    for ii in range(2):
                                        i = 2 * tq + ii
                                        fs = first_stats[0]
                                        t.matmul(out=ps_stat[:, i:i + 1],
                                                 lhsT=xb2[:, dc % 2, ii * 128:(ii + 1) * 128],
                                                 rhs=ones_bf[:, 0:1], start=fs, stop=False)
                                        first_stats[0] = False
                                        mm = t.matmul(out=ps_stat[:, 32 + i:33 + i],
                                                      lhsT=xq2[:, dc % 2, ii * 128:(ii + 1) * 128],
                                                      rhs=ones_bf[:, 0:1], start=False,
                                                      stop=(g == 1 and tq == NTQ - 1
                                                            and dc == 7 and ii == 1))
                                    mm.then_inc(pe.h, 1)

                    @block.scalar
                    def _(s, g=g, peS=peS, acS=acS, veS=veS):
                        for tq in range(NTQ):
                            ntk = ntks[tq]
                            for tk in range(ntk):
                                s.wait_ge(pe.h, pe_pos[(tq, "S", tk)])
                                if tk >= 2:
                                    s.wait_ge(pe.h, pe_pos[(tq, "PV", tk - 2)])
                                elif tq > 0:
                                    s.wait_ge(pe.h, pe_pos[(tq - 1, "PV", ntks[tq - 1] - 1)])
                                    s.wait_ge(ve.h, ve_pos[(tq - 1, "m", 1)])
                                elif g == 1:
                                    s.wait_ge(pe.h, peS - 64)
                                    s.wait_ge(ve.h, veS)
                                sb = psb[4] if tk % 2 == 0 else psb[5]
                                s.activation(out=sig[:, tk % 2, :], in_=sb[:, 0:256],
                                             func=AF.Sigmoid,
                                             scale=sc125[:, 0:1]).then_inc(ac.h, 1)

                    @block.vector
                    def _(v, g=g, peS=peS, veS=veS, ldxB=ldxB, stxB=stxB):
                        if g == 1:
                            v.wait_ge(pe.h, peS - 64)
                        for tq in range(NTQ):
                            ntk = ntks[tq]
                            for j in range(2):
                                tk = ntk - 2 + j
                                v.wait_ge(ac.h, ac_pos[(tq, tk)])
                                v.tensor_tensor(out=sigm[:, j, :], in0=sig[:, tk % 2, :],
                                                in1=amask_sb[:, j, :], op=ALU.mult).then_inc(ve.h, 1)
                            v.wait_ge(pe.h, pe_pos[(tq, "PV", ntk - 1)])
                            v.wait_ge(ldx[tq % 2].h, ldxB[tq % 2] + 16 * (tq // 2 + 1))
                            for dc in range(8):
                                if tq > 0 or dc >= 2:
                                    ptq, pdc = (tq, dc - 2) if dc >= 2 else (tq - 1, dc + 6)
                                    v.wait_ge(pe.h, pe_pos[(ptq, "T", pdc)])
                                    v.wait_ge(stx[dc % 2].h,
                                              stxB[dc % 2] + 16 * ((tq * 8 + dc) // 2))
                                elif g == 1:
                                    v.wait_ge(stx[dc % 2].h, stxB[dc % 2])
                                v.tensor_tensor(out=x2t[:, dc % 2, :],
                                                in0=psb[dc // 2][:, (dc % 2) * 256:(dc % 2) * 256 + 256],
                                                in1=xres[:, tq % 2, dc, :], op=ALU.add).then_inc(ve.h, 1)
                                v.wait_ge(ve.h, ve_pos[(tq, "a", dc)])
                                v.tensor_copy(out=xb2[:, dc % 2, :], in_=x2t[:, dc % 2, :])
                                v.tensor_tensor(out=xq2[:, dc % 2, :], in0=x2t[:, dc % 2, :],
                                                in1=x2t[:, dc % 2, :], op=ALU.mult).then_inc(ve.h, 1)

                    @block.gpsimd
                    def _(g_, g_head=g, veS=veS, stxB=stxB):
                        for tq in range(NTQ):
                            for dc in range(8):
                                g_.wait_ge(ve.h, ve_pos[(tq, "a", dc)])
                                g_.dma_start(out=rs3d[:, 8 * g_head + dc, tq * 256:(tq + 1) * 256],
                                             in_=x2t[:, dc % 2, :]).then_inc(stx[dc % 2].h, 16)
                                stx[dc % 2].bump(16)

                    pe.v = pe_total
                    ac.v = ac_total
                    ve.v = ve_total

                end_phase({"gpsimd": [(stx[0].h, stx[0].v), (stx[1].h, stx[1].v)],
                           "vector": [(ve.h, ve.v)], "tensor": [(pe.h, pe.v)],
                           "scalar": [(ac.h, ac.v)],
                           "sync": [(ldq[0].h, ldq[0].v), (ldq[1].h, ldq[1].v),
                                    (ldx[0].h, ldx[0].v), (ldx[1].h, ldx[1].v),
                                    (lda[0].h, lda[0].v), (lda[1].h, lda[1].v)]})


        ldf4 = [SEM("s_ldf0"), SEM("s_ldf1"), SEM("s_ldf2"), SEM("s_ldf3")]
        gq = SEM("s_gq"); gw = SEM("s_gw")

        def emit_ffn(ccB_ag, first_ffn, src_ag, wT_d, dst_dram,
                     with_lora=True, do_stats=True):
            peB, veB, acB, gpB, ccB = pe.v, ve.v, ac.v, gp.v, cc.v
            gqB, gwB = gq.v, gw.v
            ldxB = [ldx[0].v, ldx[1].v]
            stxB = [stx[0].v, stx[1].v]
            ldfB = [s.v for s in ldf4]
            ford = [c for q in range(4)
                    for c in (list(range(4 * q, 4 * q + 4)) + list(range(16 + 4 * q, 16 + 4 * q + 4)))]
            ag3d = src_ag.rearrange("(c p) t -> p c t", p=128)
            rs3d = resid.rearrange("(c p) t -> p c t", p=128)
            ds3d = dst_dram.rearrange("(c p) t -> p c t", p=128)
            wo3d = wT_d.rearrange("(c p) o -> p c o", p=128)
            dn3d = down_d.rearrange("(c p) r -> p c r", p=128)
            up3d = up_d.rearrange("(r p) o -> p r o", p=128)
            wq3d = wqT_d.rearrange("(c p) k -> p c k", p=128)

            with ExitStack() as ph:
                xq = mk(ph, "ff_xq", [128, 32, 1024], BF16)
                Wp = mk(ph, "ff_Wp", [128, 32, 512], BF16)
                dn = mk(ph, "ff_dn", [128, 32, 512], BF16)
                up = mk(ph, "ff_up", [128, 4, 2048], BF16)
                hTr = mk(ph, "ff_hTr", [128, 4, 1024], BF16)
                hT = mk(ph, "ff_hT", [128, 4, 1024], BF16)
                wq_sb = mk(ph, "ff_wq", [128, 16, 128], F32)
                keys_sb = mk(ph, "ff_keys", [128, 8], F32)
                sel_sb = mk(ph, "ff_sel", [8, 4, 128], F32)
                qpart = mk(ph, "ff_qp", [128, 1], F32)
                qTc = mk(ph, "ff_qT", [128, 1], F32)
                srow = mk(ph, "ff_srow", [1, 40], F32)
                scol = mk(ph, "ff_scol", [8, 1], F32)
                ccol = mk(ph, "ff_ccol", [8, 1], F32)
                xr2 = mk(ph, "ff_xr2", [128, 2, 4, 512], F32)
                x3t = mk(ph, "ff_x3t", [128, 2, 512], F32)
                xb3 = mk(ph, "ff_xb3", [128, 2, 512], BF16)
                xq3 = mk(ph, "ff_xq3", [128, 2, 512], BF16)

                pe_pos = {}
                pc = peB
                if with_lora:
                    pc += 1; pe_pos["qmm"] = pc
                for tqr in range(4):
                    if with_lora:
                        for rnd in range(8):
                            pc += 1; pe_pos[("h", tqr, rnd)] = pc
                        if tqr == 0:
                            pc += 1; pe_pos["scoresT"] = pc
                            pc += 1; pe_pos["sT"] = pc
                            pc += 1; pe_pos["cT"] = pc
                            pc += 1; pe_pos["sel"] = pc
                    for og in range(4):
                        for tch in range(2):
                            pc += 1; pe_pos[("b", tqr, og, tch)] = pc
                            if do_stats:
                                for j in range(4):
                                    pc += 1; pe_pos[("t", tqr, og, tch, j)] = pc
                pe_total = pc
                ac_pos = {}
                a = acB
                if with_lora:
                    a += 1; ac_pos["qpart"] = a
                    for rnd in range(8):
                        a += 1; ac_pos[("hev", 0, rnd)] = a
                    for nm in ("scol", "srow", "exp", "ccol", "cvec"):
                        a += 1; ac_pos[nm] = a
                    for tqr in range(1, 4):
                        for rnd in range(8):
                            a += 1; ac_pos[("hev", tqr, rnd)] = a
                ac_total = a
                ve_pos = {}
                w = veB
                if with_lora:
                    for nm in ("mx", "bias", "sum", "rec", "cf"):
                        w += 1; ve_pos[nm] = w
                for tqr in range(4):
                    if with_lora:
                        for rnd in range(8):
                            w += 1; ve_pos[("hs", tqr, rnd)] = w
                    for og in range(4):
                        for tch in range(2):
                            for j in range(4):
                                w += 1; ve_pos[("a", tqr, og, tch, j)] = w
                                if do_stats:
                                    w += 1
                                ve_pos[("q", tqr, og, tch, j)] = w
                ve_total = w
                gp_pos = {}
                gg = gpB
                for tqr in range(4):
                    for og in range(4):
                        gg += 16; gp_pos[("wp", tqr, og)] = gg
                gp_total = gg

                def evac_idx(tqr, og, tch, j):
                    return ((tqr * 4 + og) * 2 + tch) * 4 + j

                @block.sync
                def _(sync):
                    if with_lora:
                        sync.dma_start(out=wq_sb[:], in_=wq3d).then_inc(ld.h, 16)
                        sync.dma_start(out=keys_sb[:], in_=keysT_d).then_inc(ld.h, 16)
                        sync.dma_start(out=sel_sb[:], in_=sel_d.rearrange("g e k -> e g k")).then_inc(ld.h, 16)
                        ld.bump(48)
                    ii = 0
                    for tqr in range(4):
                        if tqr > 0:
                            key = ("t", tqr - 1, 3, 1, 3) if do_stats else ("b", tqr - 1, 3, 1)
                            sync.wait_ge(pe.h, pe_pos[key])
                        for q in range(4):
                            sync.wait_ge(cc.h, ccB_ag + q + 1)
                            ts = slice(tqr * 1024, (tqr + 1) * 1024)
                            sync.dma_start(out=xq[:, 8 * q:8 * q + 8, :],
                                           in_=ag3d[:, 8 * q:8 * q + 8, ts]).then_inc(ldf4[q].h, 16)
                            ldf4[q].bump(16)
                        for og in range(4):
                            for tch in range(2):
                                ii = (tqr * 4 + og) * 2 + tch
                                if ii >= 2:
                                    pv = ii - 2
                                    ptqr, pog, ptch = pv // 8, (pv // 2) % 4, pv % 2
                                    sync.wait_ge(ve.h, ve_pos[("q", ptqr, pog, ptch, 3)])
                                sync.dma_start(
                                    out=xr2[:, ii % 2, :, :],
                                    in_=rs3d[:, og * 4:(og + 1) * 4,
                                             tqr * 1024 + tch * 512:tqr * 1024 + (tch + 1) * 512]
                                ).then_inc(ldx[ii % 2].h, 16)
                                ldx[ii % 2].bump(16)

                @block.gpsimd
                def _(g):
                    if with_lora:
                        g.dma_start(out=dn[:], in_=dn3d).then_inc(gw.h, 16)
                        g.dma_start(out=up[:], in_=up3d).then_inc(gw.h, 16)
                        g.wait_ge(ac.h, ac_pos["qpart"])
                        g.dma_start(out=arq_i, in_=qpart[:]).then_inc(gq.h, 16)
                        g.wait_ge(gq.h, gqB + 16)
                        g.collective_compute("AllReduce", ALU.add, replica_groups=PAIRS,
                                             ins=[arq_i.opt()], outs=[arq_o.opt()]).then_inc(cc.h, 1)
                        g.wait_ge(cc.h, ccB + 1)
                        g.dma_start(out=qTc[:], in_=arq_o).then_inc(gq.h, 16)
                    sii = 0
                    for tqr in range(4):
                        for og in range(4):
                            if tqr + og > 0:
                                pv = tqr * 4 + og - 1
                                g.wait_ge(pe.h, pe_pos[("b", pv // 4, pv % 4, 1)])
                            g.dma_start(out=Wp[:],
                                        in_=wo3d[:, :, (og % 4) * 512:(og % 4 + 1) * 512]
                                        ).then_inc(gp.h, 16)
                            for tch in range(2):
                                for j in range(4):
                                    g.wait_ge(ve.h, ve_pos[("a", tqr, og, tch, j)])
                                    g.dma_start(
                                        out=ds3d[:, og * 4 + j,
                                                 tqr * 1024 + tch * 512:tqr * 1024 + (tch + 1) * 512],
                                        in_=x3t[:, sii % 2, :]).then_inc(stx[sii % 2].h, 16)
                                    stx[sii % 2].bump(16)
                                    sii += 1
                if with_lora:
                    cc.bump(1)
                    gq.bump(32)
                    gw.bump(32)
                gp.v = gp_total

                @block.tensor
                def _(t):
                    if with_lora:
                        t.wait_ge(ld.h, ld.v)
                        for c in range(16):
                            mm = t.matmul(out=psb[0][:, 0:1], lhsT=wq_sb[:, c, :],
                                          rhs=mcol[:, c:c + 1], start=(c == 0), stop=(c == 15))
                        mm.then_inc(pe.h, 1)
                    for tqr in range(4):
                        for q in range(4):
                            t.wait_ge(ldf4[q].h, ldfB[q] + 16 * (tqr + 1))
                        if with_lora:
                            for rnd in range(8):
                                g4, tch = rnd // 2, rnd % 2
                                bank = psb[4] if rnd % 2 == 0 else psb[5]
                                if (tqr, rnd) > (0, 1):
                                    pv = tqr * 8 + rnd - 2
                                    t.wait_ge(ac.h, ac_pos[("hev", pv // 8, pv % 8)])
                                if tqr == 0 and rnd == 0:
                                    t.wait_ge(gw.h, gwB + 32)
                                for pi in range(32):
                                    fc = ford[pi]
                                    mm = t.matmul(out=bank[:, 0:512],
                                                  lhsT=dn[:, fc, g4 * 128:(g4 + 1) * 128],
                                                  rhs=xq[:, pi, tch * 512:(tch + 1) * 512],
                                                  start=(pi == 0), stop=(pi == 31))
                                mm.then_inc(pe.h, 1)
                            if tqr == 0:
                                t.wait_ge(gq.h, gqB + 32)
                                t.matmul(out=psb[1][0:8, 0:1], lhsT=keys_sb[:],
                                         rhs=qTc[:], start=True, stop=True).then_inc(pe.h, 1)
                                t.wait_ge(ac.h, ac_pos["scol"])
                                t.transpose(out=psb[2][0:1, 0:8], in_=scol[:],
                                            identity=id_f32[0:8, 0:8]).then_inc(pe.h, 1)
                                t.wait_ge(ve.h, ve_pos["cf"])
                                t.transpose(out=psb[3][0:8, 0:1], in_=srow[0:1, 26:34],
                                            identity=id_f32[0:1, 0:1]).then_inc(pe.h, 1)
                                t.wait_ge(ac.h, ac_pos["ccol"])
                                for g4 in range(4):
                                    mm = t.matmul(out=psb[1][:, g4:g4 + 1], lhsT=sel_sb[:, g4, :],
                                                  rhs=ccol[:], start=(g4 == 0), stop=(g4 == 3))
                                mm.then_inc(pe.h, 1)
                        for og in range(4):
                            t.wait_ge(gp.h, gp_pos[("wp", tqr, og)])
                            if with_lora and tqr == 0 and og == 0:
                                t.wait_ge(ac.h, ac_pos["cvec"])
                            for tch in range(2):
                                ii = (tqr * 4 + og) * 2 + tch
                                lag = 2 if do_stats else 1
                                if ii >= lag:
                                    pv = ii - lag
                                    t.wait_ge(ve.h, ve_pos[("q", pv // 8, (pv // 2) % 4, pv % 2, 3)])
                                for pi in range(32):
                                    fc = ford[pi]
                                    for j in range(4):
                                        mm = t.matmul(out=psb[j][:, 0:512],
                                                      lhsT=Wp[:, fc, j * 128:(j + 1) * 128],
                                                      rhs=xq[:, pi, tch * 512:(tch + 1) * 512],
                                                      start=(pi == 0),
                                                      stop=(not with_lora and pi == 31))
                                if with_lora:
                                    t.wait_ge(ve.h, ve_pos[("hs", tqr, 6 + tch)])
                                    for j in range(4):
                                        for rt in range(4):
                                            mm = t.matmul(
                                                out=psb[j][:, 0:512],
                                                lhsT=up[:, rt, (og * 4 + j) * 128:(og * 4 + j + 1) * 128],
                                                rhs=hT[:, rt, tch * 512:(tch + 1) * 512],
                                                start=False, stop=(rt == 3))
                                mm.then_inc(pe.h, 1)
                                if do_stats:
                                    for j in range(4):
                                        t.wait_ge(ve.h, ve_pos[("q", tqr, og, tch, j)])
                                        for k in range(4):
                                            it = tqr * 8 + tch * 4 + k
                                            fs = first_ffn[0]
                                            t.matmul(out=ps_stat[:, it:it + 1],
                                                     lhsT=xb3[:, (evac_idx(tqr, og, tch, j) % 2),
                                                              k * 128:(k + 1) * 128],
                                                     rhs=ones_bf[:, 0:1], start=fs, stop=False)
                                            first_ffn[0] = False
                                            mm = t.matmul(out=ps_stat[:, 32 + it:33 + it],
                                                          lhsT=xq3[:, (evac_idx(tqr, og, tch, j) % 2),
                                                                   k * 128:(k + 1) * 128],
                                                          rhs=ones_bf[:, 0:1], start=False,
                                                          stop=(tqr == 3 and og == 3 and tch == 1
                                                                and j == 3 and k == 3))
                                        mm.then_inc(pe.h, 1)

                @block.scalar
                def _(s):
                    if not with_lora:
                        return

                    def hev(tqr, rnd):
                        bank = psb[4] if rnd % 2 == 0 else psb[5]
                        g4, tch = rnd // 2, rnd % 2
                        s.wait_ge(pe.h, pe_pos[("h", tqr, rnd)])
                        s.copy(out=hTr[:, g4, tch * 512:(tch + 1) * 512],
                               in_=bank[:, 0:512]).then_inc(ac.h, 1)

                    s.wait_ge(pe.h, pe_pos["qmm"])
                    s.copy(out=qpart[:], in_=psb[0][:, 0:1]).then_inc(ac.h, 1)
                    for rnd in range(8):
                        hev(0, rnd)
                    s.wait_ge(pe.h, pe_pos["scoresT"])
                    s.copy(out=scol[:], in_=psb[1][0:8, 0:1]).then_inc(ac.h, 1)
                    s.wait_ge(pe.h, pe_pos["sT"])
                    s.copy(out=srow[0:1, 0:8], in_=psb[2][0:1, 0:8]).then_inc(ac.h, 1)
                    s.wait_ge(ve.h, ve_pos["bias"])
                    s.activation(out=srow[0:1, 16:24], in_=srow[0:1, 0:8], func=AF.Exp,
                                 scale=scC[0:1, 0:1], bias=srow[0:1, 9:10]).then_inc(ac.h, 1)
                    s.wait_ge(pe.h, pe_pos["cT"])
                    s.copy(out=ccol[:], in_=psb[3][0:8, 0:1]).then_inc(ac.h, 1)
                    s.wait_ge(pe.h, pe_pos["sel"])
                    s.copy(out=cvec[:], in_=psb[1][:, 0:4]).then_inc(ac.h, 1)
                    for tqr in range(1, 4):
                        for rnd in range(8):
                            hev(tqr, rnd)
                ac.v = ac_total

                @block.vector
                def _(v):
                    if with_lora:
                        v.wait_ge(ac.h, ac_pos["srow"])
                        v.tensor_reduce(out=srow[0:1, 8:9], in_=srow[0:1, 0:8],
                                        axis=mybir.AxisListType.X, op=ALU.max).then_inc(ve.h, 1)
                        v.wait_ge(ve.h, ve_pos["mx"])
                        v.tensor_scalar(out=srow[0:1, 9:10], in0=srow[0:1, 8:9],
                                        scalar1=scC[0:1, 1:2], scalar2=None,
                                        op0=ALU.mult).then_inc(ve.h, 1)
                        v.wait_ge(ac.h, ac_pos["exp"])
                        v.tensor_reduce(out=srow[0:1, 24:25], in_=srow[0:1, 16:24],
                                        axis=mybir.AxisListType.X, op=ALU.add).then_inc(ve.h, 1)
                        v.wait_ge(ve.h, ve_pos["sum"])
                        v.reciprocal(out=srow[0:1, 25:26], in_=srow[0:1, 24:25]).then_inc(ve.h, 1)
                    if with_lora:
                        v.wait_ge(ve.h, ve_pos["rec"])
                        v.tensor_scalar(out=srow[0:1, 26:34], in0=srow[0:1, 16:24],
                                        scalar1=srow[0:1, 25:26], scalar2=None,
                                        op0=ALU.mult).then_inc(ve.h, 1)
                    for tqr in range(4):
                        if with_lora:
                            for rnd in range(8):
                                g4, tch = rnd // 2, rnd % 2
                                v.wait_ge(ac.h, ac_pos[("hev", tqr, rnd)])
                                if tqr == 0 and rnd == 0:
                                    v.wait_ge(ac.h, ac_pos["cvec"])
                                v.tensor_scalar(out=hT[:, g4, tch * 512:(tch + 1) * 512],
                                                in0=hTr[:, g4, tch * 512:(tch + 1) * 512],
                                                scalar1=cvec[:, g4:g4 + 1], scalar2=None,
                                                op0=ALU.mult).then_inc(ve.h, 1)
                        for og in range(4):
                            for tch in range(2):
                                ii = (tqr * 4 + og) * 2 + tch
                                v.wait_ge(pe.h, pe_pos[("b", tqr, og, tch)])
                                v.wait_ge(ldx[ii % 2].h, ldxB[ii % 2] + 16 * (ii // 2 + 1))
                                for j in range(4):
                                    ei = evac_idx(tqr, og, tch, j)
                                    if ei >= 2:
                                        pj = ei - 2
                                        ptqr = pj // 32
                                        prem = pj % 32
                                        pog, ptch, ppj = prem // 8, (prem // 4) % 2, prem % 4
                                        if do_stats:
                                            v.wait_ge(pe.h, pe_pos[("t", ptqr, pog, ptch, ppj)])
                                        v.wait_ge(stx[ei % 2].h, stxB[ei % 2] + 16 * (ei // 2))
                                    v.tensor_tensor(out=x3t[:, ei % 2, :],
                                                    in0=psb[j][:, 0:512],
                                                    in1=xr2[:, ii % 2, j, :],
                                                    op=ALU.add).then_inc(ve.h, 1)
                                    if do_stats:
                                        v.wait_ge(ve.h, ve_pos[("a", tqr, og, tch, j)])
                                        v.tensor_copy(out=xb3[:, ei % 2, :], in_=x3t[:, ei % 2, :])
                                        v.tensor_tensor(out=xq3[:, ei % 2, :], in0=x3t[:, ei % 2, :],
                                                        in1=x3t[:, ei % 2, :],
                                                        op=ALU.mult).then_inc(ve.h, 1)
                ve.v = ve_total
                pe.v = pe_total

                end_phase({"gpsimd": [(stx[0].h, stx[0].v), (stx[1].h, stx[1].v)],
                           "vector": [(ve.h, ve.v)], "tensor": [(pe.h, pe.v)],
                           "scalar": [(ac.h, ac.v)],
                           "sync": [(ldx[0].h, ldx[0].v), (ldx[1].h, ldx[1].v)] +
                                   [(s_.h, s_.v) for s_ in ldf4]})


        spq = [SEM("s_spq0"), SEM("s_spq1")]
        kvq = [SEM("s_kvq0"), SEM("s_kvq1")]
        ldfb = [SEM("s_ldfb0"), SEM("s_ldfb1"), SEM("s_ldfb2"), SEM("s_ldfb3")]

        def emit_attn_b(ccB_ag):
            ag3d = agA.rearrange("(c p) t -> p c t", p=128)
            q3d = qsp.rearrange("(c p) t -> p c t", p=128)
            k3d = ksp.rearrange("(c p) t -> p c t", p=128)
            ob3d = obsp.rearrange("(c p) t -> p c t", p=128)
            w3ds = [qwT_d.rearrange("(c p) o -> p c o", p=128),
                    kwT_d.rearrange("(c p) o -> p c o", p=128),
                    vwT_d.rearrange("(c p) o -> p c o", p=128)]
            ford = [c for q in range(4)
                    for c in (list(range(4 * q, 4 * q + 4)) + list(range(16 + 4 * q, 16 + 4 * q + 4)))]
            def text_rng(r):
                lo = max(513, 512 * r) - 512 * r
                hi = min(4040, 512 * (r + 1)) - 512 * r
                return (lo, hi) if hi > lo else None

            with ExitStack() as pho:
                visb = mk(pho, "ab_visb", [128, 33], F32)
                scD = mk(pho, "ab_scD", [128, 1], F32)
                qbars = mk(pho, "ab_qbars", [128, 16], F32)

                @block.vector
                def _(v):
                    v.memset(scD[:], 1.0 / math.sqrt(128.0))
                    v.memset(qbars[:], 0.0).then_inc(ve.h, 1)
                ve.bump(1)

                with ExitStack() as ph1:
                    xsl = mk(ph1, "ab_xsl", [128, 2, 32, 512], BF16)
                    Wp = mk(ph1, "ab_Wp", [128, 2, 32, 512], BF16)
                    vtm = mk(ph1, "ab_vtm", [128, 4, 32, 128], BF16)
                    vev = mk(ph1, "ab_vev", [128, 2, 512], BF16)
                    ropec = mk(ph1, "ab_rc", [128, T], BF16)
                    ropes = mk(ph1, "ab_rs", [128, T], BF16)
                    rsw = mk(ph1, "ab_rsw", [128, 2, 512], F32)
                    rt1 = mk(ph1, "ab_rt1", [128, 512], F32)
                    rt2 = mk(ph1, "ab_rt2", [128, 512], F32)
                    qtmp = mk(ph1, "ab_qtmp", [128, 1], F32)
                    robuf = mk(ph1, "ab_rob", [128, 2, 512], BF16)
                    kv32 = mk(ph1, "ab_kv32", [128, 2, 512], F32)

                    @block.sync
                    def _(sync):
                        sync.dma_start(out=ropec[:], in_=ropec_d).then_inc(ld.h, 16)
                        sync.dma_start(out=ropes[:], in_=ropes_d).then_inc(ld.h, 16)
                    ld.bump(32)
                    ld_rope = ld.v

                    panel_starts = []
                    for Wi in range(3):
                        for og in range(4):
                            peB_, acB_, veB_ = pe.v, ac.v, ve.v
                            kpan = Wi * 4 + og
                            panel_starts.append(peB_)
                            ldfB_ = [s.v for s in ldf4]
                            ldfbB_ = [s.v for s in ldfb]
                            spqB_ = [spq[0].v, spq[1].v]

                            @block.gpsimd
                            def _(g, Wi=Wi, og=og, kpan=kpan,
                                  tgt=(panel_starts[kpan - 1] if kpan >= 1 else 0)):
                                if kpan >= 2:
                                    g.wait_ge(pe.h, tgt)
                                g.dma_start(out=Wp[:, kpan % 2, :, :],
                                            in_=w3ds[Wi][:, :, og * 512:(og + 1) * 512]
                                            ).then_inc(kvq[kpan % 2].h, 16)
                            kvq[kpan % 2].bump(16)
                            gpv = (kvq[kpan % 2].h, kvq[kpan % 2].v)

                            nr = 1 if Wi < 2 else 5

                            ve_unit = []
                            ve_t2 = []
                            if Wi < 2:
                                w_ = veB_
                                for r_ in range(8):
                                    for j_ in range(4):
                                        ve_t2.append(w_ + 2)
                                        w_ += 3
                                        if Wi == 0 and text_rng(r_) is not None:
                                            w_ += 2
                                        ve_unit.append(w_)

                            @block.sync
                            def _(sync, Wi=Wi, og=og, peB_=peB_, nr=nr,
                                  ve_unit=list(ve_unit)):
                                dst = q3d if Wi == 0 else k3d
                                for t8 in range(8):
                                    if t8 < 2 and not (Wi == 0 and og == 0):
                                        sync.wait_ge(pe.h, peB_)
                                    if t8 >= 2:
                                        sync.wait_ge(pe.h, peB_ + nr * (t8 - 1))
                                    for q in range(4):
                                        if Wi == 0 and og == 0:
                                            sync.wait_ge(cc.h, ccB_ag + q + 1)
                                        sem = ldf4[q] if t8 % 2 == 0 else ldfb[q]
                                        sync.dma_start(out=xsl[:, t8 % 2, 8 * q:8 * q + 8, :],
                                                       in_=ag3d[:, 8 * q:8 * q + 8,
                                                                t8 * 512:(t8 + 1) * 512]
                                                       ).then_inc(sem.h, 16)
                                        sem.bump(16)
                                    if Wi < 2 and t8 >= 1:
                                        r = t8 - 1
                                        for j in range(4):
                                            u = 4 * r + j
                                            sync.wait_ge(ve.h, ve_unit[u])
                                            sync.dma_start(out=dst[:, og * 4 + j,
                                                                   r * 512:(r + 1) * 512],
                                                           in_=robuf[:, u % 2, :]
                                                           ).then_inc(spq[u % 2].h, 16)
                                            spq[u % 2].bump(16)

                            if Wi < 2:
                                ve_tot_ = ve_unit[-1]

                                @block.tensor
                                def _(t, Wi=Wi, og=og, kpan=kpan, peB_=peB_, acB_=acB_, veB_=veB_,
                                      ldfB_=ldfB_, ldfbB_=ldfbB_, gpv=gpv, ve_unit=list(ve_unit)):
                                    t.wait_ge(gpv[0], gpv[1])
                                    for r in range(8):
                                        for q in range(4):
                                            sem = ldf4[q] if r % 2 == 0 else ldfb[q]
                                            base = ldfB_[q] if r % 2 == 0 else ldfbB_[q]
                                            t.wait_ge(sem.h, base + 16 * (r // 2 + 1))
                                        if r >= 1:
                                            t.wait_ge(ac.h, acB_ + 4 * (r - 1) + 4)
                                            t.wait_ge(ve.h, ve_unit[4 * (r - 1) + 3])
                                        for pi in range(32):
                                            fc = ford[pi]
                                            for j in range(4):
                                                mm = t.matmul(out=psb[j][:, 0:512],
                                                              lhsT=Wp[:, kpan % 2, fc, j * 128:(j + 1) * 128],
                                                              rhs=xsl[:, r % 2, pi, :],
                                                              start=(pi == 0), stop=(pi == 31))
                                        mm.then_inc(pe.h, 1)
                                pe.bump(8)

                                @block.scalar
                                def _(s, og=og, peB_=peB_, veB_=veB_, ve_t2=list(ve_t2)):
                                    for r in range(8):
                                        for j in range(4):
                                            s.wait_ge(pe.h, peB_ + r + 1)
                                            if r * 4 + j >= 2:
                                                pu = r * 4 + j - 2
                                                s.wait_ge(ve.h, ve_t2[pu])
                                            s.copy(out=rsw[0:64, j % 2, :],
                                                   in_=psb[j][64:128, 0:512])
                                            s.copy(out=rsw[64:128, j % 2, :],
                                                   in_=psb[j][0:64, 0:512]).then_inc(ac.h, 1)
                                ac.bump(32)

                                @block.vector
                                def _(v, Wi=Wi, og=og, peB_=peB_, acB_=acB_, veB_=veB_,
                                      spqB_=spqB_, ve_unit=list(ve_unit), ld_rope=ld_rope):
                                    if Wi == 0 and og == 0:
                                        v.wait_ge(ld.h, ld_rope)
                                    for r in range(8):
                                        ts = slice(r * 512, (r + 1) * 512)
                                        for j in range(4):
                                            u = 4 * r + j
                                            v.wait_ge(pe.h, peB_ + r + 1)
                                            v.wait_ge(ac.h, acB_ + 4 * r + j + 1)
                                            if u >= 2:
                                                v.wait_ge(spq[u % 2].h,
                                                          spqB_[u % 2] + 16 * (u // 2))
                                            elif og > 0 or Wi > 0:
                                                v.wait_ge(spq[u % 2].h, spqB_[u % 2])
                                            if u >= 1:
                                                v.wait_ge(ve.h, ve_unit[u - 1])
                                            v.tensor_tensor(out=rt1[:], in0=psb[j][:, 0:512],
                                                            in1=ropec[:, ts],
                                                            op=ALU.mult).then_inc(ve.h, 1)
                                            v.tensor_tensor(out=rt2[:], in0=rsw[:, j % 2, :],
                                                            in1=ropes[:, ts],
                                                            op=ALU.mult).then_inc(ve.h, 1)
                                            v.wait_ge(ve.h, ve_unit[u] - 1 -
                                                      (2 if (Wi == 0 and text_rng(r)) else 0))
                                            v.tensor_tensor(out=robuf[:, u % 2, :], in0=rt1[:],
                                                            in1=rt2[:], op=ALU.add).then_inc(ve.h, 1)
                                            if Wi == 0 and text_rng(r) is not None:
                                                lo, hi = text_rng(r)
                                                v.wait_ge(ve.h, ve_unit[u] - 2)
                                                v.tensor_reduce(out=qtmp[:], in_=robuf[:, u % 2, lo:hi],
                                                                axis=mybir.AxisListType.X,
                                                                op=ALU.add).then_inc(ve.h, 1)
                                                v.wait_ge(ve.h, ve_unit[u] - 1)
                                                v.tensor_tensor(out=qbars[:, 4 * og + j:4 * og + j + 1],
                                                                in0=qbars[:, 4 * og + j:4 * og + j + 1],
                                                                in1=qtmp[:], op=ALU.add).then_inc(ve.h, 1)
                                ve.v = ve_tot_

                                @block.sync
                                def _(sync, Wi=Wi, og=og, ve_unit=list(ve_unit)):
                                    dst = q3d if Wi == 0 else k3d
                                    r = 7
                                    for j in range(4):
                                        u = 4 * r + j
                                        sync.wait_ge(ve.h, ve_unit[u])
                                        sync.dma_start(out=dst[:, og * 4 + j, r * 512:(r + 1) * 512],
                                                       in_=robuf[:, u % 2, :]
                                                       ).then_inc(spq[u % 2].h, 16)
                                        spq[u % 2].bump(16)
                            else:
                                @block.tensor
                                def _(t, og=og, kpan=kpan, peB_=peB_, acB_=acB_, ldfB_=ldfB_,
                                      ldfbB_=ldfbB_, gpv=gpv):
                                    t.wait_ge(gpv[0], gpv[1])
                                    for r in range(8):
                                        for q in range(4):
                                            sem = ldf4[q] if r % 2 == 0 else ldfb[q]
                                            base = ldfB_[q] if r % 2 == 0 else ldfbB_[q]
                                            t.wait_ge(sem.h, base + 16 * (r // 2 + 1))
                                        if r >= 1:
                                            t.wait_ge(ac.h, acB_ + 8 * (r - 1) + 7)
                                        for pi in range(32):
                                            fc = ford[pi]
                                            for j in range(4):
                                                mm = t.matmul(out=psb[j][:, 0:512],
                                                              lhsT=Wp[:, kpan % 2, fc, j * 128:(j + 1) * 128],
                                                              rhs=xsl[:, r % 2, pi, :],
                                                              start=(pi == 0), stop=(pi == 31))
                                        mm.then_inc(pe.h, 1)
                                        for j in range(4):
                                            t.wait_ge(ac.h, acB_ + 8 * r + 2 * j + 1)
                                            if r * 4 + j >= 2:
                                                pj = r * 4 + j - 2
                                                t.wait_ge(ac.h, acB_ + 8 * (pj // 4) + 2 * (pj % 4) + 2)
                                            for k in range(4):
                                                mm = t.transpose(
                                                    out=psT[:, (j % 2) * 512 + k * 128:(j % 2) * 512 + (k + 1) * 128],
                                                    in_=vev[:, j % 2, k * 128:(k + 1) * 128],
                                                    identity=id_bf[:])
                                            mm.then_inc(pe.h, 1)
                                pe.bump(40)

                                @block.scalar
                                def _(s, og=og, peB_=peB_, spqB_=spqB_):
                                    for r in range(8):
                                        for j in range(4):
                                            s.wait_ge(pe.h, peB_ + 5 * r + 1)
                                            if r * 4 + j >= 2:
                                                s.wait_ge(pe.h, peB_ + 5 * r + j)
                                            s.copy(out=vev[:, j % 2, :],
                                                   in_=psb[j][:, 0:512]).then_inc(ac.h, 1)
                                            if og > 0 and r == 0 and j == 0:
                                                s.wait_ge(spq[(og - 1) % 2].h, spqB_[(og - 1) % 2])
                                            s.wait_ge(pe.h, peB_ + 5 * r + 1 + (j + 1))
                                            s.copy(out=vtm[:, j, 4 * r:4 * r + 4, :],
                                                   in_=psT[:, (j % 2) * 512:(j % 2) * 512 + 512]
                                                   ).then_inc(ac.h, 1)
                                ac.bump(64)

                                @block.sync
                                def _(sync, og=og, acE=ac.v):
                                    sync.wait_ge(ac.h, acE)
                                    sync.dma_start(out=vsp[og * 4:(og + 1) * 4].rearrange("h p tt d -> p h tt d"),
                                                   in_=vtm[:]).then_inc(spq[og % 2].h, 16)
                                spq[og % 2].bump(16)

                    peI, acI, gpI = pe.v, ac.v, gp.v
                    kvqB = [kvq[0].v, kvq[1].v]
                    spq_fin = [spq[0].v, spq[1].v]

                    @block.gpsimd
                    def _(g):
                        g.wait_ge(spq[0].h, spq_fin[0])
                        g.wait_ge(spq[1].h, spq_fin[1])
                        for h in range(16):
                            if h >= 2:
                                g.wait_ge(pe.h, peI + h - 1)
                            g.dma_start(out=kv32[:, h % 2, :],
                                        in_=ksp[128 * h:128 * (h + 1), 1:513]
                                        ).then_inc(kvq[h % 2].h, 16)
                            kvq[h % 2].bump(16)

                    @block.tensor
                    def _(t):
                        for h in range(16):
                            t.wait_ge(kvq[h % 2].h, kvqB[h % 2] + 16 * (h // 2 + 1))
                            t.matmul(out=psb[6][0:1, 0:512],
                                     lhsT=qbars[:, h:h + 1], rhs=kv32[:, h % 2, :],
                                     start=(h == 0), stop=(h == 15)).then_inc(pe.h, 1)
                    pe.bump(16)

                    end_phase({"gpsimd": [(spq[0].h, spq[0].v), (spq[1].h, spq[1].v),
                                          (kvq[0].h, kvq[0].v), (kvq[1].h, kvq[1].v)],
                               "vector": [(ve.h, ve.v)], "tensor": [(pe.h, pe.v)],
                               "scalar": [(ac.h, ac.v)],
                               "sync": [(s_.h, s_.v) for s_ in ldf4] +
                                       [(s_.h, s_.v) for s_ in ldfb] + [(ld.h, ld.v)]})

                with ExitStack() as ph15:
                    impw = mk(ph15, "ab_impw", [128, 560], F32)
                    ext = mk(ph15, "ab_ext", [1, 640], F32)
                    irow = mk(ph15, "ab_irow", [1, 512], F32)
                    iotab = mk(ph15, "ab_iotab", [128, 512], F32)
                    impg_sb = mk(ph15, "ab_impg", [1, 1], F32)
                    w1 = mk(ph15, "ab_w1", [128, 512], F32)
                    w2 = mk(ph15, "ab_w2", [128, 512], F32)
                    w3 = mk(ph15, "ab_w3", [128, 512], F32)
                    iota_r = mk(ph15, "ab_iotar", [1, 512], F32)
                    iota_c = mk(ph15, "ab_iotac", [128, 1], F32)

                    peM, acM, veM, gpM, ccM = pe.v, ac.v, ve.v, gp.v, cc.v

                    @block.sync
                    def _(sync):
                        sync.dma_start(out=iota_r[:], in_=iota512_d).then_inc(ld.h, 16)
                        sync.dma_start(out=iota_c[:], in_=iotac_d).then_inc(ld.h, 16)
                        sync.dma_start(out=impg_sb[:], in_=impg_d).then_inc(ld.h, 16)
                    ld.bump(48)

                    @block.scalar
                    def _(s):
                        s.copy(out=irow[:], in_=psb[6][0:1, 0:512]).then_inc(ac.h, 1)
                    ac.bump(1)

                    @block.vector
                    def _(v):
                        v.wait_ge(ac.h, acM + 1)
                        v.wait_ge(ld.h, ld.v)
                        v.tensor_scalar(out=irow[:], in0=irow[:], scalar1=impg_sb[0:1, 0:1],
                                        scalar2=None, op0=ALU.mult).then_inc(ve.h, 1)
                    ve.bump(1)

                    @block.gpsimd
                    def _(g):
                        g.wait_ge(ve.h, veM + 1)
                        g.dma_start(out=ari_imp, in_=irow[:]).then_inc(gp.h, 16)
                        g.wait_ge(gp.h, gpM + 16)
                        g.collective_compute("AllReduce", ALU.add, replica_groups=WORLD,
                                             ins=[ari_imp.opt()], outs=[aro_imp.opt()]).then_inc(cc.h, 1)
                        g.wait_ge(cc.h, ccM + 1)
                        g.dma_start(out=irow[:], in_=aro_imp).then_inc(gp.h, 16)
                    gp.bump(32); cc.bump(1)

                    @block.tensor
                    def _(t):
                        t.wait_ge(gp.h, gpM + 32)
                        t.matmul(out=psb[0][:, 0:512], lhsT=ones_row[:], rhs=irow[:],
                                 start=True, stop=True).then_inc(pe.h, 1)
                        t.matmul(out=psb[1][:, 0:512], lhsT=ones_row[:], rhs=iota_r[:],
                                 start=True, stop=True).then_inc(pe.h, 1)
                        for g4 in range(4):
                            t.wait_ge(ac.h, (acM + 1) if g4 == 0 else (acM + 3 + g4))
                            t.transpose(out=psb[2][0:128, 0:1],
                                        in_=irow[0:1, 128 * g4:128 * (g4 + 1)],
                                        identity=id_f32[0:1, 0:1]).then_inc(pe.h, 1)
                    pe.bump(6)

                    @block.scalar
                    def _(s):
                        s.wait_ge(pe.h, peM + 1)
                        s.copy(out=impw[:, 0:512], in_=psb[0][:, 0:512]).then_inc(ac.h, 1)
                        s.wait_ge(pe.h, peM + 2)
                        s.copy(out=iotab[:], in_=psb[1][:, 0:512]).then_inc(ac.h, 1)
                        for g4 in range(4):
                            s.wait_ge(pe.h, peM + 3 + g4)
                            s.copy(out=impw[:, 512 + g4:513 + g4],
                                   in_=psb[2][0:128, 0:1]).then_inc(ac.h, 1)
                    ac.bump(6)

                    @block.vector
                    def _(v):
                        k = veM + 1
                        v.wait_ge(ac.h, acM + 7)
                        for g4 in range(4):
                            v.tensor_scalar(out=impw[:, 516 + g4:517 + g4], in0=iota_c[:],
                                            scalar1=float(128 * g4), scalar2=None,
                                            op0=ALU.add).then_inc(ve.h, 1); k += 1
                        for g4 in range(4):
                            v.wait_ge(ve.h, k)
                            v.tensor_scalar(out=w1[:], in0=impw[:, 0:512],
                                            scalar1=impw[:, 512 + g4:513 + g4], scalar2=None,
                                            op0=ALU.is_gt).then_inc(ve.h, 1); k += 1
                            v.wait_ge(ve.h, k)
                            v.tensor_reduce(out=impw[:, 524:525], in_=w1[:],
                                            axis=mybir.AxisListType.X, op=ALU.add).then_inc(ve.h, 1); k += 1
                            v.wait_ge(ve.h, k)
                            v.tensor_scalar(out=w2[:], in0=impw[:, 0:512],
                                            scalar1=impw[:, 512 + g4:513 + g4], scalar2=None,
                                            op0=ALU.is_equal).then_inc(ve.h, 1); k += 1
                            v.wait_ge(ve.h, k)
                            v.tensor_scalar(out=w3[:], in0=iotab[:],
                                            scalar1=impw[:, 516 + g4:517 + g4], scalar2=None,
                                            op0=ALU.is_lt).then_inc(ve.h, 1); k += 1
                            v.wait_ge(ve.h, k)
                            v.tensor_tensor(out=w2[:], in0=w2[:], in1=w3[:],
                                            op=ALU.mult).then_inc(ve.h, 1); k += 1
                            v.wait_ge(ve.h, k)
                            v.tensor_reduce(out=impw[:, 525:526], in_=w2[:],
                                            axis=mybir.AxisListType.X, op=ALU.add).then_inc(ve.h, 1); k += 1
                            v.wait_ge(ve.h, k)
                            v.tensor_tensor(out=impw[:, 526:527], in0=impw[:, 524:525],
                                            in1=impw[:, 525:526], op=ALU.add).then_inc(ve.h, 1); k += 1
                            v.wait_ge(ve.h, k)
                            v.tensor_scalar(out=impw[:, 520 + g4:521 + g4], in0=impw[:, 526:527],
                                            scalar1=255.5, scalar2=-1.0e30,
                                            op0=ALU.is_gt, op1=ALU.mult).then_inc(ve.h, 1); k += 1
                        v.memset(ext[:], 0.0)
                        v.memset(visb[:], -30.0).then_inc(ve.h, 1); k += 1
                    ve.bump(37)

                    peM2, acM2, veM2 = pe.v, ac.v, ve.v

                    @block.tensor
                    def _(t):
                        t.wait_ge(ve.h, veM2)
                        for g4 in range(4):
                            if g4 >= 1:
                                t.wait_ge(ac.h, acM2 + g4)
                            t.transpose(out=psb[3][0:1, 0:128],
                                        in_=impw[:, 520 + g4:521 + g4],
                                        identity=id_f32[:]).then_inc(pe.h, 1)
                        for tt in range(5):
                            t.wait_ge(ac.h, acM2 + 4 + tt)
                            t.transpose(out=psb[3][0:128, 0:1],
                                        in_=ext[0:1, 128 * tt:128 * (tt + 1)],
                                        identity=id_f32[0:1, 0:1]).then_inc(pe.h, 1)
                    pe.bump(9)

                    @block.scalar
                    def _(s):
                        for g4 in range(4):
                            s.wait_ge(pe.h, peM2 + g4 + 1)
                            s.copy(out=ext[0:1, 1 + 128 * g4:1 + 128 * (g4 + 1)],
                                   in_=psb[3][0:1, 0:128]).then_inc(ac.h, 1)
                        for tt in range(5):
                            s.wait_ge(pe.h, peM2 + 4 + tt + 1)
                            s.copy(out=visb[:, tt:tt + 1],
                                   in_=psb[3][0:128, 0:1]).then_inc(ac.h, 1)
                    ac.bump(9)

                    @block.vector
                    def _(v):
                        v.wait_ge(ac.h, acM2 + 9)
                        v.tensor_scalar(out=visb[:, 0:5], in0=visb[:, 0:5],
                                        scalar1=-30.0, scalar2=None, op0=ALU.add).then_inc(ve.h, 1)
                    ve.bump(1)

                    end_phase({"vector": [(ve.h, ve.v)], "tensor": [(pe.h, pe.v)],
                               "scalar": [(ac.h, ac.v)], "gpsimd": [(gp.h, gp.v)],
                               "sync": [(ld.h, ld.v)]})

                with ExitStack() as ph2:
                    cbias_sb = mk(ph2, "ab_cb", [128, 4, 512], F32)
                    qh = mk(ph2, "ab_qh", [128, 2, T], BF16)
                    kh = mk(ph2, "ab_kh", [128, 2, T], BF16)
                    vh = mk(ph2, "ab_vh", [128, 2, 32, 128], BF16)
                    Eb = mk(ph2, "ab_E", [128, 2, 512], BF16)
                    s2 = mk(ph2, "ab_s2", [1, 512], F32)
                    s2r = mk(ph2, "ab_s2r", [1, 512], F32)
                    ibc = mk(ph2, "ab_ibc", [128, 512], F32)
                    ob = mk(ph2, "ab_ob", [128, 2, T], BF16)

                    @block.sync
                    def _(sync):
                        sync.dma_start(out=cbias_sb[:], in_=cbias_d.rearrange("d p j -> p d j")
                                       ).then_inc(ld.h, 16)
                    ld.bump(16)

                    head_tots = []
                    prev_head_plan = None
                    pending_ags = []
                    for h in range(16):
                        peH, acH, veH = pe.v, ac.v, ve.v
                        ldqB = [ldq[0].v, ldq[1].v]
                        ldxB = [ldx[0].v, ldx[1].v]
                        ldaB = [lda[0].v, lda[1].v]

                        @block.sync
                        def _(sync, h=h, hts=list(head_tots), prev=prev_head_plan):
                            if h >= 2:
                                sync.wait_ge(pe.h, hts[h - 2])
                            if h == 0:
                                sync.wait_ge(ld.h, ld.v)
                            if prev is not None:
                                pve, ph_ = prev
                                for t8 in range(8):
                                    sync.wait_ge(ve.h, pve[t8])
                                    sync.dma_start(out=ob3d[:, ph_, t8 * 512:(t8 + 1) * 512],
                                                   in_=ob[:, ph_ % 2, t8 * 512:(t8 + 1) * 512]
                                                   ).then_inc(spq[ph_ % 2].h, 16)
                                    spq[ph_ % 2].bump(16)
                            sync.dma_start(out=qh[:, h % 2, :],
                                           in_=q3d[:, h, :]).then_inc(ldq[h % 2].h, 16)
                            ldq[h % 2].bump(16)
                            sync.dma_start(out=kh[:, h % 2, :],
                                           in_=k3d[:, h, :]).then_inc(ldx[h % 2].h, 16)
                            ldx[h % 2].bump(16)
                            sync.dma_start(out=vh[:, h % 2, :, :],
                                           in_=vsp[h].rearrange("p tt d -> p tt d")).then_inc(lda[h % 2].h, 16)
                            lda[h % 2].bump(16)

                        ntks_b = [4 * (t8 + 1) for t8 in range(8)]
                        plan = []
                        for t8 in range(8):
                            ntk = ntks_b[t8]
                            seq = [("S", 0), ("S", 1)]
                            for k in range(2, ntk):
                                seq.append(("PVS", k - 2))
                                seq.append(("S", k))
                            seq.append(("PVS", ntk - 2))
                            seq.append(("PVS", ntk - 1))
                            seq.append(("BC", 0))
                            plan.append(seq)
                        pe_pos = {}
                        pc = pe.v
                        for t8, seq in enumerate(plan):
                            for ent in seq:
                                pc += 1
                                pe_pos[(t8,) + ent] = pc
                        pe_tot = pc
                        ac_pos = {}
                        a = ac.v
                        for t8 in range(8):
                            for tk in range(ntks_b[t8]):
                                a += 1
                                ac_pos[(t8, "e", tk)] = a
                            a += 1
                            ac_pos[(t8, "sum")] = a
                            a += 1
                            ac_pos[(t8, "ibc")] = a
                        ac_tot = a
                        ve_pos = {}
                        w = ve.v
                        for t8 in range(8):
                            for d4 in range(4):
                                w += 1
                                ve_pos[(t8, "cb", d4)] = w
                            w += 1
                            ve_pos[(t8, "rec")] = w
                            w += 1
                            ve_pos[(t8, "oev")] = w
                        ve_tot = w

                        @block.tensor
                        def _(t, h=h, peH=peH, acH=acH, veH=veH, ldqB=ldqB, ldxB=ldxB,
                              ldaB=ldaB, pe_pos=pe_pos, ac_pos=ac_pos, ve_pos=ve_pos,
                              ntks_b=ntks_b, plan=plan):
                            t.wait_ge(ldq[h % 2].h, ldqB[h % 2] + 16)
                            t.wait_ge(ldx[h % 2].h, ldxB[h % 2] + 16)
                            t.wait_ge(lda[h % 2].h, ldaB[h % 2] + 16)
                            if h == 0:
                                t.wait_ge(ve.h, veH)
                            for t8 in range(8):
                                ntk = ntks_b[t8]
                                for ent in plan[t8]:
                                    kind, idx = ent
                                    if kind == "S":
                                        tk = idx
                                        sb = psb[2] if tk % 2 == 0 else psb[3]
                                        if tk <= 1:
                                            if t8 > 0:
                                                t.wait_ge(ac.h, ac_pos[(t8 - 1, "ibc")])
                                            elif h > 0:
                                                t.wait_ge(ac.h, acH)
                                        t.matmul(out=sb[:, 0:512],
                                                 lhsT=kh[:, h % 2, tk * 128:(tk + 1) * 128],
                                                 rhs=qh[:, h % 2, t8 * 512:(t8 + 1) * 512],
                                                 start=True, stop=True).then_inc(pe.h, 1)
                                    elif kind == "PVS":
                                        tk = idx
                                        t.wait_ge(ac.h, ac_pos[(t8, "e", tk)])
                                        if tk == 0:
                                            if t8 > 0:
                                                t.wait_ge(ve.h, ve_pos[(t8 - 1, "oev")])
                                            elif h > 0:
                                                t.wait_ge(ve.h, veH)
                                        t.matmul(out=psb[0][:, 0:512],
                                                 lhsT=vh[:, h % 2, tk, :],
                                                 rhs=Eb[:, tk % 2, :],
                                                 start=(tk == 0), stop=(tk == ntk - 1))
                                        mm = t.matmul(out=psb[1][0:1, 0:512],
                                                      lhsT=ones_bf[:, 0:1],
                                                      rhs=Eb[:, tk % 2, :],
                                                      start=(tk == 0), stop=(tk == ntk - 1))
                                        mm.then_inc(pe.h, 1)
                                    else:
                                        t.wait_ge(ve.h, ve_pos[(t8, "rec")])
                                        t.matmul(out=psb[2][:, 0:512], lhsT=ones_row[:],
                                                 rhs=s2r[:], start=True, stop=True).then_inc(pe.h, 1)

                        @block.scalar
                        def _(s, h=h, acH=acH, pe_pos=pe_pos, ac_pos=ac_pos,
                              ve_pos=ve_pos, ntks_b=ntks_b):
                            for t8 in range(8):
                                ntk = ntks_b[t8]
                                for tk in range(ntk):
                                    s.wait_ge(pe.h, pe_pos[(t8, "S", tk)])
                                    if tk >= ntk - 4:
                                        s.wait_ge(ve.h, ve_pos[(t8, "cb", tk - (ntk - 4))])
                                    if tk >= 2:
                                        s.wait_ge(pe.h, pe_pos[(t8, "PVS", tk - 2)])
                                    sb = psb[2] if tk % 2 == 0 else psb[3]
                                    vcolix = tk if tk < 5 else 32
                                    s.activation(out=Eb[:, tk % 2, :], in_=sb[:, 0:512],
                                                 func=AF.Exp, scale=scD[:, 0:1],
                                                 bias=visb[:, vcolix:vcolix + 1]).then_inc(ac.h, 1)
                                s.wait_ge(pe.h, pe_pos[(t8, "PVS", ntk - 1)])
                                s.copy(out=s2[:], in_=psb[1][0:1, 0:512]).then_inc(ac.h, 1)
                                s.wait_ge(pe.h, pe_pos[(t8, "BC", 0)])
                                s.copy(out=ibc[:], in_=psb[2][:, 0:512]).then_inc(ac.h, 1)

                        spqB2 = [spq[0].v, spq[1].v]

                        @block.vector
                        def _(v, h=h, pe_pos=pe_pos, ac_pos=ac_pos, ve_pos=ve_pos,
                              ntks_b=ntks_b, spqB2=spqB2):
                            if h >= 2:
                                v.wait_ge(spq[h % 2].h, spqB2[h % 2])
                            for t8 in range(8):
                                ntk = ntks_b[t8]
                                for d4 in range(4):
                                    tk = ntk - 4 + d4
                                    sb = psb[2] if tk % 2 == 0 else psb[3]
                                    v.wait_ge(pe.h, pe_pos[(t8, "S", tk)])
                                    v.tensor_tensor(out=sb[:, 0:512], in0=sb[:, 0:512],
                                                    in1=cbias_sb[:, d4, :],
                                                    op=ALU.add).then_inc(ve.h, 1)
                                v.wait_ge(ac.h, ac_pos[(t8, "sum")])
                                v.reciprocal(out=s2r[:], in_=s2[:]).then_inc(ve.h, 1)
                                v.wait_ge(ac.h, ac_pos[(t8, "ibc")])
                                v.tensor_tensor(out=ob[:, h % 2, t8 * 512:(t8 + 1) * 512],
                                                in0=psb[0][:, 0:512], in1=ibc[:],
                                                op=ALU.mult).then_inc(ve.h, 1)
                        pe.v = pe_tot
                        ac.v = ac_tot
                        ve.v = ve_tot
                        head_tots.append(pe_tot)

                        prev_head_plan = ([ve_pos[(t8, "oev")] for t8 in range(8)], h)
                        if h % 4 == 3:
                            pending_ags.append(h // 4)

                    @block.sync
                    def _(sync, prev=prev_head_plan):
                        pve, ph_ = prev
                        for t8 in range(8):
                            sync.wait_ge(ve.h, pve[t8])
                            sync.dma_start(out=ob3d[:, ph_, t8 * 512:(t8 + 1) * 512],
                                           in_=ob[:, ph_ % 2, t8 * 512:(t8 + 1) * 512]
                                           ).then_inc(spq[ph_ % 2].h, 16)
                            spq[ph_ % 2].bump(16)

                    @block.gpsimd
                    def _(g, s0=spq[0].v, s1=spq[1].v):
                        for qq in range(4):
                            g.wait_ge(spq[0].h, s0 - 256 * (3 - qq))
                            g.wait_ge(spq[1].h, s1 - 256 * (3 - qq))
                            g.collective_compute(
                                "AllGather", ALU.bypass, replica_groups=PAIRS,
                                ins=[obsp[512 * qq:512 * (qq + 1), :].opt()],
                                outs=[agB[1024 * qq:1024 * (qq + 1), :].opt()]).then_inc(cc.h, 1)
                    cc.bump(4)

                    end_phase({"gpsimd": [(spq[0].h, spq[0].v), (spq[1].h, spq[1].v)],
                               "vector": [(ve.h, ve.v)], "tensor": [(pe.h, pe.v)],
                               "scalar": [(ac.h, ac.v)],
                               "sync": [(ldq[0].h, ldq[0].v), (ldq[1].h, ldq[1].v),
                                        (ldx[0].h, ldx[0].v), (ldx[1].h, ldx[1].v),
                                        (lda[0].h, lda[0].v), (lda[1].h, lda[1].v)]})

        emit_ln_statspass(x_in)
        emit_ln_finish(x_in)
        emit_attn_a(x_in)
        if BUILD_SUBLAYERS >= 2:
            emit_ln_finish(resid, with_mean=True, do_ag=True)
            emit_ffn(cc.v - 4, [True], agA, worgT, resid)
        if BUILD_SUBLAYERS >= 3:
            emit_ln_finish(resid, with_mean=False, do_ag=True)
            emit_attn_b(cc.v - 4)
            emit_ffn(cc.v - 4, [True], agB, owT_d, resid,
                     with_lora=False, do_stats=True)
        if BUILD_SUBLAYERS >= 4:
            emit_ln_finish(resid, with_mean=True, do_ag=True)
            emit_ffn(cc.v - 4, [True], agA, worgT, out,
                     with_lora=True, do_stats=False)

        @block.sync
        def _(sync):
            if BUILD_SUBLAYERS < 4:
                sync.dma_start(out=out, in_=resid).then_inc(st.h, 16)
                st.bump(16)
            sync.dma_start(out=dbg, in_=xn_dram).then_inc(st.h, 16)
            st.bump(16)
            sync.wait_ge(st.h, st.v)

    return nc


def kernel(L0_attn_a_x, L0_attn_a_bv_q, L0_attn_a_bv_k, L0_attn_a_bv_v,
           L0_attn_b_q_w, L0_attn_b_k_w, L0_attn_b_v_w, L0_attn_b_o_w,
           L0_ffn_W_org, L0_ffn_lora_down, L0_ffn_lora_up, L0_ffn_lora_keys,
           L0_ffn_Wq):
    x = np.asarray(L0_attn_a_x)
    bq = np.asarray(L0_attn_a_bv_q); bk = np.asarray(L0_attn_a_bv_k)
    bv = np.asarray(L0_attn_a_bv_v)
    W_org = np.asarray(L0_ffn_W_org)
    lora_down = np.asarray(L0_ffn_lora_down); lora_up = np.asarray(L0_ffn_lora_up)
    lora_keys = np.asarray(L0_ffn_lora_keys); Wq = np.asarray(L0_ffn_Wq)
    worgT_full = np.ascontiguousarray(W_org.T)
    down_cat = np.ascontiguousarray(lora_down.transpose(1, 0, 2).reshape(DM, 512))
    up_cat = np.ascontiguousarray(lora_up.reshape(512, DM))
    wqT_full = np.ascontiguousarray(Wq.T)
    keysT = np.ascontiguousarray(lora_keys.T)
    sel = np.zeros((4, 8, 128), np.float32)
    for g4 in range(4):
        sel[g4, 2 * g4, 0:64] = 1.0
        sel[g4, 2 * g4 + 1, 64:128] = 1.0
    q_w = np.asarray(L0_attn_b_q_w); k_w = np.asarray(L0_attn_b_k_w)
    v_w = np.asarray(L0_attn_b_v_w); o_w = np.asarray(L0_attn_b_o_w)
    qwT = np.ascontiguousarray(q_w.T); kwT = np.ascontiguousarray(k_w.T)
    vwT = np.ascontiguousarray(v_w.T); owT = np.ascontiguousarray(o_w.T)
    dd = np.arange(128)
    invf = (10000.0 ** (-(dd % 64).astype(np.float64) / 64.0))
    tt_ = np.arange(4096, dtype=np.float64)
    ang = invf[:, None] * tt_[None, :]
    ropec_h = np.cos(ang).astype(ml_dtypes.bfloat16)
    sgn_ = np.where(dd < 64, -1.0, 1.0)[:, None]
    ropes_h = (sgn_ * np.sin(ang)).astype(ml_dtypes.bfloat16)
    ppj, jjj = np.meshgrid(np.arange(128), np.arange(512), indexing="ij")
    cbias_h = np.stack([np.where(ppj + 128 * d <= jjj, 0.0, -1.0e30)
                        for d in range(4)]).astype(np.float32)
    iota512 = np.arange(512, dtype=np.float32)[None, :]
    iotac = np.arange(128, dtype=np.float32)[:, None]
    ident_bf = np.eye(128, dtype=ml_dtypes.bfloat16)
    ident_f = np.eye(128, dtype=np.float32)
    pp, jj = np.meshgrid(np.arange(128), np.arange(256), indexing="ij")
    amask = np.stack([(pp <= jj), (pp + 128 <= jj)]).astype(ml_dtypes.bfloat16)

    nc = build_nc()
    in_maps = []
    for c in range(8):
        b, z = c // 2, c % 2
        xi = np.ascontiguousarray(x[b].T[z * HALF:(z + 1) * HALF, :]).astype(np.float32)
        m = {
            "x_in": xi, "ident": ident_bf, "identf": ident_f, "amask": amask,
            "bvq": np.ascontiguousarray(bq[2 * z:2 * z + 2]).astype(np.float32),
            "bvk": np.ascontiguousarray(bk[2 * z:2 * z + 2]).astype(np.float32),
            "bvv": np.ascontiguousarray(bv[2 * z:2 * z + 2]).astype(np.float32),
            "worgT": np.ascontiguousarray(worgT_full[:, z * HALF:(z + 1) * HALF]).astype(np.float32),
            "down_d": down_cat.astype(np.float32),
            "up_d": np.ascontiguousarray(up_cat[:, z * HALF:(z + 1) * HALF]).astype(np.float32),
            "wqT_d": np.ascontiguousarray(wqT_full[z * HALF:(z + 1) * HALF, :]).astype(np.float32),
            "keysT_d": keysT.astype(np.float32),
            "sel_d": sel,
            "qwT_d": np.ascontiguousarray(qwT[:, z * HALF:(z + 1) * HALF]).astype(np.float32),
            "kwT_d": np.ascontiguousarray(kwT[:, z * HALF:(z + 1) * HALF]).astype(np.float32),
            "vwT_d": np.ascontiguousarray(vwT[:, z * HALF:(z + 1) * HALF]).astype(np.float32),
            "owT_d": np.ascontiguousarray(owT[:, z * HALF:(z + 1) * HALF]).astype(np.float32),
            "ropec_d": ropec_h, "ropes_d": ropes_h, "cbias_d": cbias_h,
            "iota512_d": iota512, "iotac_d": iotac,
            "impg_d": np.array([[1.0 if c < 2 else 0.0]], np.float32),
        }
        in_maps.append(m)
    res = run_bass_kernel_spmd(nc, in_maps, core_ids=list(range(8)))
    outs = []
    for b in range(4):
        fm = np.concatenate([res.results[2 * b]["out"], res.results[2 * b + 1]["out"]], axis=0)
        outs.append(fm.T)
    kernel._dbg = [r["dbg"] for r in res.results]
    return np.stack(outs, axis=0).astype(np.float32)


# version 04
# speedup vs baseline: 1.0662x; 1.0329x over previous
import math
import numpy as np
import ml_dtypes
import concourse.bass as bass
import concourse.mybir as mybir
from concourse.bass_utils import run_bass_kernel_spmd
from concourse.alu_op_type import AluOpType as ALU
from contextlib import ExitStack

AF = mybir.ActivationFunctionType
F32 = mybir.dt.float32
BF16 = mybir.dt.bfloat16

DM = 4096
T = 4096
HALF = 2048
NFC = HALF // 128
NTT = T // 128
NTQ = 16
PAIRS = [[0, 1], [2, 3], [4, 5], [6, 7]]
WORLD = [[0, 1, 2, 3, 4, 5, 6, 7]]
EPS = 1e-5

BUILD_SUBLAYERS = 4


class Ctr:
    def __init__(self, handle):
        self.h = handle
        self.v = 0

    def bump(self, n=1):
        self.v += n
        return self.v


def _rows(ap, c0):
    return ap[c0 * 128:(c0 + 1) * 128, :]


def build_nc():
    nc = bass.Bass("TRN2", target_bir_lowering=False, debug=False)

    x_in = nc.dram_tensor("x_in", [HALF, T], F32, kind="ExternalInput").ap()
    ident = nc.dram_tensor("ident", [128, 128], BF16, kind="ExternalInput").ap()
    identf = nc.dram_tensor("identf", [128, 128], F32, kind="ExternalInput").ap()
    bvq = nc.dram_tensor("bvq", [2, 1024], F32, kind="ExternalInput").ap()
    bvk = nc.dram_tensor("bvk", [2, 1024], F32, kind="ExternalInput").ap()
    bvv = nc.dram_tensor("bvv", [2, 1024], F32, kind="ExternalInput").ap()
    amask_d = nc.dram_tensor("amask", [2, 128, 256], BF16, kind="ExternalInput").ap()
    worgT = nc.dram_tensor("worgT", [DM, HALF], F32, kind="ExternalInput").ap()
    down_d = nc.dram_tensor("down_d", [DM, 512], F32, kind="ExternalInput").ap()
    up_d = nc.dram_tensor("up_d", [512, HALF], F32, kind="ExternalInput").ap()
    wqT_d = nc.dram_tensor("wqT_d", [HALF, 128], F32, kind="ExternalInput").ap()
    keysT_d = nc.dram_tensor("keysT_d", [128, 8], F32, kind="ExternalInput").ap()
    sel_d = nc.dram_tensor("sel_d", [4, 8, 128], F32, kind="ExternalInput").ap()
    qwT_d = nc.dram_tensor("qwT_d", [DM, HALF], F32, kind="ExternalInput").ap()
    kwT_d = nc.dram_tensor("kwT_d", [DM, HALF], F32, kind="ExternalInput").ap()
    vwT_d = nc.dram_tensor("vwT_d", [DM, HALF], F32, kind="ExternalInput").ap()
    owT_d = nc.dram_tensor("owT_d", [DM, HALF], F32, kind="ExternalInput").ap()
    ropec_d = nc.dram_tensor("ropec_d", [128, T], BF16, kind="ExternalInput").ap()
    ropes_d = nc.dram_tensor("ropes_d", [128, T], BF16, kind="ExternalInput").ap()
    cbias_d = nc.dram_tensor("cbias_d", [4, 128, 512], F32, kind="ExternalInput").ap()
    iota512_d = nc.dram_tensor("iota512_d", [1, 512], F32, kind="ExternalInput").ap()
    iotac_d = nc.dram_tensor("iotac_d", [128, 1], F32, kind="ExternalInput").ap()
    impg_d = nc.dram_tensor("impg_d", [1, 1], F32, kind="ExternalInput").ap()
    out = nc.dram_tensor("out", [HALF, T], F32, kind="ExternalOutput").ap()
    dbg = nc.dram_tensor("dbg", [HALF, T], BF16, kind="ExternalOutput").ap()

    resid = nc.dram_tensor("resid", [HALF, T], F32).ap()
    xn_dram = nc.dram_tensor("xn_dram", [HALF, T], BF16).ap()
    arbuf_i = nc.dram_tensor("arbuf_i", [128, 64], F32).ap()
    arbuf_o = nc.dram_tensor("arbuf_o", [128, 64], F32).ap()
    rowbounce = nc.dram_tensor("rowbounce", [64, 128], F32).ap()
    bind_dram = nc.dram_tensor("bind_dram", [2, 2048], F32).ap()
    agA = nc.dram_tensor("agA", [DM, T], BF16).ap()
    arq_i = nc.dram_tensor("arq_i", [128, 1], F32).ap()
    arq_o = nc.dram_tensor("arq_o", [128, 1], F32).ap()
    agB = nc.dram_tensor("agB", [DM, T], BF16).ap()
    qsp = nc.dram_tensor("qsp", [HALF, T], BF16).ap()
    ksp = nc.dram_tensor("ksp", [HALF, T], BF16).ap()
    vsp = nc.dram_tensor("vsp", [16, 128, 32, 128], BF16).ap()
    obsp = nc.dram_tensor("obsp", [HALF, T], BF16).ap()
    ari_imp = nc.dram_tensor("ari_imp", [1, 512], F32).ap()
    aro_imp = nc.dram_tensor("aro_imp", [1, 512], F32).ap()

    with ExitStack() as outer:
        def SEM(name):
            return Ctr(outer.enter_context(nc.semaphore(name)))

        def SBO(name, shape, dt):
            return outer.enter_context(nc.sbuf_tensor(name, shape, dt))

        id_bf = SBO("id_bf", [128, 128], BF16)
        id_f32 = SBO("id_f32", [128, 128], F32)
        ones_bf = SBO("ones_bf", [128, 2], BF16)
        ones_row = SBO("ones_row", [1, 128], F32)
        eps_col = SBO("eps_col", [128, 1], F32)
        ones_c32 = SBO("ones_c32", [128, 1], F32)
        sc125 = SBO("sc125", [128, 1], F32)
        stat_s = SBO("stat_s", [128, 64], F32)
        stat_w = SBO("stat_w", [128, 224], F32)
        qkcol = SBO("qkcol", [128, 16], F32)
        mcol = SBO("mcol", [128, 16], F32)
        cvec = SBO("cvec", [128, 4], F32)
        scC = SBO("scC", [1, 2], F32)
        vcol = SBO("vcol", [128, 16], F32)
        bind_stack = ExitStack()
        bindw = bind_stack.enter_context(nc.sbuf_tensor("bindw", [2, 8192], F32))

        psb = [outer.enter_context(nc.psum_tensor(f"psb{i}", [128, 512], F32))
               for i in range(7)]
        psT = outer.enter_context(nc.psum_tensor("psT", [128, 1024], BF16))
        ps_stat = psb[6]

        pe = SEM("s_pe"); ve = SEM("s_ve"); ac = SEM("s_ac"); gp = SEM("s_gp")
        cc = SEM("s_cc"); st = SEM("s_st"); ld = SEM("s_ld"); bar = SEM("s_bar")
        lda = [SEM("s_lda0"), SEM("s_lda1")]
        ldq = [SEM("s_ldq0"), SEM("s_ldq1")]
        ldx = [SEM("s_ldx0"), SEM("s_ldx1")]
        stn = [SEM("s_stn0"), SEM("s_stn1")]
        stx = [SEM("s_stx0"), SEM("s_stx1")]

        _uid = [0]

        def mk(ph, nm, shape, dt):
            _uid[0] += 1
            return ph.enter_context(nc.sbuf_tensor(f"{nm}_{_uid[0]}", shape, dt))

        block = outer.enter_context(nc.Block())
        ENG = {"sync": block.sync, "tensor": block.tensor, "vector": block.vector,
               "scalar": block.scalar, "gpsimd": block.gpsimd}

        def end_phase(waits_by_engine):
            for name, wlist in waits_by_engine.items():
                dec = ENG[name]

                @dec
                def _(e, wlist=wlist):
                    for (h, v) in wlist:
                        e.wait_ge(h, v)
                    e.sem_inc(bar.h, 1)
                bar.bump(1)
            for name, dec in ENG.items():
                @dec
                def _(e):
                    e.wait_ge(bar.h, bar.v)

        @block.sync
        def _(sync):
            sync.dma_start(out=id_bf[:], in_=ident).then_inc(ld.h, 16)
            sync.dma_start(out=id_f32[:], in_=identf).then_inc(ld.h, 16)
            sync.dma_start(out=bindw[:, 0:1024], in_=bvq).then_inc(ld.h, 16)
            sync.dma_start(out=bindw[:, 1024:2048], in_=bvk).then_inc(ld.h, 16)
            sync.dma_start(out=bindw[:, 2048:3072], in_=bvv).then_inc(ld.h, 16)
        ld.bump(80)

        @block.vector
        def _(v):
            v.wait_ge(ld.h, ld.v)
            v.memset(ones_bf[:], 1.0)
            v.memset(ones_row[:], 1.0)
            v.memset(eps_col[:], EPS)
            v.memset(ones_c32[:], 1.0)
            v.memset(sc125[:], 0.125)
            v.memset(scC[0:1, 0:1], 1.0 / (4096.0 * math.sqrt(128.0)))
            v.memset(scC[0:1, 1:2], -1.0 / (4096.0 * math.sqrt(128.0))).then_inc(ve.h, 1)
            for i, base in enumerate((0, 1024, 2048)):
                v.tensor_reduce(out=bindw[:, 3072 + i:3073 + i], in_=bindw[:, base:base + 1024],
                                axis=mybir.AxisListType.X, op=ALU.add,
                                apply_absolute_value=True).then_inc(ve.h, 1)
        ve.bump(4)

        @block.scalar
        def _(s):
            s.wait_ge(ve.h, ve.v)
            s.sign(out=bindw[:, 4096:5120], in_=bindw[:, 0:1024])
            s.sign(out=bindw[:, 5120:6144], in_=bindw[:, 1024:2048])
            s.sign(out=bindw[:, 6144:7168], in_=bindw[:, 2048:3072]).then_inc(ac.h, 1)
        ac.bump(1)

        @block.vector
        def _(v):
            veB = ve.v
            v.wait_ge(ac.h, ac.v)
            v.tensor_scalar(out=bindw[:, 3072:3075], in0=bindw[:, 3072:3075],
                            scalar1=1.0 / 1024.0, scalar2=None, op0=ALU.mult).then_inc(ve.h, 1)
            k = 1
            for i in range(3):
                sgn = bindw[:, 4096 + 1024 * i:4096 + 1024 * (i + 1)]
                al = bindw[:, 3072 + i:3073 + i]
                v.wait_ge(ve.h, veB + k)
                v.tensor_scalar(out=sgn, in0=sgn, scalar1=al, scalar2=None,
                                op0=ALU.mult).then_inc(ve.h, 1); k += 1
                v.wait_ge(ve.h, veB + k)
                v.tensor_scalar(out=bindw[:, 7168:8192], in0=sgn, scalar1=0.0, scalar2=al,
                                op0=ALU.is_equal, op1=ALU.mult).then_inc(ve.h, 1); k += 1
                v.wait_ge(ve.h, veB + k)
                v.tensor_tensor(out=sgn, in0=sgn, in1=bindw[:, 7168:8192],
                                op=ALU.add).then_inc(ve.h, 1); k += 1
            v.wait_ge(ve.h, veB + k)
            v.tensor_tensor(out=bindw[:, 0:1024], in0=bindw[:, 4096:5120],
                            in1=bindw[:, 5120:6144], op=ALU.mult).then_inc(ve.h, 1)
        ve.bump(11)

        @block.tensor
        def _(t):
            peB, acB = pe.v, ac.v
            t.wait_ge(ve.h, ve.v)
            for dc in range(8):
                if dc >= 2:
                    t.wait_ge(ac.h, acB + (dc - 1))
                t.transpose(out=psb[dc % 2][0:128, 0:2],
                            in_=bindw[0:2, dc * 128:(dc + 1) * 128],
                            identity=id_f32[0:2, 0:2]).then_inc(pe.h, 1)
            for dc in range(8):
                if dc >= 2:
                    t.wait_ge(ac.h, acB + 8 + (dc - 1))
                t.transpose(out=psb[2 + dc % 2][0:128, 0:2],
                            in_=bindw[0:2, 6144 + dc * 128:6144 + (dc + 1) * 128],
                            identity=id_f32[0:2, 0:2]).then_inc(pe.h, 1)
        pe.bump(16)

        @block.scalar
        def _(s):
            peB, acB = pe.v - 16, ac.v
            for dc in range(8):
                s.wait_ge(pe.h, peB + dc + 1)
                s.copy(out=qkcol[:, 2 * dc:2 * dc + 2],
                       in_=psb[dc % 2][0:128, 0:2]).then_inc(ac.h, 1)
            for dc in range(8):
                s.wait_ge(pe.h, peB + 8 + dc + 1)
                s.copy(out=vcol[:, 2 * dc:2 * dc + 2],
                       in_=psb[2 + dc % 2][0:128, 0:2]).then_inc(ac.h, 1)
        ac.bump(16)

        end_phase({"scalar": [(ac.h, ac.v)], "vector": [(ve.h, ve.v)],
                   "tensor": [(pe.h, pe.v)], "sync": [(ld.h, ld.v)]})
        bind_stack.close()

        def emit_ln_statspass(src_dram):
            peB, veB = pe.v, ve.v
            ldaB = [lda[0].v, lda[1].v]
            with ExitStack() as ph:
                xch = mk(ph, "lns_xch", [128, 2, T], F32)
                xsq = mk(ph, "lns_xsq", [128, T], BF16)
                xbf = mk(ph, "lns_xbf", [128, T], BF16)

                @block.sync
                def _(sync):
                    for c in range(NFC):
                        if c >= 2:
                            sync.wait_ge(ve.h, veB + 2 * (c - 2) + 2)
                        sync.dma_start(out=xch[:, c % 2, :],
                                       in_=_rows(src_dram, c)).then_inc(lda[c % 2].h, 16)
                        lda[c % 2].bump(16)

                @block.vector
                def _(v):
                    for c in range(NFC):
                        v.wait_ge(lda[c % 2].h, ldaB[c % 2] + 16 * (c // 2 + 1))
                        if c >= 1:
                            v.wait_ge(pe.h, peB + c)
                        v.tensor_tensor(out=xsq[:], in0=xch[:, c % 2, :],
                                        in1=xch[:, c % 2, :], op=ALU.mult)
                        v.tensor_copy(out=xbf[:], in_=xch[:, c % 2, :]).then_inc(ve.h, 2)
                ve.bump(2 * NFC)

                @block.tensor
                def _(t):
                    for c in range(NFC):
                        t.wait_ge(ve.h, veB + 2 * c + 2)
                        for i in range(NTT):
                            first = (c == 0 and i == 0)
                            t.matmul(out=ps_stat[:, i:i + 1],
                                     lhsT=xbf[:, i * 128:(i + 1) * 128], rhs=ones_bf[:, 0:1],
                                     start=first, stop=False)
                            mm = t.matmul(out=ps_stat[:, 32 + i:33 + i],
                                          lhsT=xsq[:, i * 128:(i + 1) * 128], rhs=ones_bf[:, 0:1],
                                          start=False, stop=(c == NFC - 1 and i == NTT - 1))
                        mm.then_inc(pe.h, 1)
                pe.bump(NFC)
                end_phase({"tensor": [(pe.h, pe.v)], "vector": [(ve.h, ve.v)],
                           "sync": [(lda[0].h, lda[0].v), (lda[1].h, lda[1].v)]})

        def emit_ln_finish(src_dram, with_mean=False, do_ag=False):
            peB, veB, acB, gpB, ccB = pe.v, ve.v, ac.v, gp.v, cc.v
            with ExitStack() as ph:
                ab_row = mk(ph, "lnf_ab", [128, T], F32)
                rbT = mk(ph, "lnf_rbT", [64, 128], F32)
                arow_a = mk(ph, "lnf_ara", [1, T], F32)
                arow_b = mk(ph, "lnf_arb", [1, T], F32)
                bb_row = mk(ph, "lnf_bb", [128, T], F32)
                xch = mk(ph, "lnf_xch", [128, 2, T], F32)
                xt0 = mk(ph, "lnf_xt0", [128, T], F32)
                xnc = mk(ph, "lnf_xnc", [128, 2, T], BF16)

                @block.scalar
                def _(s):
                    s.copy(out=stat_s[:, 0:64], in_=ps_stat[:, 0:64]).then_inc(ac.h, 1)
                ac.bump(1)

                @block.gpsimd
                def _(g):
                    g.wait_ge(ac.h, acB + 1)
                    g.dma_start(out=arbuf_i, in_=stat_s[:, 0:64]).then_inc(gp.h, 16)
                    g.wait_ge(gp.h, gpB + 16)
                    g.collective_compute("AllReduce", ALU.add, replica_groups=PAIRS,
                                         ins=[arbuf_i.opt()], outs=[arbuf_o.opt()]).then_inc(cc.h, 1)
                    g.wait_ge(cc.h, ccB + 1)
                    g.dma_start(out=stat_s[:, 0:64], in_=arbuf_o).then_inc(gp.h, 16)
                gp.bump(32); cc.bump(1)

                @block.vector
                def _(v):
                    v.wait_ge(gp.h, gpB + 32)
                    v.tensor_scalar(out=stat_w[:, 0:32], in0=stat_s[:, 0:32],
                                    scalar1=1.0 / DM, scalar2=None, op0=ALU.mult)
                    v.tensor_scalar(out=stat_w[:, 32:64], in0=stat_s[:, 32:64],
                                    scalar1=1.0 / DM, scalar2=None, op0=ALU.mult).then_inc(ve.h, 1)
                    v.wait_ge(ve.h, veB + 1)
                    v.tensor_tensor(out=stat_w[:, 64:96], in0=stat_w[:, 0:32],
                                    in1=stat_w[:, 0:32], op=ALU.mult).then_inc(ve.h, 1)
                    v.wait_ge(ve.h, veB + 2)
                    v.tensor_tensor(out=stat_w[:, 96:128], in0=stat_w[:, 32:64],
                                    in1=stat_w[:, 64:96], op=ALU.subtract).then_inc(ve.h, 1)
                ve.bump(3)

                @block.scalar
                def _(s):
                    s.wait_ge(ve.h, veB + 3)
                    s.activation(out=stat_w[:, 128:160], in_=stat_w[:, 96:128],
                                 func=AF.Sqrt, bias=eps_col[:, 0:1], scale=1.0).then_inc(ac.h, 1)
                ac.bump(1)

                @block.vector
                def _(v):
                    v.wait_ge(ac.h, acB + 2)
                    v.reciprocal(out=stat_w[:, 160:192], in_=stat_w[:, 128:160]).then_inc(ve.h, 1)
                    v.wait_ge(ve.h, veB + 4)
                    v.scalar_tensor_tensor(out=stat_w[:, 192:224], in0=stat_w[:, 0:32],
                                           scalar=-1.0, in1=stat_w[:, 160:192],
                                           op0=ALU.mult, op1=ALU.mult).then_inc(ve.h, 1)
                ve.bump(2)

                @block.tensor
                def _(t):
                    t.wait_ge(ve.h, veB + 5)
                    t.transpose(out=psb[4][0:64, 0:128], in_=stat_w[:, 160:224],
                                identity=id_f32[:]).then_inc(pe.h, 1)
                pe.bump(1)

                @block.scalar
                def _(s):
                    s.wait_ge(pe.h, peB + 1)
                    s.copy(out=rbT[:], in_=psb[4][0:64, 0:128]).then_inc(ac.h, 1)
                ac.bump(1)

                @block.gpsimd
                def _(g):
                    g.wait_ge(ac.h, acB + 3)
                    g.dma_start(out=rowbounce, in_=rbT[:]).then_inc(gp.h, 16)
                    g.wait_ge(gp.h, gpB + 48)
                    g.dma_start(out=arow_a[:],
                                in_=rowbounce[0:32, :].rearrange("i p -> (i p)")[None, :]).then_inc(gp.h, 16)
                    g.dma_start(out=arow_b[:],
                                in_=rowbounce[32:64, :].rearrange("i p -> (i p)")[None, :]).then_inc(gp.h, 16)
                gp.bump(48)

                @block.tensor
                def _(t):
                    t.wait_ge(gp.h, gpB + 80)
                    for j in range(8):
                        if j >= 2:
                            t.wait_ge(ac.h, acB + 4 + (j - 2))
                        dst = psb[4] if j % 2 == 0 else psb[5]
                        t.matmul(out=dst[:, 0:512], lhsT=ones_row[:],
                                 rhs=arow_a[:, j * 512:(j + 1) * 512],
                                 start=True, stop=True).then_inc(pe.h, 1)
                pe.bump(8)

                @block.scalar
                def _(s):
                    for j in range(8):
                        s.wait_ge(pe.h, peB + 2 + j)
                        src = psb[4] if j % 2 == 0 else psb[5]
                        s.copy(out=ab_row[:, j * 512:(j + 1) * 512],
                               in_=src[:, 0:512]).then_inc(ac.h, 1)
                ac.bump(8)

                @block.tensor
                def _(t):
                    for j in range(8):
                        if j < 2:
                            t.wait_ge(ac.h, acB + 11)
                        else:
                            t.wait_ge(ac.h, acB + 12 + (j - 2))
                        dst = psb[4] if j % 2 == 0 else psb[5]
                        t.matmul(out=dst[:, 0:512], lhsT=ones_row[:],
                                 rhs=arow_b[:, j * 512:(j + 1) * 512],
                                 start=True, stop=True).then_inc(pe.h, 1)
                pe.bump(8)

                @block.scalar
                def _(s):
                    for j in range(8):
                        s.wait_ge(pe.h, peB + 10 + j)
                        src = psb[4] if j % 2 == 0 else psb[5]
                        s.copy(out=bb_row[:, j * 512:(j + 1) * 512],
                               in_=src[:, 0:512]).then_inc(ac.h, 1)
                ac.bump(8)

                ldaB = [lda[0].v, lda[1].v]
                stnB = [stn[0].v, stn[1].v]
                veB2 = veB + 5
                K2_ = 3 if with_mean else 2

                @block.sync
                def _(sync):
                    for c in range(NFC):
                        if c >= 2:
                            sync.wait_ge(ve.h, veB2 + K2_ * (c - 2) + K2_)
                        sync.dma_start(out=xch[:, c % 2, :],
                                       in_=_rows(src_dram, c)).then_inc(lda[c % 2].h, 16)
                        lda[c % 2].bump(16)

                K_ = 3 if with_mean else 2

                @block.vector
                def _(v):
                    for c in range(NFC):
                        v.wait_ge(lda[c % 2].h, ldaB[c % 2] + 16 * (c // 2 + 1))
                        if c == 0:
                            v.wait_ge(ac.h, acB + 11)
                        if c >= 2:
                            v.wait_ge(stn[c % 2].h, stnB[c % 2] + 16 * (c // 2))
                        if c >= 1:
                            v.wait_ge(ve.h, veB2 + K_ * (c - 1) + K_)
                        v.tensor_tensor(out=xt0[:], in0=xch[:, c % 2, :],
                                        in1=ab_row[:], op=ALU.mult).then_inc(ve.h, 1)
                        if c == 0:
                            v.wait_ge(ac.h, acB + 19)
                        v.wait_ge(ve.h, veB2 + K_ * c + 1)
                        v.tensor_tensor(out=xnc[:, c % 2, :], in0=xt0[:],
                                        in1=bb_row[:], op=ALU.add).then_inc(ve.h, 1)
                        if with_mean:
                            v.wait_ge(ve.h, veB2 + K_ * c + 2)
                            v.tensor_reduce(out=mcol[:, c:c + 1], in_=xnc[:, c % 2, :],
                                            axis=mybir.AxisListType.X,
                                            op=ALU.add).then_inc(ve.h, 1)
                ve.bump(K_ * NFC)

                stnB2 = [stn[0].v, stn[1].v]
                gpB2 = gp.v

                @block.gpsimd
                def _(g):
                    for c in range(NFC):
                        g.wait_ge(ve.h, veB2 + K2_ * c + 2)
                        g.dma_start(out=_rows(xn_dram, c),
                                    in_=xnc[:, c % 2, :]).then_inc(stn[c % 2].h, 16)
                        stn[c % 2].bump(16)
                    if do_ag:
                        for q in range(4):
                            g.wait_ge(stn[0].h, stnB2[0] + 16 * (2 * q + 2))
                            g.wait_ge(stn[1].h, stnB2[1] + 16 * (2 * q + 2))
                            g.collective_compute(
                                "AllGather", ALU.bypass, replica_groups=PAIRS,
                                ins=[xn_dram[512 * q:512 * (q + 1), :].opt()],
                                outs=[agA[1024 * q:1024 * (q + 1), :].opt()]).then_inc(cc.h, 1)
                if do_ag:
                    cc.bump(4)

                end_phase({"gpsimd": [(stn[0].h, stn[0].v), (stn[1].h, stn[1].v)],
                           "vector": [(ve.h, ve.v)], "tensor": [(pe.h, pe.v)],
                           "scalar": [(ac.h, ac.v)],
                           "sync": [(lda[0].h, lda[0].v), (lda[1].h, lda[1].v)]})

        def emit_attn_a(xsrc_dram):
            xn3d = xn_dram.rearrange("(c p) t -> p c t", p=128)
            xs3d = xsrc_dram.rearrange("(c p) t -> p c t", p=128)
            rs3d = resid.rearrange("(c p) t -> p c t", p=128)
            ntks = [2 * (tq + 1) for tq in range(NTQ)]

            with ExitStack() as ph:
                amask_sb = mk(ph, "aa_amask", [128, 2, 256], BF16)
                Ks = mk(ph, "aa_Ks", [128, 8, T], BF16)
                Vt = mk(ph, "aa_Vt", [128, 32, 1024], BF16)
                diag = mk(ph, "aa_diag", [128, 8, 128], BF16)
                raw = mk(ph, "aa_raw", [128, 2, T], BF16)
                qt = mk(ph, "aa_qt", [128, 2, 8, 256], BF16)
                sig = mk(ph, "aa_sig", [128, 2, 256], BF16)
                sigm = mk(ph, "aa_sigm", [128, 2, 256], BF16)
                xres = mk(ph, "aa_xres", [128, 2, 8, 256], F32)
                x2t = mk(ph, "aa_x2t", [128, 2, 256], F32)
                xb2 = mk(ph, "aa_xb2", [128, 2, 256], BF16)
                xq2 = mk(ph, "aa_xq2", [128, 2, 256], BF16)

                first_stats = [True]

                @block.sync
                def _(sync):
                    sync.dma_start(out=amask_sb[:],
                                   in_=amask_d.rearrange("d p j -> p d j")).then_inc(ld.h, 16)
                ld.bump(16)
                ld_amask = ld.v

                for g in range(2):
                    peB, veB, acB = pe.v, ve.v, ac.v
                    ldaB = [lda[0].v, lda[1].v]

                    @block.sync
                    def _(sync, g=g, peB=peB, acB=acB):
                        for dc in range(8):
                            if dc >= 2:
                                sync.wait_ge(pe.h, peB + 8 * (dc - 1))
                                sync.wait_ge(ac.h, acB + 9 * (dc - 1))
                            sync.dma_start(out=raw[:, dc % 2, :],
                                           in_=xn3d[:, 8 * g + dc, :]).then_inc(lda[dc % 2].h, 16)
                            lda[dc % 2].bump(16)

                    @block.vector
                    def _(v, g=g, peB=peB, veB=veB):
                        if g == 1:
                            v.wait_ge(pe.h, peB)
                        for dc in range(8):
                            v.tensor_scalar(out=diag[:, dc, :], in0=id_bf[:],
                                            scalar1=vcol[:, 2 * dc + g:2 * dc + g + 1],
                                            scalar2=None, op0=ALU.mult).then_inc(ve.h, 1)
                    ve.bump(8)

                    def _ks_ac(dc):
                        return acB + dc * 9 + 1

                    def _ev_ac(gi):
                        return acB + (gi // 8) * 9 + 1 + (gi % 8) + 1

                    @block.scalar
                    def _(s, g=g, peB=peB, acB=acB, ldaB=ldaB):
                        if g == 1:
                            s.wait_ge(pe.h, peB)
                        for dc in range(8):
                            s.wait_ge(lda[dc % 2].h, ldaB[dc % 2] + 16 * (dc // 2 + 1))
                            s.mul(out=Ks[:, dc, :], in_=raw[:, dc % 2, :],
                                  mul=qkcol[:, 2 * dc + g:2 * dc + g + 1]).then_inc(ac.h, 1)
                            for i4 in range(8):
                                gi = dc * 8 + i4
                                bank = psb[4] if gi % 2 == 0 else psb[5]
                                s.wait_ge(pe.h, peB + gi + 1)
                                s.copy(out=Vt[:, i4 * 4:(i4 + 1) * 4, dc * 128:(dc + 1) * 128],
                                       in_=bank[:, 0:512].rearrange("p (k c) -> p k c", k=4)
                                       ).then_inc(ac.h, 1)
                    ac.bump(72)

                    @block.tensor
                    def _(t, g=g, peB=peB, veB=veB, acB=acB, ldaB=ldaB):
                        for dc in range(8):
                            t.wait_ge(ve.h, veB + dc + 1)
                            t.wait_ge(lda[dc % 2].h, ldaB[dc % 2] + 16 * (dc // 2 + 1))
                            for i4 in range(8):
                                gi = dc * 8 + i4
                                bank = psb[4] if gi % 2 == 0 else psb[5]
                                if gi >= 2:
                                    t.wait_ge(ac.h, _ev_ac(gi - 2))
                                for k in range(4):
                                    i = i4 * 4 + k
                                    mm = t.matmul(out=bank[:, k * 128:(k + 1) * 128],
                                                  lhsT=raw[:, dc % 2, i * 128:(i + 1) * 128],
                                                  rhs=diag[:, dc, :],
                                                  start=(k == 0), stop=(k == 3))
                                mm.then_inc(pe.h, 1)
                    pe.bump(64)



                    peS, acS, veS = pe.v, ac.v, ve.v
                    ldqB = [ldq[0].v, ldq[1].v]
                    ldxB = [ldx[0].v, ldx[1].v]
                    stxB = [stx[0].v, stx[1].v]

                    plans = []
                    for tq in range(NTQ):
                        ntk = ntks[tq]
                        seq = [("S", 0), ("S", 1)]
                        for k in range(2, ntk):
                            seq.append(("PV", k - 2))
                            seq.append(("S", k))
                        seq.append(("PV", ntk - 2))
                        seq.append(("PV", ntk - 1))
                        seq += [("T", dc) for dc in range(8)]
                        plans.append(seq)
                    pe_pos = {}
                    pc = peS
                    for tq, seq in enumerate(plans):
                        for ent in seq:
                            pc += 1
                            pe_pos[(tq,) + ent] = pc
                    pe_total = pc
                    ac_pos = {}
                    a = acS
                    for tq in range(NTQ):
                        for tk in range(ntks[tq]):
                            a += 1
                            ac_pos[(tq, tk)] = a
                    ac_total = a
                    ve_pos = {}
                    w = veS
                    for tq in range(NTQ):
                        for j in range(2):
                            w += 1
                            ve_pos[(tq, "m", j)] = w
                        for dc in range(8):
                            w += 1
                            ve_pos[(tq, "a", dc)] = w
                            w += 1
                            ve_pos[(tq, "q", dc)] = w
                    ve_total = w

                    @block.sync
                    def _(sync, g=g, peS=peS, ldqB=ldqB, ldxB=ldxB):
                        if g == 1:
                            sync.wait_ge(pe.h, peS - 64)
                        for tq in range(NTQ):
                            if tq >= 2:
                                sync.wait_ge(pe.h, pe_pos[(tq - 2, "PV", ntks[tq - 2] - 1)])
                                sync.wait_ge(ve.h, ve_pos[(tq - 2, "q", 7)])
                            sync.dma_start(out=qt[:, tq % 2, :, :],
                                           in_=xn3d[:, 8 * g:8 * g + 8, tq * 256:(tq + 1) * 256]
                                           ).then_inc(ldq[tq % 2].h, 16)
                            ldq[tq % 2].bump(16)
                            sync.dma_start(out=xres[:, tq % 2, :, :],
                                           in_=xs3d[:, 8 * g:8 * g + 8, tq * 256:(tq + 1) * 256]
                                           ).then_inc(ldx[tq % 2].h, 16)
                            ldx[tq % 2].bump(16)

                    @block.tensor
                    def _(t, g=g, peS=peS, acS=acS, veS=veS, ldqB=ldqB):
                        t.wait_ge(ac.h, acS)
                        for tq in range(NTQ):
                            ntk = ntks[tq]
                            t.wait_ge(ldq[tq % 2].h, ldqB[tq % 2] + 16 * (tq // 2 + 1))
                            for ent in plans[tq]:
                                kind, idx = ent
                                if kind == "S":
                                    tk = idx
                                    sb = psb[4] if tk % 2 == 0 else psb[5]
                                    for dc in range(8):
                                        mm = t.matmul(out=sb[:, 0:256],
                                                      lhsT=Ks[:, dc, tk * 128:(tk + 1) * 128],
                                                      rhs=qt[:, tq % 2, dc, :],
                                                      start=(dc == 0), stop=(dc == 7))
                                    mm.then_inc(pe.h, 1)
                                elif kind == "PV":
                                    tk = idx
                                    masked = (tk >= ntk - 2)
                                    if tk == 0:
                                        if tq > 0:
                                            t.wait_ge(ve.h, ve_pos[(tq - 1, "q", 7)])
                                        elif g == 1:
                                            t.wait_ge(ve.h, veS)
                                    if masked:
                                        j = tk - (ntk - 2)
                                        t.wait_ge(ve.h, ve_pos[(tq, "m", j)])
                                        src = sigm[:, j, :]
                                    else:
                                        t.wait_ge(ac.h, ac_pos[(tq, tk)])
                                        src = sig[:, tk % 2, :]
                                    for dc in range(8):
                                        mm = t.matmul(
                                            out=psb[dc // 2][:, (dc % 2) * 256:(dc % 2) * 256 + 256],
                                            lhsT=Vt[:, tk, dc * 128:(dc + 1) * 128],
                                            rhs=src,
                                            start=(tk == 0 and dc % 2 == 0),
                                            stop=(tk == ntk - 1 and dc % 2 == 1))
                                    mm.then_inc(pe.h, 1)
                                else:
                                    dc = idx
                                    t.wait_ge(ve.h, ve_pos[(tq, "q", dc)])
                                    for ii in range(2):
                                        i = 2 * tq + ii
                                        fs = first_stats[0]
                                        t.matmul(out=ps_stat[:, i:i + 1],
                                                 lhsT=xb2[:, dc % 2, ii * 128:(ii + 1) * 128],
                                                 rhs=ones_bf[:, 0:1], start=fs, stop=False)
                                        first_stats[0] = False
                                        mm = t.matmul(out=ps_stat[:, 32 + i:33 + i],
                                                      lhsT=xq2[:, dc % 2, ii * 128:(ii + 1) * 128],
                                                      rhs=ones_bf[:, 0:1], start=False,
                                                      stop=(g == 1 and tq == NTQ - 1
                                                            and dc == 7 and ii == 1))
                                    mm.then_inc(pe.h, 1)

                    @block.scalar
                    def _(s, g=g, peS=peS, acS=acS, veS=veS):
                        for tq in range(NTQ):
                            ntk = ntks[tq]
                            for tk in range(ntk):
                                s.wait_ge(pe.h, pe_pos[(tq, "S", tk)])
                                if tk >= 2:
                                    s.wait_ge(pe.h, pe_pos[(tq, "PV", tk - 2)])
                                elif tq > 0:
                                    s.wait_ge(pe.h, pe_pos[(tq - 1, "PV", ntks[tq - 1] - 1)])
                                    s.wait_ge(ve.h, ve_pos[(tq - 1, "m", 1)])
                                elif g == 1:
                                    s.wait_ge(pe.h, peS - 64)
                                    s.wait_ge(ve.h, veS)
                                sb = psb[4] if tk % 2 == 0 else psb[5]
                                s.activation(out=sig[:, tk % 2, :], in_=sb[:, 0:256],
                                             func=AF.Sigmoid,
                                             scale=sc125[:, 0:1]).then_inc(ac.h, 1)

                    @block.vector
                    def _(v, g=g, peS=peS, veS=veS, ldxB=ldxB, stxB=stxB, ld_amask=ld_amask):
                        if g == 1:
                            v.wait_ge(pe.h, peS - 64)
                        for tq in range(NTQ):
                            ntk = ntks[tq]
                            for j in range(2):
                                tk = ntk - 2 + j
                                if g == 0 and tq == 0 and j == 0:
                                    v.wait_ge(ld.h, ld_amask)
                                v.wait_ge(ac.h, ac_pos[(tq, tk)])
                                v.tensor_tensor(out=sigm[:, j, :], in0=sig[:, tk % 2, :],
                                                in1=amask_sb[:, j, :], op=ALU.mult).then_inc(ve.h, 1)
                            v.wait_ge(pe.h, pe_pos[(tq, "PV", ntk - 1)])
                            v.wait_ge(ldx[tq % 2].h, ldxB[tq % 2] + 16 * (tq // 2 + 1))
                            for dc in range(8):
                                if tq > 0 or dc >= 2:
                                    ptq, pdc = (tq, dc - 2) if dc >= 2 else (tq - 1, dc + 6)
                                    v.wait_ge(pe.h, pe_pos[(ptq, "T", pdc)])
                                    v.wait_ge(stx[dc % 2].h,
                                              stxB[dc % 2] + 16 * ((tq * 8 + dc) // 2))
                                elif g == 1:
                                    v.wait_ge(stx[dc % 2].h, stxB[dc % 2])
                                v.tensor_tensor(out=x2t[:, dc % 2, :],
                                                in0=psb[dc // 2][:, (dc % 2) * 256:(dc % 2) * 256 + 256],
                                                in1=xres[:, tq % 2, dc, :], op=ALU.add).then_inc(ve.h, 1)
                                v.wait_ge(ve.h, ve_pos[(tq, "a", dc)])
                                v.tensor_copy(out=xb2[:, dc % 2, :], in_=x2t[:, dc % 2, :])
                                v.tensor_tensor(out=xq2[:, dc % 2, :], in0=x2t[:, dc % 2, :],
                                                in1=x2t[:, dc % 2, :], op=ALU.mult).then_inc(ve.h, 1)

                    @block.gpsimd
                    def _(g_, g_head=g, veS=veS, stxB=stxB):
                        for tq in range(NTQ):
                            for dc in range(8):
                                g_.wait_ge(ve.h, ve_pos[(tq, "a", dc)])
                                g_.dma_start(out=rs3d[:, 8 * g_head + dc, tq * 256:(tq + 1) * 256],
                                             in_=x2t[:, dc % 2, :]).then_inc(stx[dc % 2].h, 16)
                                stx[dc % 2].bump(16)

                    pe.v = pe_total
                    ac.v = ac_total
                    ve.v = ve_total

                end_phase({"gpsimd": [(stx[0].h, stx[0].v), (stx[1].h, stx[1].v)],
                           "vector": [(ve.h, ve.v)], "tensor": [(pe.h, pe.v)],
                           "scalar": [(ac.h, ac.v)],
                           "sync": [(ldq[0].h, ldq[0].v), (ldq[1].h, ldq[1].v),
                                    (ldx[0].h, ldx[0].v), (ldx[1].h, ldx[1].v),
                                    (lda[0].h, lda[0].v), (lda[1].h, lda[1].v)]})


        ldf4 = [SEM("s_ldf0"), SEM("s_ldf1"), SEM("s_ldf2"), SEM("s_ldf3")]
        gq = SEM("s_gq"); gw = SEM("s_gw")
        gpn = [SEM("s_gpn0"), SEM("s_gpn1")]

        def emit_ffn(ccB_ag, first_ffn, src_ag, wT_d, dst_dram,
                     with_lora=True, do_stats=True):
            peB, veB, acB, gpB, ccB = pe.v, ve.v, ac.v, gp.v, cc.v
            gqB, gwB = gq.v, gw.v
            ldxB = [ldx[0].v, ldx[1].v]
            stxB = [stx[0].v, stx[1].v]
            ldfB = [s.v for s in ldf4]
            ford = [c for q in range(4)
                    for c in (list(range(4 * q, 4 * q + 4)) + list(range(16 + 4 * q, 16 + 4 * q + 4)))]
            ag3d = src_ag.rearrange("(c p) t -> p c t", p=128)
            rs3d = resid.rearrange("(c p) t -> p c t", p=128)
            ds3d = dst_dram.rearrange("(c p) t -> p c t", p=128)
            wo3d = wT_d.rearrange("(c p) o -> p c o", p=128)
            dn3d = down_d.rearrange("(c p) r -> p c r", p=128)
            up3d = up_d.rearrange("(r p) o -> p r o", p=128)
            wq3d = wqT_d.rearrange("(c p) k -> p c k", p=128)

            with ExitStack() as ph:
                xq = mk(ph, "ff_xq", [128, 32, 1024], BF16)
                Wp = mk(ph, "ff_Wp", [128, 2, 32, 512], BF16)
                dn = mk(ph, "ff_dn", [128, 32, 512], BF16)
                up = mk(ph, "ff_up", [128, 4, 2048], BF16)
                hTr = mk(ph, "ff_hTr", [128, 4, 1024], BF16)
                hT = hTr
                wq_sb = mk(ph, "ff_wq", [128, 16, 128], BF16)
                mcolb = mk(ph, "ff_mcb", [128, 16], BF16)
                keys_sb = mk(ph, "ff_keys", [128, 8], F32)
                sel_sb = mk(ph, "ff_sel", [8, 4, 128], BF16)
                qpart = mk(ph, "ff_qp", [128, 1], F32)
                qTc = mk(ph, "ff_qT", [128, 1], F32)
                srow = mk(ph, "ff_srow", [1, 40], F32)
                scol = mk(ph, "ff_scol", [8, 1], F32)
                ccol = mk(ph, "ff_ccol", [8, 1], BF16)
                xr2 = mk(ph, "ff_xr2", [128, 1, 4, 512], F32)
                x3t = mk(ph, "ff_x3t", [128, 2, 512], F32)
                xq3 = mk(ph, "ff_xq3", [128, 2, 512], BF16)

                if with_lora:
                    @block.vector
                    def _(v):
                        v.tensor_copy(out=mcolb[:], in_=mcol[:]).then_inc(ve.h, 1)
                    ve.bump(1)
                    veB = ve.v
                pe_pos = {}
                pc = peB
                if with_lora:
                    pc += 1; pe_pos["qmm"] = pc
                for tqr in range(4):
                    if with_lora:
                        for rnd in range(8):
                            pc += 1; pe_pos[("h", tqr, rnd)] = pc
                        if tqr == 0:
                            pc += 1; pe_pos["scoresT"] = pc
                            pc += 1; pe_pos["sT"] = pc
                            pc += 1; pe_pos["cT"] = pc
                            pc += 1; pe_pos["sel"] = pc
                    for og in range(4):
                        for tch in range(2):
                            pc += 1; pe_pos[("b", tqr, og, tch)] = pc
                            if do_stats:
                                for j in range(4):
                                    pc += 1; pe_pos[("t", tqr, og, tch, j)] = pc
                pe_total = pc
                ac_pos = {}
                a = acB
                if with_lora:
                    a += 1; ac_pos["qpart"] = a
                    for rnd in range(8):
                        a += 1; ac_pos[("hev", 0, rnd)] = a
                    for nm in ("scol", "srow", "exp", "ccol", "cvec"):
                        a += 1; ac_pos[nm] = a
                    for tqr in range(1, 4):
                        for rnd in range(8):
                            a += 1; ac_pos[("hev", tqr, rnd)] = a
                ac_total = a
                ve_pos = {}
                w = veB
                if with_lora:
                    for nm in ("mx", "bias", "sum", "rec", "cf"):
                        w += 1; ve_pos[nm] = w
                for tqr in range(4):
                    if with_lora:
                        for rnd in range(8):
                            w += 1; ve_pos[("hs", tqr, rnd)] = w
                    for og in range(4):
                        for tch in range(2):
                            for j in range(4):
                                w += 1; ve_pos[("a", tqr, og, tch, j)] = w
                                if do_stats:
                                    w += 1
                                ve_pos[("q", tqr, og, tch, j)] = w
                ve_total = w
                gp_total = gpB
                gpnB = [gpn[0].v, gpn[1].v]
                wp_tgt = {}

                def evac_idx(tqr, og, tch, j):
                    return ((tqr * 4 + og) * 2 + tch) * 4 + j

                @block.sync
                def _(sync):
                    if with_lora:
                        sync.dma_start(out=keys_sb[:], in_=keysT_d).then_inc(ld.h, 16)
                        ld.bump(16)
                    ii = 0
                    for tqr in range(4):
                        if tqr > 0:
                            key = ("t", tqr - 1, 3, 1, 3) if do_stats else ("b", tqr - 1, 3, 1)
                            sync.wait_ge(pe.h, pe_pos[key])
                        for q in range(4):
                            sync.wait_ge(cc.h, ccB_ag + q + 1)
                            ts = slice(tqr * 1024, (tqr + 1) * 1024)
                            sync.dma_start(out=xq[:, 8 * q:8 * q + 8, :],
                                           in_=ag3d[:, 8 * q:8 * q + 8, ts]).then_inc(ldf4[q].h, 16)
                            ldf4[q].bump(16)
                        for og in range(4):
                            for tch in range(2):
                                ii = (tqr * 4 + og) * 2 + tch
                                if ii >= 1:
                                    pv = ii - 1
                                    ptqr, pog, ptch = pv // 8, (pv // 2) % 4, pv % 2
                                    sync.wait_ge(ve.h, ve_pos[("q", ptqr, pog, ptch, 3)])
                                sync.dma_start(
                                    out=xr2[:, 0, :, :],
                                    in_=rs3d[:, og * 4:(og + 1) * 4,
                                             tqr * 1024 + tch * 512:tqr * 1024 + (tch + 1) * 512]
                                ).then_inc(ldx[0].h, 16)
                                ldx[0].bump(16)

                @block.gpsimd
                def _(g):
                    if with_lora:
                        g.dma_start(out=wq_sb[:], in_=wq3d).then_inc(gw.h, 16)
                        g.dma_start(out=sel_sb[:], in_=sel_d.rearrange("g e k -> e g k")).then_inc(gw.h, 16)
                        g.dma_start(out=dn[:], in_=dn3d).then_inc(gw.h, 16)
                        g.dma_start(out=up[:], in_=up3d).then_inc(gw.h, 16)
                        g.wait_ge(ac.h, ac_pos["qpart"])
                        g.dma_start(out=arq_i, in_=qpart[:]).then_inc(gq.h, 16)
                        g.wait_ge(gq.h, gqB + 16)
                        g.collective_compute("AllReduce", ALU.add, replica_groups=PAIRS,
                                             ins=[arq_i.opt()], outs=[arq_o.opt()]).then_inc(cc.h, 1)
                        g.wait_ge(cc.h, ccB + 1)
                        g.dma_start(out=qTc[:], in_=arq_o).then_inc(gq.h, 16)
                    sii = 0
                    for k0 in range(2):
                        g.dma_start(out=Wp[:, k0 % 2, :, :],
                                    in_=wo3d[:, :, (k0 % 4) * 512:(k0 % 4 + 1) * 512]
                                    ).then_inc(gpn[k0 % 2].h, 16)
                        gpn[k0 % 2].bump(16)
                        wp_tgt[k0] = gpn[k0 % 2].v
                    for tqr in range(4):
                        for og in range(4):
                            k = tqr * 4 + og
                            for tch in range(2):
                                for j in range(4):
                                    g.wait_ge(ve.h, ve_pos[("a", tqr, og, tch, j)])
                                    g.dma_start(
                                        out=ds3d[:, og * 4 + j,
                                                 tqr * 1024 + tch * 512:tqr * 1024 + (tch + 1) * 512],
                                        in_=x3t[:, sii % 2, :]).then_inc(stx[sii % 2].h, 16)
                                    stx[sii % 2].bump(16)
                                    sii += 1
                            if k + 2 <= 15:
                                kk = k + 2
                                pv = kk - 2
                                g.wait_ge(pe.h, pe_pos[("b", pv // 4, pv % 4, 1)])
                                g.dma_start(out=Wp[:, kk % 2, :, :],
                                            in_=wo3d[:, :, (kk % 4) * 512:(kk % 4 + 1) * 512]
                                            ).then_inc(gpn[kk % 2].h, 16)
                                gpn[kk % 2].bump(16)
                                wp_tgt[kk] = gpn[kk % 2].v
                if with_lora:
                    cc.bump(1)
                    gq.bump(32)
                    gw.bump(64)
                gp.v = gp_total

                @block.tensor
                def _(t):
                    if with_lora:
                        t.wait_ge(gw.h, gwB + 64)
                        t.wait_ge(ve.h, veB)
                        for c in range(16):
                            mm = t.matmul(out=psb[0][:, 0:1], lhsT=wq_sb[:, c, :],
                                          rhs=mcolb[:, c:c + 1], start=(c == 0), stop=(c == 15))
                        mm.then_inc(pe.h, 1)
                    for tqr in range(4):
                        for q in range(4):
                            t.wait_ge(ldf4[q].h, ldfB[q] + 16 * (tqr + 1))
                        if with_lora:
                            for rnd in range(8):
                                g4, tch = rnd // 2, rnd % 2
                                bank = psb[4] if rnd % 2 == 0 else psb[5]
                                if (tqr, rnd) > (0, 1):
                                    pv = tqr * 8 + rnd - 2
                                    t.wait_ge(ac.h, ac_pos[("hev", pv // 8, pv % 8)])
                                if tqr == 0 and rnd == 0:
                                    t.wait_ge(gw.h, gwB + 64)
                                for pi in range(32):
                                    fc = ford[pi]
                                    mm = t.matmul(out=bank[:, 0:512],
                                                  lhsT=dn[:, fc, g4 * 128:(g4 + 1) * 128],
                                                  rhs=xq[:, pi, tch * 512:(tch + 1) * 512],
                                                  start=(pi == 0), stop=(pi == 31))
                                mm.then_inc(pe.h, 1)
                            if tqr == 0:
                                t.wait_ge(gq.h, gqB + 32)
                                t.wait_ge(ld.h, ld.v)
                                t.matmul(out=psb[1][0:8, 0:1], lhsT=keys_sb[:],
                                         rhs=qTc[:], start=True, stop=True).then_inc(pe.h, 1)
                                t.wait_ge(ac.h, ac_pos["scol"])
                                t.transpose(out=psb[2][0:1, 0:8], in_=scol[:],
                                            identity=id_f32[0:8, 0:8]).then_inc(pe.h, 1)
                                t.wait_ge(ve.h, ve_pos["cf"])
                                t.transpose(out=psb[3][0:8, 0:1], in_=srow[0:1, 26:34],
                                            identity=id_f32[0:1, 0:1]).then_inc(pe.h, 1)
                                t.wait_ge(ac.h, ac_pos["ccol"])
                                for g4 in range(4):
                                    mm = t.matmul(out=psb[1][:, g4:g4 + 1], lhsT=sel_sb[:, g4, :],
                                                  rhs=ccol[:], start=(g4 == 0), stop=(g4 == 3))
                                mm.then_inc(pe.h, 1)
                        for og in range(4):
                            kpanf = tqr * 4 + og
                            t.wait_ge(gpn[kpanf % 2].h, wp_tgt[kpanf])
                            if with_lora and tqr == 0 and og == 0:
                                t.wait_ge(ac.h, ac_pos["cvec"])
                            for tch in range(2):
                                ii = (tqr * 4 + og) * 2 + tch
                                lag = 2 if do_stats else 1
                                if ii >= lag:
                                    pv = ii - lag
                                    t.wait_ge(ve.h, ve_pos[("q", pv // 8, (pv // 2) % 4, pv % 2, 3)])
                                for pi in range(32):
                                    fc = ford[pi]
                                    for j in range(4):
                                        mm = t.matmul(out=psb[j][:, 0:512],
                                                      lhsT=Wp[:, kpanf % 2, fc, j * 128:(j + 1) * 128],
                                                      rhs=xq[:, pi, tch * 512:(tch + 1) * 512],
                                                      start=(pi == 0),
                                                      stop=(not with_lora and pi == 31))
                                if with_lora:
                                    t.wait_ge(ve.h, ve_pos[("hs", tqr, 6 + tch)])
                                    for j in range(4):
                                        for rt in range(4):
                                            mm = t.matmul(
                                                out=psb[j][:, 0:512],
                                                lhsT=up[:, rt, (og * 4 + j) * 128:(og * 4 + j + 1) * 128],
                                                rhs=hT[:, rt, tch * 512:(tch + 1) * 512],
                                                start=False, stop=(rt == 3))
                                mm.then_inc(pe.h, 1)
                                if do_stats:
                                    for j in range(4):
                                        t.wait_ge(ve.h, ve_pos[("q", tqr, og, tch, j)])
                                        for k in range(4):
                                            it = tqr * 8 + tch * 4 + k
                                            fs = first_ffn[0]
                                            t.matmul(out=ps_stat[:, it:it + 1],
                                                     lhsT=x3t[:, (evac_idx(tqr, og, tch, j) % 2),
                                                              k * 128:(k + 1) * 128],
                                                     rhs=ones_c32[:, 0:1], start=fs, stop=False)
                                            first_ffn[0] = False
                                            mm = t.matmul(out=ps_stat[:, 32 + it:33 + it],
                                                          lhsT=xq3[:, (evac_idx(tqr, og, tch, j) % 2),
                                                                   k * 128:(k + 1) * 128],
                                                          rhs=ones_bf[:, 0:1], start=False,
                                                          stop=(tqr == 3 and og == 3 and tch == 1
                                                                and j == 3 and k == 3))
                                        mm.then_inc(pe.h, 1)

                @block.scalar
                def _(s):
                    if not with_lora:
                        return

                    def hev(tqr, rnd):
                        bank = psb[4] if rnd % 2 == 0 else psb[5]
                        g4, tch = rnd // 2, rnd % 2
                        s.wait_ge(pe.h, pe_pos[("h", tqr, rnd)])
                        s.copy(out=hTr[:, g4, tch * 512:(tch + 1) * 512],
                               in_=bank[:, 0:512]).then_inc(ac.h, 1)

                    s.wait_ge(pe.h, pe_pos["qmm"])
                    s.copy(out=qpart[:], in_=psb[0][:, 0:1]).then_inc(ac.h, 1)
                    for rnd in range(8):
                        hev(0, rnd)
                    s.wait_ge(pe.h, pe_pos["scoresT"])
                    s.copy(out=scol[:], in_=psb[1][0:8, 0:1]).then_inc(ac.h, 1)
                    s.wait_ge(pe.h, pe_pos["sT"])
                    s.copy(out=srow[0:1, 0:8], in_=psb[2][0:1, 0:8]).then_inc(ac.h, 1)
                    s.wait_ge(ve.h, ve_pos["bias"])
                    s.activation(out=srow[0:1, 16:24], in_=srow[0:1, 0:8], func=AF.Exp,
                                 scale=scC[0:1, 0:1], bias=srow[0:1, 9:10]).then_inc(ac.h, 1)
                    s.wait_ge(pe.h, pe_pos["cT"])
                    s.copy(out=ccol[:], in_=psb[3][0:8, 0:1]).then_inc(ac.h, 1)
                    s.wait_ge(pe.h, pe_pos["sel"])
                    s.copy(out=cvec[:], in_=psb[1][:, 0:4]).then_inc(ac.h, 1)
                    for tqr in range(1, 4):
                        for rnd in range(8):
                            hev(tqr, rnd)
                ac.v = ac_total

                @block.vector
                def _(v):
                    if with_lora:
                        v.wait_ge(ac.h, ac_pos["srow"])
                        v.tensor_reduce(out=srow[0:1, 8:9], in_=srow[0:1, 0:8],
                                        axis=mybir.AxisListType.X, op=ALU.max).then_inc(ve.h, 1)
                        v.wait_ge(ve.h, ve_pos["mx"])
                        v.tensor_scalar(out=srow[0:1, 9:10], in0=srow[0:1, 8:9],
                                        scalar1=scC[0:1, 1:2], scalar2=None,
                                        op0=ALU.mult).then_inc(ve.h, 1)
                        v.wait_ge(ac.h, ac_pos["exp"])
                        v.tensor_reduce(out=srow[0:1, 24:25], in_=srow[0:1, 16:24],
                                        axis=mybir.AxisListType.X, op=ALU.add).then_inc(ve.h, 1)
                        v.wait_ge(ve.h, ve_pos["sum"])
                        v.reciprocal(out=srow[0:1, 25:26], in_=srow[0:1, 24:25]).then_inc(ve.h, 1)
                    if with_lora:
                        v.wait_ge(ve.h, ve_pos["rec"])
                        v.tensor_scalar(out=srow[0:1, 26:34], in0=srow[0:1, 16:24],
                                        scalar1=srow[0:1, 25:26], scalar2=None,
                                        op0=ALU.mult).then_inc(ve.h, 1)
                    for tqr in range(4):
                        if with_lora:
                            for rnd in range(8):
                                g4, tch = rnd // 2, rnd % 2
                                v.wait_ge(ac.h, ac_pos[("hev", tqr, rnd)])
                                if tqr == 0 and rnd == 0:
                                    v.wait_ge(ac.h, ac_pos["cvec"])
                                v.tensor_scalar(out=hTr[:, g4, tch * 512:(tch + 1) * 512],
                                                in0=hTr[:, g4, tch * 512:(tch + 1) * 512],
                                                scalar1=cvec[:, g4:g4 + 1], scalar2=None,
                                                op0=ALU.mult).then_inc(ve.h, 1)
                        for og in range(4):
                            for tch in range(2):
                                ii = (tqr * 4 + og) * 2 + tch
                                v.wait_ge(pe.h, pe_pos[("b", tqr, og, tch)])
                                v.wait_ge(ldx[0].h, ldxB[0] + 16 * (ii + 1))
                                for j in range(4):
                                    ei = evac_idx(tqr, og, tch, j)
                                    if ei >= 2:
                                        pj = ei - 2
                                        ptqr = pj // 32
                                        prem = pj % 32
                                        pog, ptch, ppj = prem // 8, (prem // 4) % 2, prem % 4
                                        if do_stats:
                                            v.wait_ge(pe.h, pe_pos[("t", ptqr, pog, ptch, ppj)])
                                        v.wait_ge(stx[ei % 2].h, stxB[ei % 2] + 16 * (ei // 2))
                                    v.tensor_tensor(out=x3t[:, ei % 2, :],
                                                    in0=psb[j][:, 0:512],
                                                    in1=xr2[:, 0, j, :],
                                                    op=ALU.add).then_inc(ve.h, 1)
                                    if do_stats:
                                        v.wait_ge(ve.h, ve_pos[("a", tqr, og, tch, j)])
                                        v.tensor_tensor(out=xq3[:, ei % 2, :], in0=x3t[:, ei % 2, :],
                                                        in1=x3t[:, ei % 2, :],
                                                        op=ALU.mult).then_inc(ve.h, 1)
                ve.v = ve_total
                pe.v = pe_total

                end_phase({"gpsimd": [(stx[0].h, stx[0].v), (stx[1].h, stx[1].v)],
                           "vector": [(ve.h, ve.v)], "tensor": [(pe.h, pe.v)],
                           "scalar": [(ac.h, ac.v)],
                           "sync": [(ldx[0].h, ldx[0].v), (ldx[1].h, ldx[1].v)] +
                                   [(s_.h, s_.v) for s_ in ldf4]})


        spq = [SEM("s_spq0"), SEM("s_spq1")]
        kvq = [SEM("s_kvq0"), SEM("s_kvq1")]
        ldfb = [SEM("s_ldfb0"), SEM("s_ldfb1"), SEM("s_ldfb2"), SEM("s_ldfb3")]

        def emit_attn_b(ccB_ag):
            ag3d = agA.rearrange("(c p) t -> p c t", p=128)
            q3d = qsp.rearrange("(c p) t -> p c t", p=128)
            k3d = ksp.rearrange("(c p) t -> p c t", p=128)
            ob3d = obsp.rearrange("(c p) t -> p c t", p=128)
            w3ds = [qwT_d.rearrange("(c p) o -> p c o", p=128),
                    kwT_d.rearrange("(c p) o -> p c o", p=128),
                    vwT_d.rearrange("(c p) o -> p c o", p=128)]
            ford = [c for q in range(4)
                    for c in (list(range(4 * q, 4 * q + 4)) + list(range(16 + 4 * q, 16 + 4 * q + 4)))]
            def text_rng(r):
                lo = max(513, 512 * r) - 512 * r
                hi = min(4040, 512 * (r + 1)) - 512 * r
                return (lo, hi) if hi > lo else None

            with ExitStack() as pho:
                visb = mk(pho, "ab_visb", [128, 33], F32)
                scD = mk(pho, "ab_scD", [128, 1], F32)
                qbars = mk(pho, "ab_qbars", [128, 16], F32)

                @block.vector
                def _(v):
                    v.memset(scD[:], 1.0 / math.sqrt(128.0))
                    v.memset(qbars[:], 0.0).then_inc(ve.h, 1)
                ve.bump(1)

                with ExitStack() as ph1:
                    xsl = mk(ph1, "ab_xsl", [128, 2, 32, 512], BF16)
                    Wp = mk(ph1, "ab_Wp", [128, 2, 32, 512], BF16)
                    vtm = mk(ph1, "ab_vtm", [128, 4, 32, 128], BF16)
                    vev = mk(ph1, "ab_vev", [128, 2, 512], BF16)
                    ropec = mk(ph1, "ab_rc", [128, T], BF16)
                    ropes = mk(ph1, "ab_rs", [128, T], BF16)
                    rsw = mk(ph1, "ab_rsw", [128, 2, 512], F32)
                    rt1 = mk(ph1, "ab_rt1", [128, 512], F32)
                    rt2 = mk(ph1, "ab_rt2", [128, 512], F32)
                    qtmp = mk(ph1, "ab_qtmp", [128, 1], F32)
                    robuf = mk(ph1, "ab_rob", [128, 2, 512], BF16)
                    kv32 = mk(ph1, "ab_kv32", [128, 2, 512], F32)

                    @block.sync
                    def _(sync):
                        sync.dma_start(out=ropec[:], in_=ropec_d).then_inc(ld.h, 16)
                        sync.dma_start(out=ropes[:], in_=ropes_d).then_inc(ld.h, 16)
                    ld.bump(32)
                    ld_rope = ld.v

                    panel_starts = []
                    for Wi in range(3):
                        for og in range(4):
                            peB_, acB_, veB_ = pe.v, ac.v, ve.v
                            kpan = Wi * 4 + og
                            panel_starts.append(peB_)
                            ldfB_ = [s.v for s in ldf4]
                            ldfbB_ = [s.v for s in ldfb]
                            spqB_ = [spq[0].v, spq[1].v]

                            @block.gpsimd
                            def _(g, Wi=Wi, og=og, kpan=kpan,
                                  tgt=(panel_starts[kpan - 1] if kpan >= 1 else 0)):
                                if kpan >= 2:
                                    g.wait_ge(pe.h, tgt)
                                g.dma_start(out=Wp[:, kpan % 2, :, :],
                                            in_=w3ds[Wi][:, :, og * 512:(og + 1) * 512]
                                            ).then_inc(kvq[kpan % 2].h, 16)
                            kvq[kpan % 2].bump(16)
                            gpv = (kvq[kpan % 2].h, kvq[kpan % 2].v)

                            nr = 1 if Wi < 2 else 5

                            ve_unit = []
                            ve_t2 = []
                            if Wi < 2:
                                w_ = veB_
                                for r_ in range(8):
                                    for j_ in range(4):
                                        ve_t2.append(w_ + 2)
                                        w_ += 3
                                        if Wi == 0 and text_rng(r_) is not None:
                                            w_ += 2
                                        ve_unit.append(w_)

                            @block.sync
                            def _(sync, Wi=Wi, og=og, peB_=peB_, nr=nr,
                                  ve_unit=list(ve_unit)):
                                dst = q3d if Wi == 0 else k3d
                                for t8 in range(8):
                                    if t8 < 2 and not (Wi == 0 and og == 0):
                                        sync.wait_ge(pe.h, peB_)
                                    if t8 >= 2:
                                        sync.wait_ge(pe.h, peB_ + nr * (t8 - 1))
                                    for q in range(4):
                                        if Wi == 0 and og == 0:
                                            sync.wait_ge(cc.h, ccB_ag + q + 1)
                                        sem = ldf4[q] if t8 % 2 == 0 else ldfb[q]
                                        sync.dma_start(out=xsl[:, t8 % 2, 8 * q:8 * q + 8, :],
                                                       in_=ag3d[:, 8 * q:8 * q + 8,
                                                                t8 * 512:(t8 + 1) * 512]
                                                       ).then_inc(sem.h, 16)
                                        sem.bump(16)
                                    if Wi < 2 and t8 >= 1:
                                        r = t8 - 1
                                        for j in range(4):
                                            u = 4 * r + j
                                            sync.wait_ge(ve.h, ve_unit[u])
                                            sync.dma_start(out=dst[:, og * 4 + j,
                                                                   r * 512:(r + 1) * 512],
                                                           in_=robuf[:, u % 2, :]
                                                           ).then_inc(spq[u % 2].h, 16)
                                            spq[u % 2].bump(16)

                            if Wi < 2:
                                ve_tot_ = ve_unit[-1]

                                @block.tensor
                                def _(t, Wi=Wi, og=og, kpan=kpan, peB_=peB_, acB_=acB_, veB_=veB_,
                                      ldfB_=ldfB_, ldfbB_=ldfbB_, gpv=gpv, ve_unit=list(ve_unit)):
                                    t.wait_ge(gpv[0], gpv[1])
                                    for r in range(8):
                                        for q in range(4):
                                            sem = ldf4[q] if r % 2 == 0 else ldfb[q]
                                            base = ldfB_[q] if r % 2 == 0 else ldfbB_[q]
                                            t.wait_ge(sem.h, base + 16 * (r // 2 + 1))
                                        if r >= 1:
                                            t.wait_ge(ac.h, acB_ + 4 * (r - 1) + 4)
                                            t.wait_ge(ve.h, ve_unit[4 * (r - 1) + 3])
                                        for pi in range(32):
                                            fc = ford[pi]
                                            for j in range(4):
                                                mm = t.matmul(out=psb[j][:, 0:512],
                                                              lhsT=Wp[:, kpan % 2, fc, j * 128:(j + 1) * 128],
                                                              rhs=xsl[:, r % 2, pi, :],
                                                              start=(pi == 0), stop=(pi == 31))
                                        mm.then_inc(pe.h, 1)
                                pe.bump(8)

                                @block.scalar
                                def _(s, og=og, peB_=peB_, veB_=veB_, ve_t2=list(ve_t2)):
                                    for r in range(8):
                                        for j in range(4):
                                            s.wait_ge(pe.h, peB_ + r + 1)
                                            if r * 4 + j >= 2:
                                                pu = r * 4 + j - 2
                                                s.wait_ge(ve.h, ve_t2[pu])
                                            s.copy(out=rsw[0:64, j % 2, :],
                                                   in_=psb[j][64:128, 0:512])
                                            s.copy(out=rsw[64:128, j % 2, :],
                                                   in_=psb[j][0:64, 0:512]).then_inc(ac.h, 1)
                                ac.bump(32)

                                @block.vector
                                def _(v, Wi=Wi, og=og, peB_=peB_, acB_=acB_, veB_=veB_,
                                      spqB_=spqB_, ve_unit=list(ve_unit), ld_rope=ld_rope):
                                    if Wi == 0 and og == 0:
                                        v.wait_ge(ld.h, ld_rope)
                                    for r in range(8):
                                        ts = slice(r * 512, (r + 1) * 512)
                                        for j in range(4):
                                            u = 4 * r + j
                                            v.wait_ge(pe.h, peB_ + r + 1)
                                            v.wait_ge(ac.h, acB_ + 4 * r + j + 1)
                                            if u >= 2:
                                                v.wait_ge(spq[u % 2].h,
                                                          spqB_[u % 2] + 16 * (u // 2))
                                            elif og > 0 or Wi > 0:
                                                v.wait_ge(spq[u % 2].h, spqB_[u % 2])
                                            if u >= 1:
                                                v.wait_ge(ve.h, ve_unit[u - 1])
                                            v.tensor_tensor(out=rt1[:], in0=psb[j][:, 0:512],
                                                            in1=ropec[:, ts],
                                                            op=ALU.mult).then_inc(ve.h, 1)
                                            v.tensor_tensor(out=rt2[:], in0=rsw[:, j % 2, :],
                                                            in1=ropes[:, ts],
                                                            op=ALU.mult).then_inc(ve.h, 1)
                                            v.wait_ge(ve.h, ve_unit[u] - 1 -
                                                      (2 if (Wi == 0 and text_rng(r)) else 0))
                                            v.tensor_tensor(out=robuf[:, u % 2, :], in0=rt1[:],
                                                            in1=rt2[:], op=ALU.add).then_inc(ve.h, 1)
                                            if Wi == 0 and text_rng(r) is not None:
                                                lo, hi = text_rng(r)
                                                v.wait_ge(ve.h, ve_unit[u] - 2)
                                                v.tensor_reduce(out=qtmp[:], in_=robuf[:, u % 2, lo:hi],
                                                                axis=mybir.AxisListType.X,
                                                                op=ALU.add).then_inc(ve.h, 1)
                                                v.wait_ge(ve.h, ve_unit[u] - 1)
                                                v.tensor_tensor(out=qbars[:, 4 * og + j:4 * og + j + 1],
                                                                in0=qbars[:, 4 * og + j:4 * og + j + 1],
                                                                in1=qtmp[:], op=ALU.add).then_inc(ve.h, 1)
                                ve.v = ve_tot_

                                @block.sync
                                def _(sync, Wi=Wi, og=og, ve_unit=list(ve_unit)):
                                    dst = q3d if Wi == 0 else k3d
                                    r = 7
                                    for j in range(4):
                                        u = 4 * r + j
                                        sync.wait_ge(ve.h, ve_unit[u])
                                        sync.dma_start(out=dst[:, og * 4 + j, r * 512:(r + 1) * 512],
                                                       in_=robuf[:, u % 2, :]
                                                       ).then_inc(spq[u % 2].h, 16)
                                        spq[u % 2].bump(16)
                            else:
                                @block.tensor
                                def _(t, og=og, kpan=kpan, peB_=peB_, acB_=acB_, ldfB_=ldfB_,
                                      ldfbB_=ldfbB_, gpv=gpv):
                                    t.wait_ge(gpv[0], gpv[1])
                                    for r in range(8):
                                        for q in range(4):
                                            sem = ldf4[q] if r % 2 == 0 else ldfb[q]
                                            base = ldfB_[q] if r % 2 == 0 else ldfbB_[q]
                                            t.wait_ge(sem.h, base + 16 * (r // 2 + 1))
                                        if r >= 1:
                                            t.wait_ge(ac.h, acB_ + 8 * (r - 1) + 7)
                                        for pi in range(32):
                                            fc = ford[pi]
                                            for j in range(4):
                                                mm = t.matmul(out=psb[j][:, 0:512],
                                                              lhsT=Wp[:, kpan % 2, fc, j * 128:(j + 1) * 128],
                                                              rhs=xsl[:, r % 2, pi, :],
                                                              start=(pi == 0), stop=(pi == 31))
                                        mm.then_inc(pe.h, 1)
                                        for j in range(4):
                                            t.wait_ge(ac.h, acB_ + 8 * r + 2 * j + 1)
                                            if r * 4 + j >= 2:
                                                pj = r * 4 + j - 2
                                                t.wait_ge(ac.h, acB_ + 8 * (pj // 4) + 2 * (pj % 4) + 2)
                                            for k in range(4):
                                                mm = t.transpose(
                                                    out=psT[:, (j % 2) * 512 + k * 128:(j % 2) * 512 + (k + 1) * 128],
                                                    in_=vev[:, j % 2, k * 128:(k + 1) * 128],
                                                    identity=id_bf[:])
                                            mm.then_inc(pe.h, 1)
                                pe.bump(40)

                                @block.scalar
                                def _(s, og=og, peB_=peB_, spqB_=spqB_):
                                    for r in range(8):
                                        for j in range(4):
                                            s.wait_ge(pe.h, peB_ + 5 * r + 1)
                                            if r * 4 + j >= 2:
                                                s.wait_ge(pe.h, peB_ + 5 * r + j)
                                            s.copy(out=vev[:, j % 2, :],
                                                   in_=psb[j][:, 0:512]).then_inc(ac.h, 1)
                                            if og > 0 and r == 0 and j == 0:
                                                s.wait_ge(spq[(og - 1) % 2].h, spqB_[(og - 1) % 2])
                                            s.wait_ge(pe.h, peB_ + 5 * r + 1 + (j + 1))
                                            s.copy(out=vtm[:, j, 4 * r:4 * r + 4, :],
                                                   in_=psT[:, (j % 2) * 512:(j % 2) * 512 + 512]
                                                   ).then_inc(ac.h, 1)
                                ac.bump(64)

                                @block.sync
                                def _(sync, og=og, acE=ac.v):
                                    sync.wait_ge(ac.h, acE)
                                    sync.dma_start(out=vsp[og * 4:(og + 1) * 4].rearrange("h p tt d -> p h tt d"),
                                                   in_=vtm[:]).then_inc(spq[og % 2].h, 16)
                                spq[og % 2].bump(16)

                    peI, acI, gpI = pe.v, ac.v, gp.v
                    kvqB = [kvq[0].v, kvq[1].v]
                    spq_fin = [spq[0].v, spq[1].v]

                    @block.gpsimd
                    def _(g):
                        g.wait_ge(spq[0].h, spq_fin[0])
                        g.wait_ge(spq[1].h, spq_fin[1])
                        for h in range(16):
                            if h >= 2:
                                g.wait_ge(pe.h, peI + h - 1)
                            g.dma_start(out=kv32[:, h % 2, :],
                                        in_=ksp[128 * h:128 * (h + 1), 1:513]
                                        ).then_inc(kvq[h % 2].h, 16)
                            kvq[h % 2].bump(16)

                    @block.tensor
                    def _(t):
                        for h in range(16):
                            t.wait_ge(kvq[h % 2].h, kvqB[h % 2] + 16 * (h // 2 + 1))
                            t.matmul(out=psb[6][0:1, 0:512],
                                     lhsT=qbars[:, h:h + 1], rhs=kv32[:, h % 2, :],
                                     start=(h == 0), stop=(h == 15)).then_inc(pe.h, 1)
                    pe.bump(16)

                    end_phase({"gpsimd": [(spq[0].h, spq[0].v), (spq[1].h, spq[1].v),
                                          (kvq[0].h, kvq[0].v), (kvq[1].h, kvq[1].v)],
                               "vector": [(ve.h, ve.v)], "tensor": [(pe.h, pe.v)],
                               "scalar": [(ac.h, ac.v)],
                               "sync": [(s_.h, s_.v) for s_ in ldf4] +
                                       [(s_.h, s_.v) for s_ in ldfb] + [(ld.h, ld.v)]})

                with ExitStack() as ph15:
                    impw = mk(ph15, "ab_impw", [128, 560], F32)
                    ext = mk(ph15, "ab_ext", [1, 640], F32)
                    irow = mk(ph15, "ab_irow", [1, 512], F32)
                    iotab = mk(ph15, "ab_iotab", [128, 512], F32)
                    impg_sb = mk(ph15, "ab_impg", [1, 1], F32)
                    w1 = mk(ph15, "ab_w1", [128, 512], F32)
                    w2 = mk(ph15, "ab_w2", [128, 512], F32)
                    w3 = mk(ph15, "ab_w3", [128, 512], F32)
                    iota_r = mk(ph15, "ab_iotar", [1, 512], F32)
                    iota_c = mk(ph15, "ab_iotac", [128, 1], F32)

                    peM, acM, veM, gpM, ccM = pe.v, ac.v, ve.v, gp.v, cc.v

                    @block.sync
                    def _(sync):
                        sync.dma_start(out=iota_r[:], in_=iota512_d).then_inc(ld.h, 16)
                        sync.dma_start(out=iota_c[:], in_=iotac_d).then_inc(ld.h, 16)
                        sync.dma_start(out=impg_sb[:], in_=impg_d).then_inc(ld.h, 16)
                    ld.bump(48)

                    @block.scalar
                    def _(s):
                        s.copy(out=irow[:], in_=psb[6][0:1, 0:512]).then_inc(ac.h, 1)
                    ac.bump(1)

                    @block.vector
                    def _(v):
                        v.wait_ge(ac.h, acM + 1)
                        v.wait_ge(ld.h, ld.v)
                        v.tensor_scalar(out=irow[:], in0=irow[:], scalar1=impg_sb[0:1, 0:1],
                                        scalar2=None, op0=ALU.mult).then_inc(ve.h, 1)
                    ve.bump(1)

                    @block.gpsimd
                    def _(g):
                        g.wait_ge(ve.h, veM + 1)
                        g.dma_start(out=ari_imp, in_=irow[:]).then_inc(gp.h, 16)
                        g.wait_ge(gp.h, gpM + 16)
                        g.collective_compute("AllReduce", ALU.add, replica_groups=WORLD,
                                             ins=[ari_imp.opt()], outs=[aro_imp.opt()]).then_inc(cc.h, 1)
                        g.wait_ge(cc.h, ccM + 1)
                        g.dma_start(out=irow[:], in_=aro_imp).then_inc(gp.h, 16)
                    gp.bump(32); cc.bump(1)

                    @block.tensor
                    def _(t):
                        t.wait_ge(gp.h, gpM + 32)
                        t.matmul(out=psb[0][:, 0:512], lhsT=ones_row[:], rhs=irow[:],
                                 start=True, stop=True).then_inc(pe.h, 1)
                        t.matmul(out=psb[1][:, 0:512], lhsT=ones_row[:], rhs=iota_r[:],
                                 start=True, stop=True).then_inc(pe.h, 1)
                        for g4 in range(4):
                            t.wait_ge(ac.h, (acM + 1) if g4 == 0 else (acM + 3 + g4))
                            t.transpose(out=psb[2][0:128, 0:1],
                                        in_=irow[0:1, 128 * g4:128 * (g4 + 1)],
                                        identity=id_f32[0:1, 0:1]).then_inc(pe.h, 1)
                    pe.bump(6)

                    @block.scalar
                    def _(s):
                        s.wait_ge(pe.h, peM + 1)
                        s.copy(out=impw[:, 0:512], in_=psb[0][:, 0:512]).then_inc(ac.h, 1)
                        s.wait_ge(pe.h, peM + 2)
                        s.copy(out=iotab[:], in_=psb[1][:, 0:512]).then_inc(ac.h, 1)
                        for g4 in range(4):
                            s.wait_ge(pe.h, peM + 3 + g4)
                            s.copy(out=impw[:, 512 + g4:513 + g4],
                                   in_=psb[2][0:128, 0:1]).then_inc(ac.h, 1)
                    ac.bump(6)

                    @block.vector
                    def _(v):
                        k = veM + 1
                        v.wait_ge(ac.h, acM + 7)
                        for g4 in range(4):
                            v.tensor_scalar(out=impw[:, 516 + g4:517 + g4], in0=iota_c[:],
                                            scalar1=float(128 * g4), scalar2=None,
                                            op0=ALU.add).then_inc(ve.h, 1); k += 1
                        for g4 in range(4):
                            v.wait_ge(ve.h, k)
                            v.tensor_scalar(out=w1[:], in0=impw[:, 0:512],
                                            scalar1=impw[:, 512 + g4:513 + g4], scalar2=None,
                                            op0=ALU.is_gt).then_inc(ve.h, 1); k += 1
                            v.wait_ge(ve.h, k)
                            v.tensor_reduce(out=impw[:, 524:525], in_=w1[:],
                                            axis=mybir.AxisListType.X, op=ALU.add).then_inc(ve.h, 1); k += 1
                            v.wait_ge(ve.h, k)
                            v.tensor_scalar(out=w2[:], in0=impw[:, 0:512],
                                            scalar1=impw[:, 512 + g4:513 + g4], scalar2=None,
                                            op0=ALU.is_equal).then_inc(ve.h, 1); k += 1
                            v.wait_ge(ve.h, k)
                            v.tensor_scalar(out=w3[:], in0=iotab[:],
                                            scalar1=impw[:, 516 + g4:517 + g4], scalar2=None,
                                            op0=ALU.is_lt).then_inc(ve.h, 1); k += 1
                            v.wait_ge(ve.h, k)
                            v.tensor_tensor(out=w2[:], in0=w2[:], in1=w3[:],
                                            op=ALU.mult).then_inc(ve.h, 1); k += 1
                            v.wait_ge(ve.h, k)
                            v.tensor_reduce(out=impw[:, 525:526], in_=w2[:],
                                            axis=mybir.AxisListType.X, op=ALU.add).then_inc(ve.h, 1); k += 1
                            v.wait_ge(ve.h, k)
                            v.tensor_tensor(out=impw[:, 526:527], in0=impw[:, 524:525],
                                            in1=impw[:, 525:526], op=ALU.add).then_inc(ve.h, 1); k += 1
                            v.wait_ge(ve.h, k)
                            v.tensor_scalar(out=impw[:, 520 + g4:521 + g4], in0=impw[:, 526:527],
                                            scalar1=255.5, scalar2=-1.0e30,
                                            op0=ALU.is_gt, op1=ALU.mult).then_inc(ve.h, 1); k += 1
                        v.memset(ext[:], 0.0)
                        v.memset(visb[:], -30.0).then_inc(ve.h, 1); k += 1
                    ve.bump(37)

                    peM2, acM2, veM2 = pe.v, ac.v, ve.v

                    @block.tensor
                    def _(t):
                        t.wait_ge(ve.h, veM2)
                        for g4 in range(4):
                            if g4 >= 1:
                                t.wait_ge(ac.h, acM2 + g4)
                            t.transpose(out=psb[3][0:1, 0:128],
                                        in_=impw[:, 520 + g4:521 + g4],
                                        identity=id_f32[:]).then_inc(pe.h, 1)
                        for tt in range(5):
                            t.wait_ge(ac.h, acM2 + 4 + tt)
                            t.transpose(out=psb[3][0:128, 0:1],
                                        in_=ext[0:1, 128 * tt:128 * (tt + 1)],
                                        identity=id_f32[0:1, 0:1]).then_inc(pe.h, 1)
                    pe.bump(9)

                    @block.scalar
                    def _(s):
                        for g4 in range(4):
                            s.wait_ge(pe.h, peM2 + g4 + 1)
                            s.copy(out=ext[0:1, 1 + 128 * g4:1 + 128 * (g4 + 1)],
                                   in_=psb[3][0:1, 0:128]).then_inc(ac.h, 1)
                        for tt in range(5):
                            s.wait_ge(pe.h, peM2 + 4 + tt + 1)
                            s.copy(out=visb[:, tt:tt + 1],
                                   in_=psb[3][0:128, 0:1]).then_inc(ac.h, 1)
                    ac.bump(9)

                    @block.vector
                    def _(v):
                        v.wait_ge(ac.h, acM2 + 9)
                        v.tensor_scalar(out=visb[:, 0:5], in0=visb[:, 0:5],
                                        scalar1=-30.0, scalar2=None, op0=ALU.add).then_inc(ve.h, 1)
                    ve.bump(1)

                    end_phase({"vector": [(ve.h, ve.v)], "tensor": [(pe.h, pe.v)],
                               "scalar": [(ac.h, ac.v)], "gpsimd": [(gp.h, gp.v)],
                               "sync": [(ld.h, ld.v)]})

                with ExitStack() as ph2:
                    cbias_sb = mk(ph2, "ab_cb", [128, 4, 512], F32)
                    qh = mk(ph2, "ab_qh", [128, 2, T], BF16)
                    kh = mk(ph2, "ab_kh", [128, 2, T], BF16)
                    vh = mk(ph2, "ab_vh", [128, 2, 32, 128], BF16)
                    Eb = mk(ph2, "ab_E", [128, 2, 512], BF16)
                    s2 = mk(ph2, "ab_s2", [1, 512], F32)
                    s2r = mk(ph2, "ab_s2r", [1, 512], F32)
                    ibc = mk(ph2, "ab_ibc", [128, 512], F32)
                    ob = mk(ph2, "ab_ob", [128, 2, T], BF16)

                    @block.sync
                    def _(sync):
                        sync.dma_start(out=cbias_sb[:], in_=cbias_d.rearrange("d p j -> p d j")
                                       ).then_inc(ld.h, 16)
                    ld.bump(16)

                    head_tots = []
                    prev_head_plan = None
                    pending_ags = []
                    for h in range(16):
                        peH, acH, veH = pe.v, ac.v, ve.v
                        ldqB = [ldq[0].v, ldq[1].v]
                        ldxB = [ldx[0].v, ldx[1].v]
                        ldaB = [lda[0].v, lda[1].v]

                        @block.sync
                        def _(sync, h=h, hts=list(head_tots), prev=prev_head_plan):
                            if h >= 2:
                                sync.wait_ge(pe.h, hts[h - 2])
                            if h == 0:
                                sync.wait_ge(ld.h, ld.v)
                            if prev is not None:
                                pve, ph_ = prev
                                for t8 in range(8):
                                    sync.wait_ge(ve.h, pve[t8])
                                    sync.dma_start(out=ob3d[:, ph_, t8 * 512:(t8 + 1) * 512],
                                                   in_=ob[:, ph_ % 2, t8 * 512:(t8 + 1) * 512]
                                                   ).then_inc(spq[ph_ % 2].h, 16)
                                    spq[ph_ % 2].bump(16)
                            sync.dma_start(out=qh[:, h % 2, :],
                                           in_=q3d[:, h, :]).then_inc(ldq[h % 2].h, 16)
                            ldq[h % 2].bump(16)
                            sync.dma_start(out=kh[:, h % 2, :],
                                           in_=k3d[:, h, :]).then_inc(ldx[h % 2].h, 16)
                            ldx[h % 2].bump(16)
                            sync.dma_start(out=vh[:, h % 2, :, :],
                                           in_=vsp[h].rearrange("p tt d -> p tt d")).then_inc(lda[h % 2].h, 16)
                            lda[h % 2].bump(16)

                        ntks_b = [4 * (t8 + 1) for t8 in range(8)]
                        plan = []
                        for t8 in range(8):
                            ntk = ntks_b[t8]
                            seq = [("S", 0), ("S", 1)]
                            for k in range(2, ntk):
                                seq.append(("PVS", k - 2))
                                seq.append(("S", k))
                            seq.append(("PVS", ntk - 2))
                            seq.append(("PVS", ntk - 1))
                            seq.append(("BC", 0))
                            plan.append(seq)
                        pe_pos = {}
                        pc = pe.v
                        for t8, seq in enumerate(plan):
                            for ent in seq:
                                pc += 1
                                pe_pos[(t8,) + ent] = pc
                        pe_tot = pc
                        ac_pos = {}
                        a = ac.v
                        for t8 in range(8):
                            for tk in range(ntks_b[t8]):
                                a += 1
                                ac_pos[(t8, "e", tk)] = a
                            a += 1
                            ac_pos[(t8, "sum")] = a
                            a += 1
                            ac_pos[(t8, "ibc")] = a
                        ac_tot = a
                        ve_pos = {}
                        w = ve.v
                        for t8 in range(8):
                            for d4 in range(4):
                                w += 1
                                ve_pos[(t8, "cb", d4)] = w
                            w += 1
                            ve_pos[(t8, "rec")] = w
                            w += 1
                            ve_pos[(t8, "oev")] = w
                        ve_tot = w

                        @block.tensor
                        def _(t, h=h, peH=peH, acH=acH, veH=veH, ldqB=ldqB, ldxB=ldxB,
                              ldaB=ldaB, pe_pos=pe_pos, ac_pos=ac_pos, ve_pos=ve_pos,
                              ntks_b=ntks_b, plan=plan):
                            t.wait_ge(ldq[h % 2].h, ldqB[h % 2] + 16)
                            t.wait_ge(ldx[h % 2].h, ldxB[h % 2] + 16)
                            t.wait_ge(lda[h % 2].h, ldaB[h % 2] + 16)
                            if h == 0:
                                t.wait_ge(ve.h, veH)
                            for t8 in range(8):
                                ntk = ntks_b[t8]
                                for ent in plan[t8]:
                                    kind, idx = ent
                                    if kind == "S":
                                        tk = idx
                                        sb = psb[2] if tk % 2 == 0 else psb[3]
                                        if tk <= 1:
                                            if t8 > 0:
                                                t.wait_ge(ac.h, ac_pos[(t8 - 1, "ibc")])
                                            elif h > 0:
                                                t.wait_ge(ac.h, acH)
                                        t.matmul(out=sb[:, 0:512],
                                                 lhsT=kh[:, h % 2, tk * 128:(tk + 1) * 128],
                                                 rhs=qh[:, h % 2, t8 * 512:(t8 + 1) * 512],
                                                 start=True, stop=True).then_inc(pe.h, 1)
                                    elif kind == "PVS":
                                        tk = idx
                                        t.wait_ge(ac.h, ac_pos[(t8, "e", tk)])
                                        if tk == 0:
                                            if t8 > 0:
                                                t.wait_ge(ve.h, ve_pos[(t8 - 1, "oev")])
                                            elif h > 0:
                                                t.wait_ge(ve.h, veH)
                                        t.matmul(out=psb[0][:, 0:512],
                                                 lhsT=vh[:, h % 2, tk, :],
                                                 rhs=Eb[:, tk % 2, :],
                                                 start=(tk == 0), stop=(tk == ntk - 1))
                                        mm = t.matmul(out=psb[1][0:1, 0:512],
                                                      lhsT=ones_bf[:, 0:1],
                                                      rhs=Eb[:, tk % 2, :],
                                                      start=(tk == 0), stop=(tk == ntk - 1))
                                        mm.then_inc(pe.h, 1)
                                    else:
                                        t.wait_ge(ve.h, ve_pos[(t8, "rec")])
                                        t.matmul(out=psb[2][:, 0:512], lhsT=ones_row[:],
                                                 rhs=s2r[:], start=True, stop=True).then_inc(pe.h, 1)

                        @block.scalar
                        def _(s, h=h, acH=acH, pe_pos=pe_pos, ac_pos=ac_pos,
                              ve_pos=ve_pos, ntks_b=ntks_b):
                            for t8 in range(8):
                                ntk = ntks_b[t8]
                                for tk in range(ntk):
                                    s.wait_ge(pe.h, pe_pos[(t8, "S", tk)])
                                    if tk >= ntk - 4:
                                        s.wait_ge(ve.h, ve_pos[(t8, "cb", tk - (ntk - 4))])
                                    if tk >= 2:
                                        s.wait_ge(pe.h, pe_pos[(t8, "PVS", tk - 2)])
                                    sb = psb[2] if tk % 2 == 0 else psb[3]
                                    vcolix = tk if tk < 5 else 32
                                    s.activation(out=Eb[:, tk % 2, :], in_=sb[:, 0:512],
                                                 func=AF.Exp, scale=scD[:, 0:1],
                                                 bias=visb[:, vcolix:vcolix + 1]).then_inc(ac.h, 1)
                                s.wait_ge(pe.h, pe_pos[(t8, "PVS", ntk - 1)])
                                s.copy(out=s2[:], in_=psb[1][0:1, 0:512]).then_inc(ac.h, 1)
                                s.wait_ge(pe.h, pe_pos[(t8, "BC", 0)])
                                s.copy(out=ibc[:], in_=psb[2][:, 0:512]).then_inc(ac.h, 1)

                        spqB2 = [spq[0].v, spq[1].v]

                        @block.vector
                        def _(v, h=h, pe_pos=pe_pos, ac_pos=ac_pos, ve_pos=ve_pos,
                              ntks_b=ntks_b, spqB2=spqB2):
                            if h >= 2:
                                v.wait_ge(spq[h % 2].h, spqB2[h % 2])
                            for t8 in range(8):
                                ntk = ntks_b[t8]
                                for d4 in range(4):
                                    tk = ntk - 4 + d4
                                    sb = psb[2] if tk % 2 == 0 else psb[3]
                                    v.wait_ge(pe.h, pe_pos[(t8, "S", tk)])
                                    v.tensor_tensor(out=sb[:, 0:512], in0=sb[:, 0:512],
                                                    in1=cbias_sb[:, d4, :],
                                                    op=ALU.add).then_inc(ve.h, 1)
                                v.wait_ge(ac.h, ac_pos[(t8, "sum")])
                                v.reciprocal(out=s2r[:], in_=s2[:]).then_inc(ve.h, 1)
                                v.wait_ge(ac.h, ac_pos[(t8, "ibc")])
                                v.tensor_tensor(out=ob[:, h % 2, t8 * 512:(t8 + 1) * 512],
                                                in0=psb[0][:, 0:512], in1=ibc[:],
                                                op=ALU.mult).then_inc(ve.h, 1)
                        pe.v = pe_tot
                        ac.v = ac_tot
                        ve.v = ve_tot
                        head_tots.append(pe_tot)

                        prev_head_plan = ([ve_pos[(t8, "oev")] for t8 in range(8)], h)
                        if h % 4 == 3:
                            pending_ags.append(h // 4)

                    @block.sync
                    def _(sync, prev=prev_head_plan):
                        pve, ph_ = prev
                        for t8 in range(8):
                            sync.wait_ge(ve.h, pve[t8])
                            sync.dma_start(out=ob3d[:, ph_, t8 * 512:(t8 + 1) * 512],
                                           in_=ob[:, ph_ % 2, t8 * 512:(t8 + 1) * 512]
                                           ).then_inc(spq[ph_ % 2].h, 16)
                            spq[ph_ % 2].bump(16)

                    @block.gpsimd
                    def _(g, s0=spq[0].v, s1=spq[1].v):
                        for qq in range(4):
                            g.wait_ge(spq[0].h, s0 - 256 * (3 - qq))
                            g.wait_ge(spq[1].h, s1 - 256 * (3 - qq))
                            g.collective_compute(
                                "AllGather", ALU.bypass, replica_groups=PAIRS,
                                ins=[obsp[512 * qq:512 * (qq + 1), :].opt()],
                                outs=[agB[1024 * qq:1024 * (qq + 1), :].opt()]).then_inc(cc.h, 1)
                    cc.bump(4)

                    end_phase({"gpsimd": [(spq[0].h, spq[0].v), (spq[1].h, spq[1].v)],
                               "vector": [(ve.h, ve.v)], "tensor": [(pe.h, pe.v)],
                               "scalar": [(ac.h, ac.v)],
                               "sync": [(ldq[0].h, ldq[0].v), (ldq[1].h, ldq[1].v),
                                        (ldx[0].h, ldx[0].v), (ldx[1].h, ldx[1].v),
                                        (lda[0].h, lda[0].v), (lda[1].h, lda[1].v)]})

        emit_ln_statspass(x_in)
        emit_ln_finish(x_in)
        emit_attn_a(x_in)
        if BUILD_SUBLAYERS >= 2:
            emit_ln_finish(resid, with_mean=True, do_ag=True)
            emit_ffn(cc.v - 4, [True], agA, worgT, resid)
        if BUILD_SUBLAYERS >= 3:
            emit_ln_finish(resid, with_mean=False, do_ag=True)
            emit_attn_b(cc.v - 4)
            emit_ffn(cc.v - 4, [True], agB, owT_d, resid,
                     with_lora=False, do_stats=True)
        if BUILD_SUBLAYERS >= 4:
            emit_ln_finish(resid, with_mean=True, do_ag=True)
            emit_ffn(cc.v - 4, [True], agA, worgT, out,
                     with_lora=True, do_stats=False)

        @block.sync
        def _(sync):
            if BUILD_SUBLAYERS < 4:
                sync.dma_start(out=out, in_=resid).then_inc(st.h, 16)
                st.bump(16)
            sync.dma_start(out=dbg, in_=xn_dram).then_inc(st.h, 16)
            st.bump(16)
            sync.wait_ge(st.h, st.v)

    return nc


def kernel(L0_attn_a_x, L0_attn_a_bv_q, L0_attn_a_bv_k, L0_attn_a_bv_v,
           L0_attn_b_q_w, L0_attn_b_k_w, L0_attn_b_v_w, L0_attn_b_o_w,
           L0_ffn_W_org, L0_ffn_lora_down, L0_ffn_lora_up, L0_ffn_lora_keys,
           L0_ffn_Wq):
    x = np.asarray(L0_attn_a_x)
    bq = np.asarray(L0_attn_a_bv_q); bk = np.asarray(L0_attn_a_bv_k)
    bv = np.asarray(L0_attn_a_bv_v)
    W_org = np.asarray(L0_ffn_W_org)
    lora_down = np.asarray(L0_ffn_lora_down); lora_up = np.asarray(L0_ffn_lora_up)
    lora_keys = np.asarray(L0_ffn_lora_keys); Wq = np.asarray(L0_ffn_Wq)
    worgT_full = np.ascontiguousarray(W_org.T)
    down_cat = np.ascontiguousarray(lora_down.transpose(1, 0, 2).reshape(DM, 512))
    up_cat = np.ascontiguousarray(lora_up.reshape(512, DM))
    wqT_full = np.ascontiguousarray(Wq.T)
    keysT = np.ascontiguousarray(lora_keys.T)
    sel = np.zeros((4, 8, 128), np.float32)
    for g4 in range(4):
        sel[g4, 2 * g4, 0:64] = 1.0
        sel[g4, 2 * g4 + 1, 64:128] = 1.0
    q_w = np.asarray(L0_attn_b_q_w); k_w = np.asarray(L0_attn_b_k_w)
    v_w = np.asarray(L0_attn_b_v_w); o_w = np.asarray(L0_attn_b_o_w)
    qwT = np.ascontiguousarray(q_w.T); kwT = np.ascontiguousarray(k_w.T)
    vwT = np.ascontiguousarray(v_w.T); owT = np.ascontiguousarray(o_w.T)
    dd = np.arange(128)
    invf = (10000.0 ** (-(dd % 64).astype(np.float64) / 64.0))
    tt_ = np.arange(4096, dtype=np.float64)
    ang = invf[:, None] * tt_[None, :]
    ropec_h = np.cos(ang).astype(ml_dtypes.bfloat16)
    sgn_ = np.where(dd < 64, -1.0, 1.0)[:, None]
    ropes_h = (sgn_ * np.sin(ang)).astype(ml_dtypes.bfloat16)
    ppj, jjj = np.meshgrid(np.arange(128), np.arange(512), indexing="ij")
    cbias_h = np.stack([np.where(ppj + 128 * d <= jjj, 0.0, -1.0e30)
                        for d in range(4)]).astype(np.float32)
    iota512 = np.arange(512, dtype=np.float32)[None, :]
    iotac = np.arange(128, dtype=np.float32)[:, None]
    ident_bf = np.eye(128, dtype=ml_dtypes.bfloat16)
    ident_f = np.eye(128, dtype=np.float32)
    pp, jj = np.meshgrid(np.arange(128), np.arange(256), indexing="ij")
    amask = np.stack([(pp <= jj), (pp + 128 <= jj)]).astype(ml_dtypes.bfloat16)

    nc = build_nc()
    in_maps = []
    for c in range(8):
        b, z = c // 2, c % 2
        xi = np.ascontiguousarray(x[b].T[z * HALF:(z + 1) * HALF, :]).astype(np.float32)
        m = {
            "x_in": xi, "ident": ident_bf, "identf": ident_f, "amask": amask,
            "bvq": np.ascontiguousarray(bq[2 * z:2 * z + 2]).astype(np.float32),
            "bvk": np.ascontiguousarray(bk[2 * z:2 * z + 2]).astype(np.float32),
            "bvv": np.ascontiguousarray(bv[2 * z:2 * z + 2]).astype(np.float32),
            "worgT": np.ascontiguousarray(worgT_full[:, z * HALF:(z + 1) * HALF]).astype(np.float32),
            "down_d": down_cat.astype(np.float32),
            "up_d": np.ascontiguousarray(up_cat[:, z * HALF:(z + 1) * HALF]).astype(np.float32),
            "wqT_d": np.ascontiguousarray(wqT_full[z * HALF:(z + 1) * HALF, :]).astype(np.float32),
            "keysT_d": keysT.astype(np.float32),
            "sel_d": sel,
            "qwT_d": np.ascontiguousarray(qwT[:, z * HALF:(z + 1) * HALF]).astype(np.float32),
            "kwT_d": np.ascontiguousarray(kwT[:, z * HALF:(z + 1) * HALF]).astype(np.float32),
            "vwT_d": np.ascontiguousarray(vwT[:, z * HALF:(z + 1) * HALF]).astype(np.float32),
            "owT_d": np.ascontiguousarray(owT[:, z * HALF:(z + 1) * HALF]).astype(np.float32),
            "ropec_d": ropec_h, "ropes_d": ropes_h, "cbias_d": cbias_h,
            "iota512_d": iota512, "iotac_d": iotac,
            "impg_d": np.array([[1.0 if c < 2 else 0.0]], np.float32),
        }
        in_maps.append(m)
    res = run_bass_kernel_spmd(nc, in_maps, core_ids=list(range(8)))
    outs = []
    for b in range(4):
        fm = np.concatenate([res.results[2 * b]["out"], res.results[2 * b + 1]["out"]], axis=0)
        outs.append(fm.T)
    kernel._dbg = [r["dbg"] for r in res.results]
    return np.stack(outs, axis=0).astype(np.float32)
